# version 12
# baseline (speedup 1.0000x reference)
import numpy as np
import ml_dtypes
import concourse.bass as bass
import concourse.mybir as mybir
from concourse.bass import AP, IndirectOffsetOnAxis
from concourse.bass_utils import run_bass_kernel_spmd

F32 = mybir.dt.float32
BF16 = mybir.dt.bfloat16
U32 = mybir.dt.uint32
AF = mybir.ActivationFunctionType
ALU = mybir.AluOpType
C_DEBUG = None
AXX = mybir.AxisListType.X

L = 4096
NST = 64
NCORE = 8
EPS = 1e-5
KCH = 512
NKC = L // KCH


def host_constants():
    lf = np.arange(L, dtype=np.float32) / np.float32(L)
    Omega = np.exp(np.complex64(-2j * np.pi) * lf.astype(np.complex64)).astype(np.complex64)
    h = ((np.complex64(1.0) - Omega) / (np.complex64(1.0) + Omega)).astype(np.complex64)
    c = (np.complex64(2.0) / (np.complex64(1.0) + Omega)).astype(np.complex64)

    def tile32(v):
        return np.ascontiguousarray(v.reshape(32, 128).T)

    consts = {
        "hre_b": np.ascontiguousarray(np.broadcast_to(h.real.astype(np.float32), (NST, L))),
        "him_b": np.ascontiguousarray(np.broadcast_to(h.imag.astype(np.float32), (NST, L))),
        "c_re": tile32(c.real.astype(np.float32)),
        "c_im": tile32(c.imag.astype(np.float32)),
    }
    lt_ = np.arange(32)
    W64 = np.exp(2j * np.pi * np.arange(64) / 64.0)
    ev = W64[np.newaxis, :] ** (2 * lt_)[:, None]
    od = W64[np.newaxis, :] ** (2 * lt_ + 1)[:, None]
    consts["W64ev_re"] = ev.real.astype(np.float32)
    consts["W64ev_im"] = ev.imag.astype(np.float32)
    consts["W64ev_imn"] = (-ev.imag).astype(np.float32)
    consts["W64od_re"] = od.real.astype(np.float32)
    consts["W64od_im"] = od.imag.astype(np.float32)
    consts["W64od_imn"] = (-od.imag).astype(np.float32)
    l1_ = np.arange(64)
    Wt = np.exp(2j * np.pi * np.outer(l1_, np.arange(64)) / 4096.0)
    consts["Wt_re"] = Wt.real.astype(np.float32)
    consts["Wt_im"] = Wt.imag.astype(np.float32)
    W2 = np.exp(2j * np.pi * np.outer(l1_, np.arange(64)) / 64.0) / L
    consts["W2_re"] = W2.real.astype(np.float32)
    consts["W2_imn"] = (-W2.imag).astype(np.float32)
    consts["ones64"] = np.ones((1, 64), np.float32)
    consts["ident"] = np.eye(128, dtype=np.float32)
    consts["identb"] = np.eye(128, dtype=np.float32).astype(ml_dtypes.bfloat16)
    return consts


CONST_SPECS = {
    "hre_b": ([NST, L], F32), "him_b": ([NST, L], F32),
    "c_re": ([128, 32], F32), "c_im": ([128, 32], F32),
    "W64ev_re": ([32, 64], F32), "W64ev_im": ([32, 64], F32),
    "W64ev_imn": ([32, 64], F32),
    "W64od_re": ([32, 64], F32), "W64od_im": ([32, 64], F32),
    "W64od_imn": ([32, 64], F32),
    "Wt_re": ([64, 64], F32), "Wt_im": ([64, 64], F32),
    "W2_re": ([64, 64], F32), "W2_imn": ([64, 64], F32),
    "ones64": ([1, 64], F32), "ident": ([128, 128], F32),
    "identb": ([128, 128], BF16),
}


class Sem:
    __slots__ = ("h", "n")

    def __init__(self, nc, name):
        self.h = nc.alloc_semaphore(name)
        self.n = 0

    def inc(self, inst, k=1):
        inst.then_inc(self.h, k)
        self.n += k
        return self.n


class Prog:
    def __init__(self, T):
        self.blk = None
        self.T = T
        self.NT = T // 128
        self.NH = 2 if self.NT >= 2 else 1
        self.TPH = T // self.NH
        self.nc = bass.Bass("TRN2", target_bir_lowering=False)
        self.s = {}
        self.t = {}

    def sem(self, name):
        if name not in self.s:
            self.s[name] = Sem(self.nc, name)
        return self.s[name]

    from contextlib import contextmanager

    @contextmanager
    def block(self, name):
        yield self.blk


def build_program(T=1024, debug=(), phases=("f1", "ssm", "f2"), hoist_f2a=True):
    p = Prog(T)
    nc = p.nc
    NT, NH, TPH = p.NT, p.NH, p.TPH
    NTH = NT // NH

    x_in = nc.dram_tensor("x", [T, L], F32, kind="ExternalInput")
    y_out = nc.dram_tensor("y", [T, L], F32, kind="ExternalOutput")
    lam_in = nc.dram_tensor("L0_attn_Lambda_ri", [NST, 2], F32, kind="ExternalInput")
    p_in = nc.dram_tensor("L0_attn_p_ri", [NST, 2], F32, kind="ExternalInput")
    q_in = nc.dram_tensor("L0_attn_q_ri", [NST, 2], F32, kind="ExternalInput")
    vc_in = nc.dram_tensor("L0_attn_Vc_ri", [NST, 2 * NST], F32, kind="ExternalInput")
    ct_in = nc.dram_tensor("L0_attn_Ct_ri", [NST, 2], F32, kind="ExternalInput")
    bp_in = nc.dram_tensor("L0_attn_Bp", [NST, 1], F32, kind="ExternalInput")
    d_in = nc.dram_tensor("L0_attn_D", [1, 1], F32, kind="ExternalInput")
    ls_in = nc.dram_tensor("L0_attn_log_step", [1, 1], F32, kind="ExternalInput")
    up_in = nc.dram_tensor("L0_ffn_up", [16, 256, 1024], F32, kind="ExternalInput")
    dn_in = nc.dram_tensor("L0_ffn_down", [16, 1024, 256], F32, kind="ExternalInput")
    pm_in = nc.dram_tensor("permT", [128, 32], U32, kind="ExternalInput")
    cin = {nm: nc.dram_tensor("c_" + nm, shp, dt, kind="ExternalInput")
           for nm, (shp, dt) in CONST_SPECS.items()}

    kext = nc.dram_tensor("kext", [2 * L], F32, kind="Internal")
    s1 = nc.dram_tensor("s1", [L * T], BF16, kind="Internal")
    s2 = nc.dram_tensor("s2", [L, T], BF16, kind="Internal")
    dbg = {}
    if "kext" in debug:
        dbg["kext"] = nc.dram_tensor("dbg_kext", [2 * L], F32, kind="ExternalOutput")
    if "tcat" in debug:
        dbg["tcat"] = nc.dram_tensor("dbg_tcat", [128, 4480], BF16, kind="ExternalOutput")
    for st in ("x1", "x2", "ln1", "s2d"):
        if st in debug:
            dbg[st] = nc.dram_tensor("dbg_" + st, [T, L],
                                     BF16 if st in ("ln1", "s2d") else F32,
                                     kind="ExternalOutput")

    XBYTES = NT * L * 4
    WRK_E = 24576
    RES_BYTES = XBYTES + WRK_E * 2 + 8192 + 5 * (KCH * 4)
    SB_BASE = (nc._sbuf_addr_for_side("left") + 63) & ~63
    nc.alloc_sbuf_tensor("reserve", [128, SB_BASE + RES_BYTES - 16384],
                         mybir.dt.uint8)

    def at(name, shape, dtype, off):
        h = nc.alloc_sbuf_tensor_at(name, shape, dtype, offset=SB_BASE + off)
        p.t[name] = h
        return h

    off = 0
    xres = at("xres", [128, NT * L], F32, 0); off += XBYTES
    wrk = at("wrk", [128, WRK_E], BF16, off); off += WRK_E * 2
    delt = at("delt", [128, 4096], BF16, off); off += 8192
    kch = {}
    for nm in ["dre", "dim", "mg", "rcre", "rcim"]:
        kch[nm] = at("k_" + nm, [NST, KCH], F32, off); off += KCH * 4

    def mk(name, shape, dtype):
        h = nc.alloc_sbuf_tensor(name, shape, dtype)
        p.t[name] = h
        return h

    sums = mk("sums", [128, NT], F32)
    ssq = mk("ssq", [128, NT], F32)
    mcol = mk("mcol", [128, NT], F32)
    rstd = mk("rstd", [128, NT], F32)
    nmr = mk("nmr", [128, NT], F32)
    idxn = mk("idxn", [128, 32], U32)
    idxt = mk("idxt", [128, 32], U32)
    idxf = mk("idxf", [128, 32], U32)
    idxf2 = mk("idxf2", [128, 32], U32)
    kbuf = mk("kbuf", [128, 256], F32)
    c1sb = mk("c1sb", [128, 256], F32)
    scr32 = mk("scr32", [128, 192], F32)
    cre = mk("cre", [128, 32], F32)
    cim = mk("cim", [128, 32], F32)
    ident = mk("ident", [128, 128], F32)
    identb = mk("identb", [128, 128], BF16)

    lam = mk("lam", [NST, 2], F32)
    pri = mk("pri", [NST, 2], F32)
    qri = mk("qri", [NST, 2], F32)
    ctri = mk("ctri", [NST, 2], F32)
    bp = mk("bp", [NST, 1], F32)
    vmat = mk("vmat", [NST, 8], F32)
    vcs = mk("vcs", [NST, 2 * NST], F32)
    vctr = mk("vctr", [NST, 2 * NST], F32)
    bvec = mk("bvec", [NST, 2], F32)
    w1 = mk("w1", [32, 6 * 64], F32)
    w2 = mk("w2", [NST, 8 * 64], F32)
    arT = mk("arT", [32, 2 * 128], F32)
    absb = mk("absb", [NST, 4 * 64], F32)
    ksb = mk("ksb", [NST, NST], F32)
    zsb = mk("zsb", [NST, NST], F32)
    sca = mk("sca", [1, 8], F32)
    s2c = mk("s2c", [NST, 1], F32)
    onesb = mk("onesb", [1, NST], F32)
    ddsb = mk("ddsb", [1, 1], F32)
    epsc = mk("epsc", [128, 1], F32)
    tct = mk("tct", [128, 768], BF16)

    FMT_O = 0
    B0_O = 32 * TPH
    GB_O = 0
    WU_O = 4 * T
    WD_O = 4 * T + 4096
    HID_O = 4 * T + 8192
    DO_O = 7 * T + 8192
    TC_O = B0_O

    psU = nc.alloc_psum_tensor("psU", [128, 2048], F32)
    psD = nc.alloc_psum_tensor("psD", [128, 2048], F32)

    def band_ap(b):
        return (psU if b < 4 else psD)[:, (b % 4) * 512:(b % 4 + 1) * 512]

    s_dx = p.sem("dx")
    s_dx2 = p.sem("dx2")
    s_dc = p.sem("dc")
    s_dg = p.sem("dg")
    s_dtp = p.sem("dtp")
    s_dwr = p.sem("dwr")
    s_pe = p.sem("pe")
    s_dve = p.sem("dve")
    s_act = p.sem("act")
    s_dst = p.sem("dst")
    s_bnd = p.sem("bnd")

    env = dict(locals())
    env['HOIST_F2A'] = hoist_f2a

    p.blk = nc.Block("all", no_gpsimd_drain=True)
    p.blk.__enter__()

    with p.block("ph0") as blk:
        @blk.sync
        def _(sy):
            s_dc.inc(sy.dma_start(out=cre[:], in_=cin["c_re"][:]), 16)
            s_dc.inc(sy.dma_start(out=cim[:], in_=cin["c_im"][:]), 16)
            s_dc.inc(sy.dma_start(out=ident[:], in_=cin["ident"][:]), 16)
            s_dc.inc(sy.dma_start(out=identb[:], in_=cin["identb"][:]), 16)
            s_dc.inc(sy.dma_start(out=onesb[:], in_=cin["ones64"][:]), 16)
            for i, nm in enumerate(["W64ev_re", "W64ev_im", "W64ev_imn",
                                    "W64od_re", "W64od_im", "W64od_imn"]):
                s_dc.inc(sy.dma_start(out=w1[:, i * 64:(i + 1) * 64], in_=cin[nm][:]), 16)
            for i, nm in enumerate(["Wt_re", "Wt_im", "W2_re", "W2_imn"]):
                s_dc.inc(sy.dma_start(out=w2[:, i * 64:(i + 1) * 64], in_=cin[nm][:]), 16)
            for t_ in range(NT):
                sm = s_dx if t_ < NT // 2 or NT == 1 else s_dx2
                sm.inc(sy.dma_start(out=xres[:, t_ * L:(t_ + 1) * L],
                                    in_=x_in[t_ * 128:(t_ + 1) * 128, :]), 16)

        @blk.scalar
        def _(sc):
            for dst_, src_ in [(lam, lam_in), (pri, p_in), (qri, q_in),
                               (ctri, ct_in), (bp, bp_in), (ddsb, d_in),
                               (vcs, vc_in)]:
                s_dc.inc(sc.dma_start(out=dst_[:], in_=src_[:]), 16)
            s_dc.inc(sc.dma_start(out=sca[:, 0:1], in_=ls_in[:]), 16)

        @blk.gpsimd
        def _(g):
            s_dg.inc(g.dma_start(out=idxn[:], in_=pm_in[:]), 16)

        @blk.vector
        def _(v):
            p.s["dve"].inc(v.memset(epsc[:], EPS))

    build_idx(p, env)
    build_kcomp_head(p, env)
    env["kc_pending"] = list(range(NKC))
    if "f1" in phases:
        if "ssm" in phases and NH == 2 and NT // NH == 4:
            def _pc1():
                ln_pair(p, env, "ssm0", 0, [0, 1], True, xbars=False)
                env.setdefault("computed_pairs", set()).add(("ssm0", 0))
            env["postC_cb_next"] = {1: _pc1}
        build_ffn(p, env, "f1")
    else:
        kc_flush(p, env)
    dump_dbg(p, env, "x1")
    if "ssm" in phases:
        cbs1 = {}
        if "f2" in phases and NH == 2 and NT // NH == 4:
            def _cb0():
                ln_pair(p, env, "f2A", 0, [0, 1], False)
                env.setdefault("done_pairs", set()).add(("f2A", 0))

            def _cb1():
                ln_pair(p, env, "f2A", 0, [2, 3], False)
                env.setdefault("done_pairs", set()).add(("f2A", 2))

            def _cb2():
                ln_pair(p, env, "f2A", 1, [4, 5], False, xbars=False)

            def _cb3():
                s1_write_half(p, env, "f2A", 0)
                ln_xbars(p, env, "f2A", 1, [4, 5])
                ln_pair(p, env, "f2A", 1, [6, 7], False, xbars=False)
                env.setdefault("done_pairs", set()).update(
                    {("f2A", 4), ("f2A", 6)})
                env["f2A_defer_xb"] = True

            cbs1 = {0: _cb0, 1: _cb1, 2: _cb2, 3: _cb3} if env.get('HOIST_F2A', True) else {}
        elif "f2" in phases and NH == 2 and NT // NH == 2:
            def _cb0():
                ln_pair(p, env, "f2A", 0, [0, 1], False)
                env.setdefault("done_pairs", set()).add(("f2A", 0))
            cbs1 = {0: _cb0}
        build_ssm(p, env, post_cbs=cbs1)
    dump_dbg(p, env, "x2")
    if "f2" in phases:
        build_ffn(p, env, "f2")
    build_store(p, env)
    finish_program(p)
    out_names = ["y"] + ["dbg_" + k for k in dbg]
    return p, env, out_names


def finish_program(p):
    nc = p.nc
    with p.block("fin") as blk:
        @blk.sync
        def _(sy):
            for nm in ["dx", "dx2", "dc", "dtp", "dwr", "dst", "ctp0", "ctp1",
                       "cr0", "cr1"]:
                if p.s[nm].n if nm in p.s else 0:
                    sy.wait_ge(p.s[nm].h, p.s[nm].n)

        @blk.gpsimd
        def _(g):
            for nm in ["dg", "gt0", "gt1", "sc"]:
                if nm in p.s and p.s[nm].n:
                    g.wait_ge(p.s[nm].h, p.s[nm].n)
    p.blk.__exit__(None, None, None)


def kc_step(p, env, n=2):
    for _ in range(n):
        if env.get("kc_pending"):
            ch = env["kc_pending"].pop(0)
            kc_chunk(p, env, ch)


def kc_flush(p, env):
    while env.get("kc_pending"):
        kc_chunk(p, env, env["kc_pending"].pop(0))
    if not env.get("kc_tail_done"):
        build_kcomp_tail(p, env)
        env["kc_tail_done"] = True
        env["kc_done_dve"] = p.s["dve"].n
        env["kc_done_pe"] = p.s["pe"].n


def build_kcomp_head(p, env):
    nc = p.nc
    s_dc, s_dve, s_act, s_pe, s_dwr, s_dg = (p.s[n] for n in
                                             ["dc", "dve", "act", "pe", "dwr", "dg"])
    g_ = env
    kch = g_["kch"]
    (lam, pri, qri, ctri, bp, vmat, vcs, vctr, bvec, w1, w2, arT, absb, ksb,
     zsb, sca, s2c, onesb, ddsb, ident, cre, cim, kbuf, c1sb, scr32,
     psU, psD, kext, dbg) = (g_[n] for n in [
         "lam", "pri", "qri", "ctri", "bp", "vmat", "vcs", "vctr", "bvec",
         "w1", "w2", "arT", "absb", "ksb", "zsb", "sca", "s2c", "onesb",
         "ddsb", "ident", "cre", "cim", "kbuf", "c1sb", "scr32",
         "psU", "psD", "kext", "dbg"])
    NCONST = s_dc.n

    with p.block("kc_sc") as blk:
        @blk.scalar
        def _(sc):
            sc.wait_ge(s_dc.h, NCONST)
            s_act.inc(sc.activation(out=sca[:, 1:2], in_=sca[:, 0:1], func=AF.Exp))

        @blk.vector
        def _(v):
            v.wait_ge(s_act.h, s_act.n)
            s_dve.inc(v.reciprocal(out=sca[:, 2:3], in_=sca[:, 1:2]))
            v.wait_ge(s_dve.h, s_dve.n)
            s_dve.inc(v.tensor_scalar(out=sca[:, 3:4], in0=sca[:, 2:3],
                                      scalar1=2.0, scalar2=None, op0=ALU.mult))

        @blk.tensor
        def _(te):
            te.wait_ge(s_dve.h, s_dve.n)
            s_pe.inc(te.matmul(psD[:NST, 0:1], onesb[:, :NST], sca[:, 3:4],
                               start=True, stop=True))

        @blk.vector
        def _(v):
            v.wait_ge(s_pe.h, s_pe.n)
            s_dve.inc(v.tensor_copy(out=s2c[:], in_=psD[:NST, 0:1]))

    with p.block("kc_vv") as blk:
        @blk.vector
        def _(v):
            v.wait_ge(s_dc.h, NCONST)
            vcr = vcs[:].rearrange("p (c two) -> p two c", two=2)
            s_dve.inc(v.tensor_copy(out=vctr[:, 0:NST], in_=vcr[:, 0, :]))
            s_dve.inc(v.tensor_copy(out=vctr[:, NST:2 * NST], in_=vcr[:, 1, :]))

        @blk.tensor
        def _(te):
            te.wait_ge(s_dve.h, s_dve.n)
            s_pe.inc(te.transpose(psU[:NST, 0:NST], vctr[:NST, 0:NST], ident[:NST, :NST]))
            s_pe.inc(te.transpose(psU[:NST, 512:512 + NST], vctr[:NST, NST:2 * NST],
                                  ident[:NST, :NST]))

        @blk.vector
        def _(v):
            v.wait_ge(s_pe.h, s_pe.n)
            s_dve.inc(v.tensor_copy(out=vcs[:, 0:NST], in_=psU[:NST, 0:NST]))
            s_dve.inc(v.tensor_copy(out=vcs[:, NST:2 * NST], in_=psU[:NST, 512:512 + NST]))

        @blk.tensor
        def _(te):
            te.wait_ge(s_dve.h, s_dve.n)
            s_pe.inc(te.matmul(psD[:NST, 512:513], vcs[:NST, 0:NST], bp[:NST, :],
                               start=True, stop=True))
            s_pe.inc(te.matmul(psD[:NST, 1024:1025], vcs[:NST, NST:2 * NST], bp[:NST, :],
                               start=True, stop=True))

        @blk.vector
        def _(v):
            v.wait_ge(s_pe.h, s_pe.n)
            s_dve.inc(v.tensor_copy(out=bvec[:, 0:1], in_=psD[:NST, 512:513]))
            s_dve.inc(v.tensor_copy(out=bvec[:, 1:2], in_=psD[:NST, 1024:1025]))
            v.wait_ge(s_dve.h, s_dve.n)

            def cmul_conj(col, a_re, a_im, c_re_, c_im_):
                s_dve.inc(v.tensor_mul(out=scr32[:NST, 0:1], in0=a_re, in1=c_re_))
                s_dve.inc(v.tensor_mul(out=scr32[:NST, 1:2], in0=a_im, in1=c_im_))
                s_dve.inc(v.tensor_mul(out=scr32[:NST, 2:3], in0=a_re, in1=c_im_))
                s_dve.inc(v.tensor_mul(out=scr32[:NST, 3:4], in0=a_im, in1=c_re_))
                v.wait_ge(s_dve.h, s_dve.n)
                s_dve.inc(v.tensor_add(out=vmat[:, col:col + 1],
                                       in0=scr32[:NST, 0:1], in1=scr32[:NST, 1:2]))
                s_dve.inc(v.tensor_sub(out=vmat[:, col + 1:col + 2],
                                       in0=scr32[:NST, 2:3], in1=scr32[:NST, 3:4]))
                v.wait_ge(s_dve.h, s_dve.n)

            cmul_conj(0, ctri[:, 0:1], ctri[:, 1:2], bvec[:, 0:1], bvec[:, 1:2])
            cmul_conj(2, ctri[:, 0:1], ctri[:, 1:2], pri[:, 0:1], pri[:, 1:2])
            cmul_conj(4, qri[:, 0:1], qri[:, 1:2], bvec[:, 0:1], bvec[:, 1:2])
            cmul_conj(6, qri[:, 0:1], qri[:, 1:2], pri[:, 0:1], pri[:, 1:2])


def kc_chunk(p, env, ch):
    nc = p.nc
    g_ = env
    kch = env["kch"]
    s_dc, s_dve, s_act, s_pe, s_dwr, s_dg = (p.s[n] for n in
                                             ["dc", "dve", "act", "pe", "dwr", "dg"])
    vmat, psU = env["vmat"], env["psU"]
    lam, s2c = env["lam"], env["s2c"]
    NLT = KCH // 128
    if True:
        with p.block(f"kc_ch{ch}") as blk:
            @blk.gpsimd
            def _(g2):
                g2.wait_ge(s_dve.h, s_dve.n)
                g2.wait_ge(s_pe.h, s_pe.n)
                s_dg.inc(g2.dma_start(out=kch["dre"][:],
                                      in_=g_["cin"]["hre_b"][:, ch * KCH:(ch + 1) * KCH]), 16)
                s_dg.inc(g2.dma_start(out=kch["dim"][:],
                                      in_=g_["cin"]["him_b"][:, ch * KCH:(ch + 1) * KCH]), 16)

            @blk.vector
            def _(v):
                v.wait_ge(s_dg.h, s_dg.n)
                s_dve.inc(v.tensor_scalar(out=kch["dre"][:], in0=kch["dre"][:],
                                          scalar1=s2c[:], scalar2=lam[:, 0:1],
                                          op0=ALU.mult, op1=ALU.subtract))
                s_dve.inc(v.tensor_scalar(out=kch["dim"][:], in0=kch["dim"][:],
                                          scalar1=s2c[:], scalar2=lam[:, 1:2],
                                          op0=ALU.mult, op1=ALU.subtract))
                v.wait_ge(s_dve.h, s_dve.n)
                s_dve.inc(v.tensor_mul(out=kch["mg"][:], in0=kch["dre"][:], in1=kch["dre"][:]))
                s_dve.inc(v.tensor_mul(out=kch["rcim"][:], in0=kch["dim"][:], in1=kch["dim"][:]))
                v.wait_ge(s_dve.h, s_dve.n)
                s_dve.inc(v.tensor_add(out=kch["mg"][:], in0=kch["mg"][:], in1=kch["rcim"][:]))
                v.wait_ge(s_dve.h, s_dve.n)
                s_dve.inc(v.reciprocal(out=kch["mg"][:], in_=kch["mg"][:]))
                v.wait_ge(s_dve.h, s_dve.n)
                s_dve.inc(v.tensor_mul(out=kch["rcre"][:], in0=kch["dre"][:], in1=kch["mg"][:]))
                s_dve.inc(v.scalar_tensor_tensor(out=kch["rcim"][:], in0=kch["dim"][:],
                                                 scalar=-1.0, in1=kch["mg"][:],
                                                 op0=ALU.mult, op1=ALU.mult))

            @blk.tensor
            def _(te):
                te.wait_ge(s_dve.h, s_dve.n)
                for i in range(NLT):
                    lt = ch * NLT + i
                    s_pe.inc(te.matmul(psU[:, lt * 8:lt * 8 + 8],
                                       kch["rcre"][:, i * 128:(i + 1) * 128],
                                       vmat[:NST, :], start=True, stop=True))
                    s_pe.inc(te.matmul(psU[:, 512 + lt * 8:512 + lt * 8 + 8],
                                       kch["rcim"][:, i * 128:(i + 1) * 128],
                                       vmat[:NST, :], start=True, stop=True))


def build_kcomp_tail(p, env):
    nc = p.nc
    g_ = env
    (lam, pri, qri, ctri, bp, vmat, vcs, vctr, bvec, w1, w2, arT, absb, ksb,
     zsb, sca, s2c, onesb, ddsb, ident, cre, cim, kbuf, c1sb, scr32,
     psU, psD, kext, dbg) = (g_[n] for n in [
         "lam", "pri", "qri", "ctri", "bp", "vmat", "vcs", "vctr", "bvec",
         "w1", "w2", "arT", "absb", "ksb", "zsb", "sca", "s2c", "onesb",
         "ddsb", "ident", "cre", "cim", "kbuf", "c1sb", "scr32",
         "psU", "psD", "kext", "dbg"])
    s_dc, s_dve, s_act, s_pe, s_dwr, s_dg = (p.s[n] for n in
                                             ["dc", "dve", "act", "pe", "dwr", "dg"])
    with p.block("kc_comb") as blk:
        @blk.vector
        def _(v):
            v.wait_ge(s_pe.h, s_pe.n)
            s_dve.inc(v.tensor_copy(out=c1sb[:], in_=psU[:, 0:256]))
            v.wait_ge(s_dve.h, s_dve.n)
            c1r = c1sb[:].rearrange("p (lt j) -> p j lt", j=8)
            c2r = psU[:, 512:768].rearrange("p (lt j) -> p j lt", j=8)
            kr = kbuf[:].rearrange("p (g lt) -> p g lt", g=8)
            for j in range(4):
                s_dve.inc(v.tensor_sub(out=kr[:, 2 * j, :], in0=c1r[:, 2 * j, :],
                                       in1=c2r[:, 2 * j + 1, :]))
                s_dve.inc(v.tensor_add(out=kr[:, 2 * j + 1, :], in0=c1r[:, 2 * j + 1, :],
                                       in1=c2r[:, 2 * j, :]))
            v.wait_ge(s_dve.h, s_dve.n)

    with p.block("kc_ar") as blk:
        @blk.vector
        def _(v):
            kr = kbuf[:].rearrange("p (g lt) -> p g lt", g=8)
            k00r, k00i = kr[:, 0, :], kr[:, 1, :]
            k01r, k01i = kr[:, 2, :], kr[:, 3, :]
            k10r, k10i = kr[:, 4, :], kr[:, 5, :]
            k11r, k11i = kr[:, 6, :], kr[:, 7, :]
            S = scr32[:].rearrange("p (g c) -> p g c", c=32)
            tre, tim, dre, dmag, u1 = (S[:, i, :] for i in range(5))
            s_dve.inc(v.tensor_mul(out=tre, in0=k01r, in1=k10r))
            s_dve.inc(v.tensor_mul(out=u1, in0=k01i, in1=k10i))
            s_dve.inc(v.tensor_mul(out=tim, in0=k01r, in1=k10i))
            s_dve.inc(v.tensor_mul(out=dre, in0=k01i, in1=k10r))
            v.wait_ge(s_dve.h, s_dve.n)
            s_dve.inc(v.tensor_sub(out=tre, in0=tre, in1=u1))
            s_dve.inc(v.tensor_add(out=tim, in0=tim, in1=dre))
            v.wait_ge(s_dve.h, s_dve.n)
            s_dve.inc(v.tensor_scalar(out=dre, in0=k11r, scalar1=1.0, scalar2=None,
                                      op0=ALU.add))
            v.wait_ge(s_dve.h, s_dve.n)
            s_dve.inc(v.tensor_mul(out=dmag, in0=dre, in1=dre))
            s_dve.inc(v.tensor_mul(out=u1, in0=k11i, in1=k11i))
            v.wait_ge(s_dve.h, s_dve.n)
            s_dve.inc(v.tensor_add(out=dmag, in0=dmag, in1=u1))
            v.wait_ge(s_dve.h, s_dve.n)
            s_dve.inc(v.reciprocal(out=dmag, in_=dmag))
            v.wait_ge(s_dve.h, s_dve.n)
            s_dve.inc(v.tensor_mul(out=u1, in0=tre, in1=dre))
            s_dve.inc(v.tensor_mul(out=k10r, in0=tim, in1=k11i))
            s_dve.inc(v.tensor_mul(out=k10i, in0=tim, in1=dre))
            s_dve.inc(v.tensor_mul(out=k01r, in0=tre, in1=k11i))
            v.wait_ge(s_dve.h, s_dve.n)
            s_dve.inc(v.tensor_add(out=u1, in0=u1, in1=k10r))
            s_dve.inc(v.tensor_sub(out=k10i, in0=k10i, in1=k01r))
            v.wait_ge(s_dve.h, s_dve.n)
            s_dve.inc(v.tensor_mul(out=u1, in0=u1, in1=dmag))
            s_dve.inc(v.tensor_mul(out=k10i, in0=k10i, in1=dmag))
            v.wait_ge(s_dve.h, s_dve.n)
            s_dve.inc(v.tensor_sub(out=k00r, in0=k00r, in1=u1))
            s_dve.inc(v.tensor_sub(out=k00i, in0=k00i, in1=k10i))
            v.wait_ge(s_dve.h, s_dve.n)
            s_dve.inc(v.tensor_mul(out=k01r, in0=cre[:], in1=k00r))
            s_dve.inc(v.tensor_mul(out=u1, in0=cim[:], in1=k00i))
            s_dve.inc(v.tensor_mul(out=k01i, in0=cre[:], in1=k00i))
            s_dve.inc(v.tensor_mul(out=k10r, in0=cim[:], in1=k00r))
            v.wait_ge(s_dve.h, s_dve.n)
            s_dve.inc(v.tensor_sub(out=k01r, in0=k01r, in1=u1))
            s_dve.inc(v.tensor_add(out=k01i, in0=k01i, in1=k10r))

    with p.block("kc_idft") as blk:
        kr = kbuf[:].rearrange("p (g lt) -> p g lt", g=8)
        ar_re, ar_im = kr[:, 2, :], kr[:, 3, :]

        @blk.tensor
        def _(te):
            te.wait_ge(s_dve.h, s_dve.n)
            s_pe.inc(te.transpose(psD[:32, 0:128], ar_re, ident[:, :]))
            s_pe.inc(te.transpose(psD[:32, 512:512 + 128], ar_im, ident[:, :]))

        @blk.vector
        def _(v):
            v.wait_ge(s_pe.h, s_pe.n)
            s_dve.inc(v.tensor_copy(out=arT[:, 0:128], in_=psD[:32, 0:128]))
            s_dve.inc(v.tensor_copy(out=arT[:, 128:256], in_=psD[:32, 512:512 + 128]))

        @blk.tensor
        def _(te):
            te.wait_ge(s_dve.h, s_dve.n)
            arL, arH = arT[:32, 0:64], arT[:32, 64:128]
            aiL, aiH = arT[:32, 128:192], arT[:32, 192:256]
            wev_re, wev_im, wev_imn = w1[:32, 0:64], w1[:32, 64:128], w1[:32, 128:192]
            wod_re, wod_im, wod_imn = w1[:32, 192:256], w1[:32, 256:320], w1[:32, 320:384]
            pa_re, pa_im = psU[:NST, 1024:1088], psU[:NST, 1536:1600]
            s_pe.inc(te.matmul(pa_re, arL, wev_re, start=True, stop=False))
            te.matmul(pa_re, arH, wod_re, start=False, stop=False)
            te.matmul(pa_re, aiL, wev_imn, start=False, stop=False)
            s_pe.inc(te.matmul(pa_re, aiH, wod_imn, start=False, stop=True))
            s_pe.inc(te.matmul(pa_im, arL, wev_im, start=True, stop=False))
            te.matmul(pa_im, arH, wod_im, start=False, stop=False)
            te.matmul(pa_im, aiL, wev_re, start=False, stop=False)
            s_pe.inc(te.matmul(pa_im, aiH, wod_re, start=False, stop=True))

        @blk.vector
        def _(v):
            v.wait_ge(s_pe.h, s_pe.n)
            s_dve.inc(v.tensor_copy(out=absb[:, 0:64], in_=psU[:NST, 1024:1088]))
            s_dve.inc(v.tensor_copy(out=absb[:, 64:128], in_=psU[:NST, 1536:1600]))
            v.wait_ge(s_dve.h, s_dve.n)
            a_re, a_im = absb[:, 0:64], absb[:, 64:128]
            b_re, b_im = absb[:, 128:192], absb[:, 192:256]
            wt_re, wt_im = w2[:, 0:64], w2[:, 64:128]
            s_dve.inc(v.tensor_mul(out=b_re, in0=a_re, in1=wt_re))
            s_dve.inc(v.tensor_mul(out=b_im, in0=a_re, in1=wt_im))
            s_dve.inc(v.tensor_mul(out=scr32[:NST, 0:64], in0=a_im, in1=wt_im))
            s_dve.inc(v.tensor_mul(out=scr32[:NST, 64:128], in0=a_im, in1=wt_re))
            v.wait_ge(s_dve.h, s_dve.n)
            s_dve.inc(v.tensor_sub(out=b_re, in0=b_re, in1=scr32[:NST, 0:64]))
            s_dve.inc(v.tensor_add(out=b_im, in0=b_im, in1=scr32[:NST, 64:128]))

        @blk.tensor
        def _(te):
            te.wait_ge(s_dve.h, s_dve.n)
            pk = psD[:NST, 1024:1088]
            s_pe.inc(te.matmul(pk, w2[:, 128:192], absb[:, 128:192], start=True, stop=False))
            s_pe.inc(te.matmul(pk, w2[:, 192:256], absb[:, 192:256], start=False, stop=True))

        @blk.vector
        def _(v):
            v.wait_ge(s_pe.h, s_pe.n)
            s_dve.inc(v.tensor_add(out=psD[:1, 1024:1025], in0=psD[:1, 1024:1025],
                                   in1=ddsb[:, :]))
            v.wait_ge(s_dve.h, s_dve.n)
            s_dve.inc(v.tensor_copy(out=ksb[:], in_=psD[:NST, 1024:1088]))
            s_dve.inc(v.memset(zsb[:], 0.0))

        @blk.sync
        def _(sy):
            sy.wait_ge(s_dve.h, s_dve.n)
            s_dwr.inc(sy.dma_start(out=AP(kext, 0, [[64, 64], [1, 64]]), in_=zsb[:]), 16)
            s_dwr.inc(sy.dma_start(out=AP(kext, L, [[64, 64], [1, 64]]), in_=ksb[:]), 16)
            env["kext_done"] = s_dwr.n
            if "kext" in dbg:
                sy.wait_ge(s_dwr.h, s_dwr.n)
                s_dwr.inc(sy.dma_start(out=AP(dbg["kext"], 0, [[1, 1], [1, 2 * L]]),
                                       in_=AP(kext, 0, [[1, 1], [1, 2 * L]])), 16)
            if "tcat" in dbg:
                pass
            sy.wait_ge(s_dwr.h, s_dwr.n)

    if "tcat" in dbg:
        wrk = env_wrk(p)
        with p.block("kc_tcdbg") as blk:
            @blk.gpsimd
            def _(g):
                g.wait_ge(s_dwr.h, s_dwr.n)
                s_dg.inc(g.dma_start(out=wrk[:, 16384:16384 + 4480],
                                     in_=AP(kext, 3585, [[1, 128], [1, 4480]])), 16)
                g.wait_ge(s_dg.h, s_dg.n)
                s_dg.inc(g.dma_start(out=dbg["tcat"][:], in_=wrk[:, 16384:16384 + 4480]), 16)
                g.wait_ge(s_dg.h, s_dg.n)


def dump_dbg(p, env, st):
    if st not in env["dbg"]:
        return
    nc = p.nc
    with p.block("dbg_" + st) as blk:
        @blk.sync
        def _(sy):
            sy.wait_ge(p.s["dve"].h, p.s["dve"].n)
            sy.wait_ge(p.s["bnd"].h, p.s["bnd"].n)
            for t_ in range(p.NT):
                p.s["dst"].inc(sy.dma_start(
                    out=env["dbg"][st][t_ * 128:(t_ + 1) * 128, :],
                    in_=env["xres"][:, t_ * L:(t_ + 1) * L]), 16)
            sy.wait_ge(p.s["dst"].h, p.s["dst"].n)


def env_wrk(p):
    return p.t["wrk"]


def env_hsrc(p, which, ch):
    return p.t["hri"]


def build_idx(p, env):
    nc = p.nc
    s_dg, s_dve = p.s["dg"], p.s["dve"]
    idxn, idxt, idxf, idxf2 = (env[n] for n in ["idxn", "idxt", "idxf", "idxf2"])
    with p.block("idx") as blk:
        @blk.vector
        def _(v):
            v.wait_ge(s_dg.h, 16)
            s_dve.inc(v.tensor_scalar(out=idxf[:], in0=idxn[:], scalar1=127,
                                      scalar2=5, op0=ALU.bitwise_and,
                                      op1=ALU.logical_shift_left))
            s_dve.inc(v.tensor_scalar(out=idxf2[:], in0=idxn[:], scalar1=7,
                                      scalar2=None, op0=ALU.logical_shift_right))
            v.wait_ge(s_dve.h, s_dve.n)
            s_dve.inc(v.tensor_tensor(out=idxt[:], in0=idxf[:], in1=idxf2[:],
                                      op=ALU.add))


_CACHE = {}


def kernel(L0_attn_u, L0_attn_Lambda_ri, L0_attn_p_ri, L0_attn_q_ri,
           L0_attn_Vc_ri, L0_attn_Ct_ri, L0_attn_Bp, L0_attn_D,
           L0_attn_log_step, L0_ffn_up, L0_ffn_down, L0_ffn_perm):
    T = 8192 // NCORE
    x = np.ascontiguousarray(np.asarray(L0_attn_u, dtype=np.float32).reshape(8192, L))
    if "prog" not in _CACHE:
        _CACHE["prog"] = build_program(T=T)
        _CACHE["consts"] = host_constants()
    p, env, outs = _CACHE["prog"]
    consts = _CACHE["consts"]

    base = {
        "L0_attn_Lambda_ri": np.asarray(L0_attn_Lambda_ri, np.float32),
        "L0_attn_p_ri": np.asarray(L0_attn_p_ri, np.float32),
        "L0_attn_q_ri": np.asarray(L0_attn_q_ri, np.float32),
        "L0_attn_Vc_ri": np.ascontiguousarray(
            np.asarray(L0_attn_Vc_ri, np.float32).reshape(NST, 2 * NST)),
        "L0_attn_Ct_ri": np.ascontiguousarray(
            np.asarray(L0_attn_Ct_ri, np.float32).reshape(NST, 2)),
        "L0_attn_Bp": np.asarray(L0_attn_Bp, np.float32).reshape(NST, 1),
        "L0_attn_D": np.asarray(L0_attn_D, np.float32).reshape(1, 1),
        "L0_attn_log_step": np.asarray(L0_attn_log_step, np.float32).reshape(1, 1),
        "L0_ffn_up": np.asarray(L0_ffn_up, np.float32),
        "L0_ffn_down": np.asarray(L0_ffn_down, np.float32),
        "permT": np.ascontiguousarray(
            np.asarray(L0_ffn_perm).astype(np.uint32).reshape(32, 128).T),
    }
    for nm in CONST_SPECS:
        base["c_" + nm] = consts[nm]

    in_maps = [dict(base, x=np.ascontiguousarray(x[c * T:(c + 1) * T]))
               for c in range(NCORE)]
    res = run_bass_kernel_spmd(p.nc, in_maps, core_ids=list(range(NCORE)))
    y = np.concatenate([np.asarray(r["y"]) for r in res.results], axis=0)
    return y.reshape(2, 4096, L)


def ln_tile(p, env, blk, t, rev):
    nc = p.nc
    s_dve, s_act, s_dx = p.s["dve"], p.s["act"], p.s["dx"]
    xt = env["xres"][:, t * L:(t + 1) * L]
    wrk, B0_O = env["wrk"], env["B0_O"]
    wrkt = wrk
    b0h = wrk
    slot = B0_O + (t % 2) * 4096
    delt = env["delt"]
    scr32 = env["scr32"]
    sums = env["sums"]

    if rev:
        nf = b0h.shape[1]
        b0dst = AP(b0h, slot + 127, [[nf, 128], [128, 32], [-1, 128]])
    else:
        b0dst = b0h[:, slot:slot + 4096]

    sdx = p.s["dx"] if (t < p.NT // 2 or p.NT == 1) else p.s["dx2"]

    @blk.vector
    def _(v):
        v.wait_ge(sdx.h, sdx.n)
        v.wait_ge(p.s["bnd"].h, env.get("x_ready", {}).get(t, 0))
        v.wait_ge(p.s["dtp"].h, env["b0_free"][t % 2])
        s_dve.inc(v.scalar_tensor_tensor(
            out=wrkt[:, slot:slot + 2048], in0=xt[:, 0:2048], scalar=1.0,
            in1=xt[:, 0:2048], op0=ALU.bypass, op1=ALU.mult,
            accum_out=scr32[:, 160 + t:161 + t]))
        s_dve.inc(v.scalar_tensor_tensor(
            out=wrkt[:, slot + 2048:slot + 4096], in0=xt[:, 2048:4096], scalar=1.0,
            in1=xt[:, 2048:4096], op0=ALU.bypass, op1=ALU.mult,
            accum_out=scr32[:, 176 + t:177 + t]))
        env["sq_done"][t] = s_dve.n

    @blk.scalar
    def _(sc):
        sc.wait_ge(sdx.h, sdx.n)
        sc.wait_ge(s_dve.h, env["sq_done"][t])
        s_act.inc(sc.activation(out=b0dst, in_=xt, func=AF.Copy,
                                accum_out=sums[:, t:t + 1]))
        env["act_cast"][t] = s_act.n


def ln_half(p, env, blk, h, ts):
    s_dve, s_act = p.s["dve"], p.s["act"]
    sums, mcol, rstd, nmr, scr32 = (env[n] for n in
                                    ["sums", "mcol", "rstd", "nmr", "scr32"])
    ssq = env["ssq"]
    wrk, B0_O = env["wrk"], env["B0_O"]
    n = len(ts)
    t0 = ts[0]
    sl = slice(t0, t0 + n)

    @blk.vector
    def _(v):
        v.wait_ge(s_act.h, s_act.n)
        v.wait_ge(s_dve.h, s_dve.n)
        s_dve.inc(v.tensor_scalar(out=mcol[:, sl], in0=sums[:, sl],
                                  scalar1=1.0 / L, scalar2=None, op0=ALU.mult))
        s_dve.inc(v.tensor_add(out=ssq[:, sl], in0=scr32[:, 160 + t0:160 + t0 + n],
                               in1=scr32[:, 176 + t0:176 + t0 + n]))
        v.wait_ge(s_dve.h, s_dve.n)
        s_dve.inc(v.tensor_mul(out=scr32[:, 184:184 + n], in0=mcol[:, sl],
                               in1=mcol[:, sl]))
        v.wait_ge(s_dve.h, s_dve.n)
        s_dve.inc(v.scalar_tensor_tensor(out=ssq[:, sl], in0=ssq[:, sl],
                                         scalar=1.0 / L, in1=scr32[:, 184:184 + n],
                                         op0=ALU.mult, op1=ALU.subtract))

    @blk.scalar
    def _(sc):
        sc.wait_ge(s_dve.h, s_dve.n)
        s_act.inc(sc.activation(out=rstd[:, sl], in_=ssq[:, sl], func=AF.Sqrt,
                                bias=env["epsc"][:, 0:1], scale=1.0))

    @blk.vector
    def _(v):
        v.wait_ge(s_act.h, s_act.n)
        s_dve.inc(v.reciprocal(out=rstd[:, sl], in_=rstd[:, sl]))
        v.wait_ge(s_dve.h, s_dve.n)
        s_dve.inc(v.scalar_tensor_tensor(out=nmr[:, sl], in0=mcol[:, sl],
                                         scalar=-1.0, in1=rstd[:, sl],
                                         op0=ALU.mult, op1=ALU.mult))
        v.wait_ge(s_dve.h, s_dve.n)
        for t in ts:
            slot = B0_O + (t % 2) * 4096
            s_dve.inc(v.tensor_scalar(out=wrk[:, slot:slot + 4096],
                                      in0=wrk[:, slot:slot + 4096],
                                      scalar1=rstd[:, t:t + 1],
                                      scalar2=nmr[:, t:t + 1],
                                      op0=ALU.mult, op1=ALU.add))
            env["b0_ready"][t] = s_dve.n


def ln_pair(p, env, tag, h, ts, rev, kc_after=False, xbars=True):
    nc = p.nc
    with p.block(f"{tag}_lnp{ts[0]}") as blk:
        for t in ts:
            ln_tile(p, env, blk, t, rev)
        ln_half(p, env, blk, h, ts)
    if xbars:
        ln_xbars(p, env, tag, h, ts)
    if kc_after:
        kc_step(p, env, 2)


def ln_xbars(p, env, tag, h, ts):
    nc = p.nc
    NTH, TPH = p.NT // p.NH, p.TPH
    s_dve, s_dtp, s_dwr, s_pe = p.s["dve"], p.s["dtp"], p.s["dwr"], p.s["pe"]
    wrk, B0_O, FMT_O = env["wrk"], env["B0_O"], env["FMT_O"]
    with p.block(f"{tag}_xb{ts[0]}") as blk:
        @blk.sync
        def _(sy):
            for t in ts:
                sy.wait_ge(s_dve.h, env["b0_ready"][t])
            sy.wait_ge(s_dwr.h, s_dwr.n)
            sy.wait_ge(p.s["dc"].h, p.s["dc"].n)
            sy.wait_ge(p.s["dg"].h, p.s["dg"].n)
            for nm in ["gt0", "gt1", "sc", "ctp0", "ctp1", "cr0", "cr1"]:
                if nm in p.s:
                    sy.wait_ge(p.s[nm].h, p.s[nm].n)
            for t in ts:
                i = t - h * NTH
                slot = B0_O + (t % 2) * 4096
                sy.wait_ge(s_pe.h, max(env.get("stripe_free", {}).get(i, 0),
                                       env.get("wrk_pe_free", 0),
                                       env.get("uth_pe_gate", 0)))
                dst = wrk[:, FMT_O:FMT_O + 32 * TPH] \
                    .rearrange("p (ft tk) -> p ft tk", ft=32)[:, :, i * 128:(i + 1) * 128]
                s_dtp.inc(sy.dma_start(out=dst, in_=wrk[:, slot:slot + 4096],
                                       transpose=True), 16)
            sy.wait_ge(s_dtp.h, s_dtp.n)
            for t in ts:
                env["b0_free"][t % 2] = s_dtp.n
                env.setdefault("uth_pair_xb", {})[t] = s_dtp.n


def s1_write_half(p, env, tag, h):
    nc = p.nc
    TPH, T = p.TPH, p.T
    s_dwr = p.s["dwr"]
    wrk, FMT_O = env["wrk"], env["FMT_O"]
    s1 = env["s1"]
    with p.block(f"{tag}_s1w{h}") as blk:
        @blk.sync
        def _(sy):
            src = wrk[:, FMT_O:FMT_O + 32 * TPH] \
                .rearrange("p (ft tk) -> p ft tk", ft=32)
            dst = AP(s1, h * TPH, [[32 * T, 128], [T, 32], [1, TPH]])
            s_dwr.inc(sy.dma_start(out=dst, in_=src), 16)
            env["s1_half"][h] = s_dwr.n
    env.setdefault("done_s1w", set()).add((tag, h))


def ln_and_transpose_half(p, env, tag, h, rev, write_s1, kc_after=False):
    NTH = p.NT // p.NH
    for i in range(0, NTH, 2):
        ts = list(range(h * NTH + i, h * NTH + min(i + 2, NTH)))
        if (tag, ts[0]) in env.get("done_pairs", set()):
            continue
        if (tag, ts[0]) in env.get("computed_pairs", set()):
            ln_xbars(p, env, tag, h, ts)
            continue
        ln_pair(p, env, tag, h, ts, rev, kc_after=kc_after)
    if write_s1 and (tag, h) not in env.get("done_s1w", set()):
        s1_write_half(p, env, tag, h)


def build_ffn(p, env, tag):
    nc = p.nc
    T, NT, NH, TPH = p.T, p.NT, p.NH, p.TPH
    NTH = NT // NH
    s_pe, s_dve, s_act, s_dg, s_dtp, s_dwr = (p.s[n] for n in
                                              ["pe", "dve", "act", "dg", "dtp", "dwr"])
    s_gt = [p.sem("gt0"), p.sem("gt1")]
    s_sc = p.sem("sc")
    wrk = env["wrk"]
    GB_O, WU_O, WD_O, HID_O, DO_O = (env[n] for n in
                                     ["GB_O", "WU_O", "WD_O", "HID_O", "DO_O"])
    up_in, dn_in, s1, s2, idxt, idxn = (env[n] for n in
                                        ["up_in", "dn_in", "s1", "s2", "idxt", "idxn"])
    psU, psD = env["psU"], env["psD"]
    delt, xres = env["delt"], env["xres"]

    env.setdefault("b0_free", {0: 0, 1: 0})
    env.setdefault("sq_done", {})
    env.setdefault("b0_freed", {0: 0, 1: 0})
    env.setdefault("b0_ready", {})
    env.setdefault("act_cast", {})
    env.setdefault("s1_half", {})
    env.setdefault("delt_free", 0)

    if env.pop("f2A_defer_xb", False) and tag == "f2":
        ln_xbars(p, env, "f2A", 1, [6, 7])
        s1_write_half(p, env, "f2A", 1)
    else:
        for h in range(NH):
            ln_and_transpose_half(p, env, tag + "A", h, rev=False, write_s1=True,
                                  kc_after=(tag == "f1"))
    if tag == "f1":
        kc_flush(p, env)

    CHS = [(c * 512, min(512, T - c * 512)) for c in range((T + 511) // 512)]
    gb_free = {k: 0 for k in range(4)}
    wu_free = {0: 0, 1: 0}
    dout_free = {0: 0, 1: 0}
    gather_done = {}
    w_done = {}
    gelu_done = {}
    evac_done = {}
    up_done = {}
    s1_final = s_dwr.n

    def prefetch2(g, head):
        if head >= 16:
            return
        g.wait_ge(s_dwr.h, s1_final)
        g.wait_ge(s_dtp.h, s_dtp.n)
        sg = s_gt[head % 2]
        for e in range(2):
            k = 2 * head + e
            g.wait_ge(s_pe.h, gb_free[k % 4])
            sg.inc(g.indirect_dma_start(
                out=wrk[:, GB_O + (k % 4) * T:GB_O + (k % 4) * T + T],
                out_offset=None,
                in_=AP(s1, 0, [[T, L], [1, T]]),
                in_offset=IndirectOffsetOnAxis(ap=idxt[:, k:k + 1], axis=0)), 16)
        g.wait_ge(s_pe.h, wu_free[head % 2])
        sg.inc(g.dma_start(
            out=wrk[:, WU_O + (head % 2) * 2048:WU_O + (head % 2) * 2048 + 2048],
            in_=up_in[head].rearrange("(c pp) m -> pp c m", pp=128)), 16)
        sg.inc(g.dma_start(
            out=wrk[:, WD_O + (head % 2) * 2048:WD_O + (head % 2) * 2048 + 2048],
            in_=dn_in[head].rearrange("(c pp) m -> pp c m", pp=128)), 16)
        gather_done[head] = (sg, sg.n)

    for head in range(17):
        if head < 16:
            for j in range(8):
                gelu_done[(head, j)] = s_act.n + j + 1
        with p.block(f"{tag}B{head}") as blk:
            @blk.gpsimd
            def _(g):
                if head > 0:
                    ph = head - 1
                    g.wait_ge(s_dve.h, evac_done[ph])
                    for e in range(2):
                        k = 2 * ph + e
                        s_sc.inc(g.indirect_dma_start(
                            out=s2[:, :],
                            out_offset=IndirectOffsetOnAxis(ap=idxn[:, k:k + 1], axis=0),
                            in_=wrk[:, DO_O + e * T:DO_O + e * T + T],
                            in_offset=None), 16)
                    dout_free[0] = dout_free[1] = s_sc.n
                if head == 0:
                    prefetch2(g, 0)
                    prefetch2(g, 1)
                else:
                    prefetch2(g, head + 1)

            if head >= 16:
                continue

            @blk.tensor
            def _(te):
                if head == 0 and env.get("kc_done_dve"):
                    te.wait_ge(s_dve.h, env["kc_done_dve"])
                te.wait_ge(gather_done[head][0].h, gather_done[head][1])
                wus = WU_O + (head % 2) * 2048
                wds = WD_O + (head % 2) * 2048
                gb0 = wrk[:, GB_O + ((2 * head) % 4) * T:GB_O + ((2 * head) % 4) * T + T]
                gb1 = wrk[:, GB_O + ((2 * head + 1) % 4) * T:GB_O + ((2 * head + 1) % 4) * T + T]
                for j in range(9):
                    if j < 8:
                        ju = j % 2
                        if j >= 2:
                            te.wait_ge(s_act.h, gelu_done[(head, j - 2)])
                        elif head > 0:
                            te.wait_ge(s_act.h, gelu_done[(head - 1, 8 - 2 + j)])
                        for co, cn in CHS:
                            te.matmul(psU[:, ju * 1024 + co:ju * 1024 + co + cn],
                                      wrk[:, wus + j * 128:wus + j * 128 + 128],
                                      gb0[:, co:co + cn], start=True, stop=False)
                        for co, cn in CHS:
                            mm = te.matmul(psU[:, ju * 1024 + co:ju * 1024 + co + cn],
                                           wrk[:, wus + 1024 + j * 128:wus + 1024 + j * 128 + 128],
                                           gb1[:, co:co + cn], start=False, stop=True)
                        s_pe.inc(mm)
                        up_done[(head, j)] = s_pe.n
                    if j >= 1:
                        jd = j - 1
                        te.wait_ge(s_act.h, gelu_done[(head, jd)])
                        if jd == 0 and head > 0:
                            te.wait_ge(s_dve.h, evac_done[head - 1])
                        hs = HID_O + (jd % 3) * T
                        for e in range(2):
                            for ci, (co, cn) in enumerate(CHS):
                                mm = te.matmul(
                                    psD[:, (2 * e + ci) * 512:(2 * e + ci) * 512 + cn],
                                    wrk[:, wds + jd * 256 + e * 128:wds + jd * 256 + e * 128 + 128],
                                    wrk[:, hs + co:hs + co + cn],
                                    start=(jd == 0), stop=(jd == 7))
                        if jd == 7:
                            s_pe.inc(mm)
                    if j < 8:
                        gb_free[(2 * head) % 4] = s_pe.n
                        gb_free[(2 * head + 1) % 4] = s_pe.n
                wu_free[head % 2] = s_pe.n
                env["ffnB_pe_done"] = s_pe.n
                env["wrk_pe_free"] = s_pe.n

            @blk.scalar
            def _(sc):
                for j in range(8):
                    sc.wait_ge(s_pe.h, up_done[(head, j)])
                    tw = CHS[-1][0] + CHS[-1][1]
                    n = s_act.inc(sc.activation(
                        out=wrk[:, HID_O + (j % 3) * T:HID_O + (j % 3) * T + tw],
                        in_=psU[:, (j % 2) * 1024:(j % 2) * 1024 + tw], func=AF.Gelu))
                    assert gelu_done[(head, j)] == n

            @blk.vector
            def _(v):
                v.wait_ge(s_pe.h, s_pe.n)
                if head > 0:
                    v.wait_ge(s_sc.h, dout_free[0])
                tw = CHS[-1][0] + CHS[-1][1]
                for e in range(2):
                    for ci, (co, cn) in enumerate(CHS):
                        s_dve.inc(v.tensor_copy(
                            out=wrk[:, DO_O + e * T + co:DO_O + e * T + co + cn],
                            in_=psD[:, (2 * e + ci) * 512:(2 * e + ci) * 512 + cn]))
                evac_done[head] = s_dve.n

    s_bnd = p.sem("bnd")
    psb = psU.bitcast(BF16)
    identb = env["identb"]
    NQ = 4 if T >= 512 else 1
    QW = T // NQ
    last = (tag == "f2")
    s_dst = p.s["dst"]
    y_out = env["y_out"]
    post_tile_cb = env.pop("postC_cb_next", None) if tag == "f1" else None
    post_tile_cb = post_tile_cb or {}

    if C_DEBUG == "skip":
        env.setdefault("x_ready", {})
        for tt in range(NT):
            env["x_ready"][tt] = s_bnd.n
        env["delt_free"] = p.s["dve"].n
        return
    read_done = {}
    TPQ0 = NT // NQ
    for q in range(NQ):
        env[f"{tag}_cq{q}"] = p.s["pe"].n + (q + 1) * TPQ0 * 32
    s_cr = [p.sem("cr0"), p.sem("cr1")]
    with p.block(f"{tag}Crd") as blk:
        @blk.sync
        def _(sy):
            sy.wait_ge(s_sc.h, s_sc.n)
            sy.wait_ge(s_gt[0].h, s_gt[0].n)
            sy.wait_ge(s_gt[1].h, s_gt[1].n)
            sy.wait_ge(s_dg.h, s_dg.n)
            for q in range(NQ):
                if q >= 2 and C_DEBUG not in ("read",):
                    sy.wait_ge(s_pe.h, env[f"{tag}_cq{q - 2}"])
                for piece in range(4):
                    src = AP(s2, q * QW + piece * 8 * 128 * T,
                             [[T, 128], [128 * T, 8], [1, QW]])
                    s_cr[q % 2].inc(sy.dma_start(
                        out=wrk[:, (q % 2) * 32 * QW + piece * 8 * QW:
                                (q % 2) * 32 * QW + (piece + 1) * 8 * QW]
                        .rearrange("p (g tok) -> p g tok", g=8),
                        in_=src), 16)
                read_done[q] = (s_cr[q % 2], s_cr[q % 2].n)

    TPQ = NT // NQ
    trans_done = {}
    slot_free = {k: 0 for k in range(8)}
    bnd_base = s_bnd.n
    kk = [0]
    if C_DEBUG == "read":
        env.setdefault("x_ready", {})
        for tt in range(NT):
            env["x_ready"][tt] = s_bnd.n
        env["delt_free"] = p.s["dve"].n
        return
    for q in range(NQ):
        with p.block(f"{tag}Cmm{q}") as blk:
            @blk.tensor
            def _(te):
                if q == 0:
                    te.wait_ge(s_act.h, s_act.n)
                    te.wait_ge(s_dve.h, s_dve.n)
                te.wait_ge(read_done[q][0].h, read_done[q][1])
                for tl in range(TPQ):
                    tt = q * TPQ + tl
                    for g in range(32):
                        k = kk[0]
                        G = k // 8
                        if G >= 2 and k % 8 == 0 and C_DEBUG not in ("mm", "nowar"):
                            te.wait_ge(s_bnd.h, bnd_base + (G - 1) * 8)
                        mm = te.transpose(
                            psb[:, (G % 2) * 1024 + (k % 8) * 128:
                                (G % 2) * 1024 + (k % 8) * 128 + 128],
                            wrk[:, (q % 2) * 32 * QW + g * QW + tl * 128:
                                (q % 2) * 32 * QW + g * QW + tl * 128 + 128],
                            identb[:])
                        s_pe.inc(mm)
                        if k % 8 == 7:
                            trans_done[G] = s_pe.n
                        kk[0] += 1
                assert env[f"{tag}_cq{q}"] == s_pe.n

            @blk.vector
            def _(v):
                if C_DEBUG == "mm":
                    return
                for tl in range(TPQ):
                    tt = q * TPQ + tl
                    for g in range(32):
                        k2 = tt * 32 + g
                        G2 = k2 // 8
                        if k2 % 8 == 0:
                            v.wait_ge(s_pe.h, trans_done[G2])
                        xv = xres[:, tt * L + g * 128:tt * L + g * 128 + 128]
                        s_bnd.inc(v.scalar_tensor_tensor(
                            out=xv,
                            in0=psb[:, (G2 % 2) * 1024 + (k2 % 8) * 128:
                                    (G2 % 2) * 1024 + (k2 % 8) * 128 + 128],
                            scalar=0.5, in1=xv, op0=ALU.mult, op1=ALU.add))
                    env.setdefault("x_ready", {})[tt] = s_bnd.n

            if last and C_DEBUG != "mm":
                @blk.scalar
                def _(sc):
                    for tl in range(TPQ):
                        tt = q * TPQ + tl
                        sc.wait_ge(s_bnd.h, bnd_base + (tt + 1) * 32)
                        s_dst.inc(sc.dma_start(
                            out=y_out[tt * 128:(tt + 1) * 128, :],
                            in_=xres[:, tt * L:(tt + 1) * L]), 16)
                env["stored"] = True
        if q in post_tile_cb:
            post_tile_cb[q]()
    if C_DEBUG == "mm":
        env.setdefault("x_ready", {})
        for tt in range(NT):
            env["x_ready"][tt] = bnd_base
        env["stored"] = False
    env["c_stt_done"] = s_bnd.n
    env["uth_pe_gate"] = env[f"{tag}_cq{NQ - 1}"]

    env["delt_free"] = s_dve.n
def ssm_half_mm(p, env, h, cbs):
    nc = p.nc
    T, NT, NH, TPH = p.T, p.NT, p.NH, p.TPH
    NTH = NT // NH
    s_pe, s_dve, s_dg = p.s["pe"], p.s["dve"], p.s["dg"]
    s_bnd = p.sem("bnd")
    wrk, xres = env["wrk"], env["xres"]
    FMT_O = env["FMT_O"]
    psU, psD = env["psU"], env["psD"]
    delt_t, tct_t = env["delt"], env["tct"]
    band_free = env["band_free"]

    def band_ap(b):
        return (psU if b < 4 else psD)[:, (b % 4) * 512:(b % 4 + 1) * 512]

    pe_base = s_pe.n
    env.setdefault("stripe_free", {})
    for tloc in range(NTH):
        env["stripe_free"][tloc] = pe_base + 8 * (tloc + 1)
    s_dtp_h = p.s["dtp"]
    uth_ready = s_dtp_h.n

    for tloc in range(NTH):
        t = h * NTH + tloc
        with p.block(f"ssmM{h}_{tloc}") as blk:
            stt = {}

            @blk.tensor
            def _(te):
                te.wait_ge(s_dg.h, env["tcat_done"])
                te.wait_ge(s_dtp_h.h, env.get("uth_pair_xb", {}).get(t, uth_ready))
                for jb in range(32):
                    lhs = wrk[:, FMT_O + jb * TPH + tloc * 128:
                              FMT_O + jb * TPH + tloc * 128 + 128]
                    bmin = max(0, jb // 4)
                    for b in range(bmin, 8):
                        if jb == 0:
                            te.wait_ge(s_bnd.h, band_free[b])
                        sidx = 3 + 4 * b - jb
                        st_, sp_ = (jb == 0), (jb == min(4 * b + 3, 31))
                        if sidx <= 28:
                            mm = te.matmul(band_ap(b), lhs,
                                           delt_t[:, sidx * 128:sidx * 128 + 512],
                                           start=st_, stop=sp_)
                        else:
                            mm = te.matmul(band_ap(b), lhs,
                                           tct_t[:, (sidx - 29) * 128:(sidx - 29) * 128 + 512],
                                           start=st_, stop=sp_)
                        if sp_:
                            s_pe.inc(mm)
                            stt[b] = s_pe.n

            @blk.vector
            def _(v):
                for b in range(8):
                    v.wait_ge(s_pe.h, stt[b])
                    xv = xres[:, t * L + b * 512:t * L + (b + 1) * 512]
                    s_bnd.inc(v.tensor_add(out=xv, in0=xv, in1=band_ap(b)))
                    band_free[b] = s_bnd.n
                env.setdefault("x_ready", {})[t] = s_bnd.n
        assert s_pe.n == env["stripe_free"][tloc], (s_pe.n, env["stripe_free"][tloc])
        if tloc in cbs:
            cbs[tloc]()


def build_ssm(p, env, post_cbs=None):
    nc = p.nc
    NTH = p.NT // p.NH
    s_dve, s_dg, s_dtp = p.s["dve"], p.s["dg"], p.s["dtp"]
    kext = env["kext"]
    env["band_free"] = {b: env.get("c_stt_done", 0) for b in range(8)}

    with p.block("ssmT") as blk:
        @blk.gpsimd
        def _(g):
            g.wait_ge(p.s["dwr"].h, env.get("kext_done", p.s["dwr"].n))
            g.wait_ge(s_dtp.h, s_dtp.n)
            s_dg.inc(g.dma_start(out=env["delt"][:, 0:4096],
                                 in_=AP(kext, 3585, [[1, 128], [1, 4096]])), 16)
            s_dg.inc(g.dma_start(out=env["tct"][:],
                                 in_=AP(kext, 3585 + 3712, [[1, 128], [1, 768]])), 16)
        env["tcat_done"] = s_dg.n

    ln_and_transpose_half(p, env, "ssm0", 0, rev=True, write_s1=False)

    if p.NH == 2:
        half1 = list(range(p.NT // 2, p.NT))
        prs = [half1[i:i + 2] for i in range(0, NTH, 2)]
        cbs0 = {}
        for k, pr in enumerate(prs):
            def _cb(pr=pr):
                ln_pair(p, env, "ssm1", 1, pr, True)
                env.setdefault("done_pairs", set()).add(("ssm1", pr[0]))
            cbs0[min(k, NTH - 1)] = _cb
        ssm_half_mm(p, env, 0, cbs0)
        ssm_half_mm(p, env, 1, post_cbs or {})
    else:
        ssm_half_mm(p, env, 0, post_cbs or {})


def build_store(p, env):
    nc = p.nc
    NT = p.NT
    s_dve, s_dst = p.s["dve"], p.s["dst"]
    xres, y_out = env["xres"], env["y_out"]
    with p.block("store") as blk:
        @blk.sync
        def _(sy):
            sy.wait_ge(s_dve.h, s_dve.n)
            sy.wait_ge(p.s["bnd"].h, p.s["bnd"].n)
            if not env.get("stored"):
                for t in range(NT):
                    s_dst.inc(sy.dma_start(out=y_out[t * 128:(t + 1) * 128, :],
                                           in_=xres[:, t * L:(t + 1) * L]), 16)
            sy.wait_ge(s_dst.h, s_dst.n)


# version 13
# speedup vs baseline: 1.1565x; 1.1565x over previous
import numpy as np
import ml_dtypes
import concourse.bass as bass
import concourse.mybir as mybir
from concourse.bass import AP, IndirectOffsetOnAxis
from concourse.bass_utils import run_bass_kernel_spmd

F32 = mybir.dt.float32
BF16 = mybir.dt.bfloat16
U32 = mybir.dt.uint32
AF = mybir.ActivationFunctionType
ALU = mybir.AluOpType
C_DEBUG = None
AXX = mybir.AxisListType.X

L = 4096
NST = 64
NCORE = 8
EPS = 1e-5
KCH = 512
NKC = L // KCH


def host_constants():
    lf = np.arange(L, dtype=np.float32) / np.float32(L)
    Omega = np.exp(np.complex64(-2j * np.pi) * lf.astype(np.complex64)).astype(np.complex64)
    h = ((np.complex64(1.0) - Omega) / (np.complex64(1.0) + Omega)).astype(np.complex64)
    c = (np.complex64(2.0) / (np.complex64(1.0) + Omega)).astype(np.complex64)

    def tile32(v):
        return np.ascontiguousarray(v.reshape(32, 128).T)

    consts = {
        "hre_b": np.ascontiguousarray(np.broadcast_to(h.real.astype(np.float32), (NST, L))),
        "him_b": np.ascontiguousarray(np.broadcast_to(h.imag.astype(np.float32), (NST, L))),
        "c_re": tile32(c.real.astype(np.float32)),
        "c_im": tile32(c.imag.astype(np.float32)),
    }
    lt_ = np.arange(32)
    W64 = np.exp(2j * np.pi * np.arange(64) / 64.0)
    ev = W64[np.newaxis, :] ** (2 * lt_)[:, None]
    od = W64[np.newaxis, :] ** (2 * lt_ + 1)[:, None]
    consts["W64ev_re"] = ev.real.astype(np.float32)
    consts["W64ev_im"] = ev.imag.astype(np.float32)
    consts["W64ev_imn"] = (-ev.imag).astype(np.float32)
    consts["W64od_re"] = od.real.astype(np.float32)
    consts["W64od_im"] = od.imag.astype(np.float32)
    consts["W64od_imn"] = (-od.imag).astype(np.float32)
    l1_ = np.arange(64)
    Wt = np.exp(2j * np.pi * np.outer(l1_, np.arange(64)) / 4096.0)
    consts["Wt_re"] = Wt.real.astype(np.float32)
    consts["Wt_im"] = Wt.imag.astype(np.float32)
    W2 = np.exp(2j * np.pi * np.outer(l1_, np.arange(64)) / 64.0) / L
    consts["W2_re"] = W2.real.astype(np.float32)
    consts["W2_imn"] = (-W2.imag).astype(np.float32)
    consts["ones64"] = np.ones((1, 64), np.float32)
    consts["ident"] = np.eye(128, dtype=np.float32)
    consts["identb"] = np.eye(128, dtype=np.float32).astype(ml_dtypes.bfloat16)
    return consts


CONST_SPECS = {
    "hre_b": ([NST, L], F32), "him_b": ([NST, L], F32),
    "c_re": ([128, 32], F32), "c_im": ([128, 32], F32),
    "W64ev_re": ([32, 64], F32), "W64ev_im": ([32, 64], F32),
    "W64ev_imn": ([32, 64], F32),
    "W64od_re": ([32, 64], F32), "W64od_im": ([32, 64], F32),
    "W64od_imn": ([32, 64], F32),
    "Wt_re": ([64, 64], F32), "Wt_im": ([64, 64], F32),
    "W2_re": ([64, 64], F32), "W2_imn": ([64, 64], F32),
    "ones64": ([1, 64], F32), "ident": ([128, 128], F32),
    "identb": ([128, 128], BF16),
}


class Sem:
    __slots__ = ("h", "n")

    def __init__(self, nc, name):
        self.h = nc.alloc_semaphore(name)
        self.n = 0

    def inc(self, inst, k=1):
        inst.then_inc(self.h, k)
        self.n += k
        return self.n


class Prog:
    def __init__(self, T):
        self.blk = None
        self.T = T
        self.NT = T // 128
        self.NH = 2 if self.NT >= 2 else 1
        self.TPH = T // self.NH
        self.nc = bass.Bass("TRN2", target_bir_lowering=False)
        self.s = {}
        self.t = {}

    def sem(self, name):
        if name not in self.s:
            self.s[name] = Sem(self.nc, name)
        return self.s[name]

    from contextlib import contextmanager

    @contextmanager
    def block(self, name):
        yield self.blk


def build_program(T=1024, debug=(), phases=("f1", "ssm", "f2"), hoist_f2a=True):
    p = Prog(T)
    nc = p.nc
    NT, NH, TPH = p.NT, p.NH, p.TPH
    NTH = NT // NH

    x_in = nc.dram_tensor("x", [T, L], F32, kind="ExternalInput")
    y_out = nc.dram_tensor("y", [T, L], F32, kind="ExternalOutput")
    lam_in = nc.dram_tensor("L0_attn_Lambda_ri", [NST, 2], F32, kind="ExternalInput")
    p_in = nc.dram_tensor("L0_attn_p_ri", [NST, 2], F32, kind="ExternalInput")
    q_in = nc.dram_tensor("L0_attn_q_ri", [NST, 2], F32, kind="ExternalInput")
    vc_in = nc.dram_tensor("L0_attn_Vc_ri", [NST, 2 * NST], F32, kind="ExternalInput")
    ct_in = nc.dram_tensor("L0_attn_Ct_ri", [NST, 2], F32, kind="ExternalInput")
    bp_in = nc.dram_tensor("L0_attn_Bp", [NST, 1], F32, kind="ExternalInput")
    d_in = nc.dram_tensor("L0_attn_D", [1, 1], F32, kind="ExternalInput")
    ls_in = nc.dram_tensor("L0_attn_log_step", [1, 1], F32, kind="ExternalInput")
    up_in = nc.dram_tensor("L0_ffn_up", [16, 256, 1024], F32, kind="ExternalInput")
    dn_in = nc.dram_tensor("L0_ffn_down", [16, 1024, 256], F32, kind="ExternalInput")
    pm_in = nc.dram_tensor("permT", [128, 32], U32, kind="ExternalInput")
    cin = {nm: nc.dram_tensor("c_" + nm, shp, dt, kind="ExternalInput")
           for nm, (shp, dt) in CONST_SPECS.items()}

    kext = nc.dram_tensor("kext", [2 * L], F32, kind="Internal")
    s1 = nc.dram_tensor("s1", [L * T], BF16, kind="Internal")
    s2 = nc.dram_tensor("s2", [L, T], BF16, kind="Internal")
    dbg = {}
    if "kext" in debug:
        dbg["kext"] = nc.dram_tensor("dbg_kext", [2 * L], F32, kind="ExternalOutput")
    if "tcat" in debug:
        dbg["tcat"] = nc.dram_tensor("dbg_tcat", [128, 4480], BF16, kind="ExternalOutput")
    for st in ("x1", "x2", "ln1", "s2d"):
        if st in debug:
            dbg[st] = nc.dram_tensor("dbg_" + st, [T, L],
                                     BF16 if st in ("ln1", "s2d") else F32,
                                     kind="ExternalOutput")

    XBYTES = NT * L * 4
    WRK_E = 24576
    RES_BYTES = XBYTES + WRK_E * 2 + 8192 + 5 * (KCH * 4)
    SB_BASE = (nc._sbuf_addr_for_side("left") + 63) & ~63
    nc.alloc_sbuf_tensor("reserve", [128, SB_BASE + RES_BYTES - 16384],
                         mybir.dt.uint8)

    def at(name, shape, dtype, off):
        h = nc.alloc_sbuf_tensor_at(name, shape, dtype, offset=SB_BASE + off)
        p.t[name] = h
        return h

    off = 0
    xres = at("xres", [128, NT * L], F32, 0); off += XBYTES
    wrk = at("wrk", [128, WRK_E], BF16, off); off += WRK_E * 2
    delt = at("delt", [128, 4096], BF16, off); off += 8192
    kch = {}
    for nm in ["dre", "dim", "mg", "rcre", "rcim"]:
        kch[nm] = at("k_" + nm, [NST, KCH], F32, off); off += KCH * 4

    def mk(name, shape, dtype):
        h = nc.alloc_sbuf_tensor(name, shape, dtype)
        p.t[name] = h
        return h

    sums = mk("sums", [128, NT], F32)
    ssq = mk("ssq", [128, NT], F32)
    mcol = mk("mcol", [128, NT], F32)
    rstd = mk("rstd", [128, NT], F32)
    nmr = mk("nmr", [128, NT], F32)
    idxn = mk("idxn", [128, 32], U32)
    idxt = mk("idxt", [128, 32], U32)
    idxf = mk("idxf", [128, 32], U32)
    idxf2 = mk("idxf2", [128, 32], U32)
    kbuf = mk("kbuf", [128, 256], F32)
    c1sb = mk("c1sb", [128, 256], F32)
    scr32 = mk("scr32", [128, 192], F32)
    cre = mk("cre", [128, 32], F32)
    cim = mk("cim", [128, 32], F32)
    ident = mk("ident", [128, 128], F32)
    identb = mk("identb", [128, 128], BF16)

    lam = mk("lam", [NST, 2], F32)
    pri = mk("pri", [NST, 2], F32)
    qri = mk("qri", [NST, 2], F32)
    ctri = mk("ctri", [NST, 2], F32)
    bp = mk("bp", [NST, 1], F32)
    vmat = mk("vmat", [NST, 8], F32)
    vcs = mk("vcs", [NST, 2 * NST], F32)
    vctr = mk("vctr", [NST, 2 * NST], F32)
    bvec = mk("bvec", [NST, 2], F32)
    w1 = mk("w1", [32, 6 * 64], F32)
    w2 = mk("w2", [NST, 8 * 64], F32)
    arT = mk("arT", [32, 2 * 128], F32)
    absb = mk("absb", [NST, 4 * 64], F32)
    ksb = mk("ksb", [NST, NST], F32)
    zsb = mk("zsb", [NST, NST], F32)
    sca = mk("sca", [1, 8], F32)
    s2c = mk("s2c", [NST, 1], F32)
    onesb = mk("onesb", [1, NST], F32)
    ddsb = mk("ddsb", [1, 1], F32)
    epsc = mk("epsc", [128, 1], F32)
    tct = mk("tct", [128, 768], BF16)

    FMT_O = 0
    B0_O = 32 * TPH
    GB_O = 0
    WU_O = 4 * T
    WD_O = 4 * T + 4096
    HID_O = 4 * T + 8192
    DO_O = 7 * T + 8192
    TC_O = B0_O

    psU = nc.alloc_psum_tensor("psU", [128, 2048], F32)
    psD = nc.alloc_psum_tensor("psD", [128, 2048], F32)

    def band_ap(b):
        return (psU if b < 4 else psD)[:, (b % 4) * 512:(b % 4 + 1) * 512]

    s_dx = p.sem("dx")
    s_dx2 = p.sem("dx2")
    s_dc = p.sem("dc")
    s_dg = p.sem("dg")
    s_dtp = p.sem("dtp")
    s_dwr = p.sem("dwr")
    s_pe = p.sem("pe")
    s_dve = p.sem("dve")
    s_act = p.sem("act")
    s_dst = p.sem("dst")
    s_bnd = p.sem("bnd")

    env = dict(locals())
    env['HOIST_F2A'] = hoist_f2a

    p.blk = nc.Block("all", no_gpsimd_drain=True)
    p.blk.__enter__()

    with p.block("ph0") as blk:
        @blk.sync
        def _(sy):
            s_dc.inc(sy.dma_start(out=cre[:], in_=cin["c_re"][:]), 16)
            s_dc.inc(sy.dma_start(out=cim[:], in_=cin["c_im"][:]), 16)
            s_dc.inc(sy.dma_start(out=ident[:], in_=cin["ident"][:]), 16)
            s_dc.inc(sy.dma_start(out=identb[:], in_=cin["identb"][:]), 16)
            s_dc.inc(sy.dma_start(out=onesb[:], in_=cin["ones64"][:]), 16)
            for i, nm in enumerate(["W64ev_re", "W64ev_im", "W64ev_imn",
                                    "W64od_re", "W64od_im", "W64od_imn"]):
                s_dc.inc(sy.dma_start(out=w1[:, i * 64:(i + 1) * 64], in_=cin[nm][:]), 16)
            for i, nm in enumerate(["Wt_re", "Wt_im", "W2_re", "W2_imn"]):
                s_dc.inc(sy.dma_start(out=w2[:, i * 64:(i + 1) * 64], in_=cin[nm][:]), 16)
            env["x_load"] = {}
            xsems = [s_dx, s_dx2, p.sem("dx3"), p.sem("dx4")]
            for t_ in range(NT):
                sm = xsems[(t_ // 2) % 4]
                sm.inc(sy.dma_start(out=xres[:, t_ * L:(t_ + 1) * L],
                                    in_=x_in[t_ * 128:(t_ + 1) * 128, :]), 16)
            for t_ in range(NT):
                sm = xsems[(t_ // 2) % 4]
                env["x_load"][t_] = (sm, sm.n)

        @blk.scalar
        def _(sc):
            for dst_, src_ in [(lam, lam_in), (pri, p_in), (qri, q_in),
                               (ctri, ct_in), (bp, bp_in), (ddsb, d_in),
                               (vcs, vc_in)]:
                s_dc.inc(sc.dma_start(out=dst_[:], in_=src_[:]), 16)
            s_dc.inc(sc.dma_start(out=sca[:, 0:1], in_=ls_in[:]), 16)

        @blk.gpsimd
        def _(g):
            s_dg.inc(g.dma_start(out=idxn[:], in_=pm_in[:]), 16)

        @blk.vector
        def _(v):
            p.s["dve"].inc(v.memset(epsc[:], EPS))

    build_idx(p, env)
    build_kcomp_head(p, env)
    env["kc_pending"] = list(range(NKC))
    if "f1" in phases:
        if "ssm" in phases and NH == 2 and NT // NH == 4:
            def _pc1():
                ln_pair(p, env, "ssm0", 0, [0, 1], True, xbars=False)
                env.setdefault("computed_pairs", set()).add(("ssm0", 0))
            env["postC_cb_next"] = {1: _pc1}
        build_ffn(p, env, "f1")
    else:
        kc_flush(p, env)
    dump_dbg(p, env, "x1")
    if "ssm" in phases:
        cbs1 = {}
        if "f2" in phases and NH == 2 and NT // NH == 4:
            def _cb0():
                ln_pair(p, env, "f2A", 0, [0, 1], False)
                env.setdefault("done_pairs", set()).add(("f2A", 0))

            def _cb1():
                ln_pair(p, env, "f2A", 0, [2, 3], False)
                env.setdefault("done_pairs", set()).add(("f2A", 2))

            def _cb2():
                ln_pair(p, env, "f2A", 1, [4, 5], False, xbars=False)

            def _cb3():
                s1_write_half(p, env, "f2A", 0)
                ln_xbars(p, env, "f2A", 1, [4, 5])
                ln_pair(p, env, "f2A", 1, [6, 7], False, xbars=False)
                env.setdefault("done_pairs", set()).update(
                    {("f2A", 4), ("f2A", 6)})
                env["f2A_defer_xb"] = True

            cbs1 = {0: _cb0, 1: _cb1, 2: _cb2, 3: _cb3} if env.get('HOIST_F2A', True) else {}
        elif "f2" in phases and NH == 2 and NT // NH == 2:
            def _cb0():
                ln_pair(p, env, "f2A", 0, [0, 1], False)
                env.setdefault("done_pairs", set()).add(("f2A", 0))
            cbs1 = {0: _cb0}
        build_ssm(p, env, post_cbs=cbs1)
    dump_dbg(p, env, "x2")
    if "f2" in phases:
        build_ffn(p, env, "f2")
    build_store(p, env)
    finish_program(p)
    out_names = ["y"] + ["dbg_" + k for k in dbg]
    return p, env, out_names


def finish_program(p):
    nc = p.nc
    with p.block("fin") as blk:
        @blk.sync
        def _(sy):
            for nm in ["dx", "dx2", "dx3", "dx4", "dc", "dtp", "dwr", "dst",
                       "ctp0", "ctp1", "cr0", "cr1"]:
                if p.s[nm].n if nm in p.s else 0:
                    sy.wait_ge(p.s[nm].h, p.s[nm].n)

        @blk.gpsimd
        def _(g):
            for nm in ["dg", "gt0", "gt1", "sc"]:
                if nm in p.s and p.s[nm].n:
                    g.wait_ge(p.s[nm].h, p.s[nm].n)
    p.blk.__exit__(None, None, None)


def kc_step(p, env, n=2):
    for _ in range(n):
        if env.get("kc_pending"):
            ch = env["kc_pending"].pop(0)
            kc_chunk(p, env, ch)


def kc_flush(p, env):
    while env.get("kc_pending"):
        kc_chunk(p, env, env["kc_pending"].pop(0))
    if not env.get("kc_tail_done"):
        build_kcomp_tail(p, env)
        env["kc_tail_done"] = True
        env["kc_done_dve"] = p.s["dve"].n
        env["kc_done_pe"] = p.s["pe"].n


def build_kcomp_head(p, env):
    nc = p.nc
    s_dc, s_dve, s_act, s_pe, s_dwr, s_dg = (p.s[n] for n in
                                             ["dc", "dve", "act", "pe", "dwr", "dg"])
    g_ = env
    kch = g_["kch"]
    (lam, pri, qri, ctri, bp, vmat, vcs, vctr, bvec, w1, w2, arT, absb, ksb,
     zsb, sca, s2c, onesb, ddsb, ident, cre, cim, kbuf, c1sb, scr32,
     psU, psD, kext, dbg) = (g_[n] for n in [
         "lam", "pri", "qri", "ctri", "bp", "vmat", "vcs", "vctr", "bvec",
         "w1", "w2", "arT", "absb", "ksb", "zsb", "sca", "s2c", "onesb",
         "ddsb", "ident", "cre", "cim", "kbuf", "c1sb", "scr32",
         "psU", "psD", "kext", "dbg"])
    NCONST = s_dc.n

    with p.block("kc_sc") as blk:
        @blk.scalar
        def _(sc):
            sc.wait_ge(s_dc.h, NCONST)
            s_act.inc(sc.activation(out=sca[:, 1:2], in_=sca[:, 0:1], func=AF.Exp))

        @blk.vector
        def _(v):
            v.wait_ge(s_act.h, s_act.n)
            s_dve.inc(v.reciprocal(out=sca[:, 2:3], in_=sca[:, 1:2]))
            v.wait_ge(s_dve.h, s_dve.n)
            s_dve.inc(v.tensor_scalar(out=sca[:, 3:4], in0=sca[:, 2:3],
                                      scalar1=2.0, scalar2=None, op0=ALU.mult))

        @blk.tensor
        def _(te):
            te.wait_ge(s_dve.h, s_dve.n)
            s_pe.inc(te.matmul(psD[:NST, 0:1], onesb[:, :NST], sca[:, 3:4],
                               start=True, stop=True))

        @blk.vector
        def _(v):
            v.wait_ge(s_pe.h, s_pe.n)
            s_dve.inc(v.tensor_copy(out=s2c[:], in_=psD[:NST, 0:1]))

    with p.block("kc_vv") as blk:
        @blk.vector
        def _(v):
            v.wait_ge(s_dc.h, NCONST)
            vcr = vcs[:].rearrange("p (c two) -> p two c", two=2)
            s_dve.inc(v.tensor_copy(out=vctr[:, 0:NST], in_=vcr[:, 0, :]))
            s_dve.inc(v.tensor_copy(out=vctr[:, NST:2 * NST], in_=vcr[:, 1, :]))

        @blk.tensor
        def _(te):
            te.wait_ge(s_dve.h, s_dve.n)
            s_pe.inc(te.transpose(psU[:NST, 0:NST], vctr[:NST, 0:NST], ident[:NST, :NST]))
            s_pe.inc(te.transpose(psU[:NST, 512:512 + NST], vctr[:NST, NST:2 * NST],
                                  ident[:NST, :NST]))

        @blk.vector
        def _(v):
            v.wait_ge(s_pe.h, s_pe.n)
            s_dve.inc(v.tensor_copy(out=vcs[:, 0:NST], in_=psU[:NST, 0:NST]))
            s_dve.inc(v.tensor_copy(out=vcs[:, NST:2 * NST], in_=psU[:NST, 512:512 + NST]))

        @blk.tensor
        def _(te):
            te.wait_ge(s_dve.h, s_dve.n)
            s_pe.inc(te.matmul(psD[:NST, 512:513], vcs[:NST, 0:NST], bp[:NST, :],
                               start=True, stop=True))
            s_pe.inc(te.matmul(psD[:NST, 1024:1025], vcs[:NST, NST:2 * NST], bp[:NST, :],
                               start=True, stop=True))

        @blk.vector
        def _(v):
            v.wait_ge(s_pe.h, s_pe.n)
            s_dve.inc(v.tensor_copy(out=bvec[:, 0:1], in_=psD[:NST, 512:513]))
            s_dve.inc(v.tensor_copy(out=bvec[:, 1:2], in_=psD[:NST, 1024:1025]))
            v.wait_ge(s_dve.h, s_dve.n)

            def cmul_conj(col, a_re, a_im, c_re_, c_im_):
                s_dve.inc(v.tensor_mul(out=scr32[:NST, 0:1], in0=a_re, in1=c_re_))
                s_dve.inc(v.tensor_mul(out=scr32[:NST, 1:2], in0=a_im, in1=c_im_))
                s_dve.inc(v.tensor_mul(out=scr32[:NST, 2:3], in0=a_re, in1=c_im_))
                s_dve.inc(v.tensor_mul(out=scr32[:NST, 3:4], in0=a_im, in1=c_re_))
                v.wait_ge(s_dve.h, s_dve.n)
                s_dve.inc(v.tensor_add(out=vmat[:, col:col + 1],
                                       in0=scr32[:NST, 0:1], in1=scr32[:NST, 1:2]))
                s_dve.inc(v.tensor_sub(out=vmat[:, col + 1:col + 2],
                                       in0=scr32[:NST, 2:3], in1=scr32[:NST, 3:4]))
                v.wait_ge(s_dve.h, s_dve.n)

            cmul_conj(0, ctri[:, 0:1], ctri[:, 1:2], bvec[:, 0:1], bvec[:, 1:2])
            cmul_conj(2, ctri[:, 0:1], ctri[:, 1:2], pri[:, 0:1], pri[:, 1:2])
            cmul_conj(4, qri[:, 0:1], qri[:, 1:2], bvec[:, 0:1], bvec[:, 1:2])
            cmul_conj(6, qri[:, 0:1], qri[:, 1:2], pri[:, 0:1], pri[:, 1:2])


def kc_chunk(p, env, ch):
    nc = p.nc
    g_ = env
    kch = env["kch"]
    s_dc, s_dve, s_act, s_pe, s_dwr, s_dg = (p.s[n] for n in
                                             ["dc", "dve", "act", "pe", "dwr", "dg"])
    vmat, psU = env["vmat"], env["psU"]
    lam, s2c = env["lam"], env["s2c"]
    NLT = KCH // 128
    if True:
        with p.block(f"kc_ch{ch}") as blk:
            @blk.gpsimd
            def _(g2):
                g2.wait_ge(s_dve.h, s_dve.n)
                g2.wait_ge(s_pe.h, s_pe.n)
                s_dg.inc(g2.dma_start(out=kch["dre"][:],
                                      in_=g_["cin"]["hre_b"][:, ch * KCH:(ch + 1) * KCH]), 16)
                s_dg.inc(g2.dma_start(out=kch["dim"][:],
                                      in_=g_["cin"]["him_b"][:, ch * KCH:(ch + 1) * KCH]), 16)

            @blk.vector
            def _(v):
                v.wait_ge(s_dg.h, s_dg.n)
                s_dve.inc(v.tensor_scalar(out=kch["dre"][:], in0=kch["dre"][:],
                                          scalar1=s2c[:], scalar2=lam[:, 0:1],
                                          op0=ALU.mult, op1=ALU.subtract))
                s_dve.inc(v.tensor_scalar(out=kch["dim"][:], in0=kch["dim"][:],
                                          scalar1=s2c[:], scalar2=lam[:, 1:2],
                                          op0=ALU.mult, op1=ALU.subtract))
                v.wait_ge(s_dve.h, s_dve.n)
                s_dve.inc(v.tensor_mul(out=kch["mg"][:], in0=kch["dre"][:], in1=kch["dre"][:]))
                s_dve.inc(v.tensor_mul(out=kch["rcim"][:], in0=kch["dim"][:], in1=kch["dim"][:]))
                v.wait_ge(s_dve.h, s_dve.n)
                s_dve.inc(v.tensor_add(out=kch["mg"][:], in0=kch["mg"][:], in1=kch["rcim"][:]))
                v.wait_ge(s_dve.h, s_dve.n)
                s_dve.inc(v.reciprocal(out=kch["mg"][:], in_=kch["mg"][:]))
                v.wait_ge(s_dve.h, s_dve.n)
                s_dve.inc(v.tensor_mul(out=kch["rcre"][:], in0=kch["dre"][:], in1=kch["mg"][:]))
                s_dve.inc(v.scalar_tensor_tensor(out=kch["rcim"][:], in0=kch["dim"][:],
                                                 scalar=-1.0, in1=kch["mg"][:],
                                                 op0=ALU.mult, op1=ALU.mult))

            @blk.tensor
            def _(te):
                te.wait_ge(s_dve.h, s_dve.n)
                for i in range(NLT):
                    lt = ch * NLT + i
                    s_pe.inc(te.matmul(psU[:, lt * 8:lt * 8 + 8],
                                       kch["rcre"][:, i * 128:(i + 1) * 128],
                                       vmat[:NST, :], start=True, stop=True))
                    s_pe.inc(te.matmul(psU[:, 512 + lt * 8:512 + lt * 8 + 8],
                                       kch["rcim"][:, i * 128:(i + 1) * 128],
                                       vmat[:NST, :], start=True, stop=True))


def build_kcomp_tail(p, env):
    nc = p.nc
    g_ = env
    (lam, pri, qri, ctri, bp, vmat, vcs, vctr, bvec, w1, w2, arT, absb, ksb,
     zsb, sca, s2c, onesb, ddsb, ident, cre, cim, kbuf, c1sb, scr32,
     psU, psD, kext, dbg) = (g_[n] for n in [
         "lam", "pri", "qri", "ctri", "bp", "vmat", "vcs", "vctr", "bvec",
         "w1", "w2", "arT", "absb", "ksb", "zsb", "sca", "s2c", "onesb",
         "ddsb", "ident", "cre", "cim", "kbuf", "c1sb", "scr32",
         "psU", "psD", "kext", "dbg"])
    s_dc, s_dve, s_act, s_pe, s_dwr, s_dg = (p.s[n] for n in
                                             ["dc", "dve", "act", "pe", "dwr", "dg"])
    with p.block("kc_comb") as blk:
        @blk.vector
        def _(v):
            v.wait_ge(s_pe.h, s_pe.n)
            s_dve.inc(v.tensor_copy(out=c1sb[:], in_=psU[:, 0:256]))
            v.wait_ge(s_dve.h, s_dve.n)
            c1r = c1sb[:].rearrange("p (lt j) -> p j lt", j=8)
            c2r = psU[:, 512:768].rearrange("p (lt j) -> p j lt", j=8)
            kr = kbuf[:].rearrange("p (g lt) -> p g lt", g=8)
            for j in range(4):
                s_dve.inc(v.tensor_sub(out=kr[:, 2 * j, :], in0=c1r[:, 2 * j, :],
                                       in1=c2r[:, 2 * j + 1, :]))
                s_dve.inc(v.tensor_add(out=kr[:, 2 * j + 1, :], in0=c1r[:, 2 * j + 1, :],
                                       in1=c2r[:, 2 * j, :]))
            v.wait_ge(s_dve.h, s_dve.n)

    with p.block("kc_ar") as blk:
        @blk.vector
        def _(v):
            kr = kbuf[:].rearrange("p (g lt) -> p g lt", g=8)
            k00r, k00i = kr[:, 0, :], kr[:, 1, :]
            k01r, k01i = kr[:, 2, :], kr[:, 3, :]
            k10r, k10i = kr[:, 4, :], kr[:, 5, :]
            k11r, k11i = kr[:, 6, :], kr[:, 7, :]
            S = scr32[:].rearrange("p (g c) -> p g c", c=32)
            tre, tim, dre, dmag, u1 = (S[:, i, :] for i in range(5))
            s_dve.inc(v.tensor_mul(out=tre, in0=k01r, in1=k10r))
            s_dve.inc(v.tensor_mul(out=u1, in0=k01i, in1=k10i))
            s_dve.inc(v.tensor_mul(out=tim, in0=k01r, in1=k10i))
            s_dve.inc(v.tensor_mul(out=dre, in0=k01i, in1=k10r))
            v.wait_ge(s_dve.h, s_dve.n)
            s_dve.inc(v.tensor_sub(out=tre, in0=tre, in1=u1))
            s_dve.inc(v.tensor_add(out=tim, in0=tim, in1=dre))
            v.wait_ge(s_dve.h, s_dve.n)
            s_dve.inc(v.tensor_scalar(out=dre, in0=k11r, scalar1=1.0, scalar2=None,
                                      op0=ALU.add))
            v.wait_ge(s_dve.h, s_dve.n)
            s_dve.inc(v.tensor_mul(out=dmag, in0=dre, in1=dre))
            s_dve.inc(v.tensor_mul(out=u1, in0=k11i, in1=k11i))
            v.wait_ge(s_dve.h, s_dve.n)
            s_dve.inc(v.tensor_add(out=dmag, in0=dmag, in1=u1))
            v.wait_ge(s_dve.h, s_dve.n)
            s_dve.inc(v.reciprocal(out=dmag, in_=dmag))
            v.wait_ge(s_dve.h, s_dve.n)
            s_dve.inc(v.tensor_mul(out=u1, in0=tre, in1=dre))
            s_dve.inc(v.tensor_mul(out=k10r, in0=tim, in1=k11i))
            s_dve.inc(v.tensor_mul(out=k10i, in0=tim, in1=dre))
            s_dve.inc(v.tensor_mul(out=k01r, in0=tre, in1=k11i))
            v.wait_ge(s_dve.h, s_dve.n)
            s_dve.inc(v.tensor_add(out=u1, in0=u1, in1=k10r))
            s_dve.inc(v.tensor_sub(out=k10i, in0=k10i, in1=k01r))
            v.wait_ge(s_dve.h, s_dve.n)
            s_dve.inc(v.tensor_mul(out=u1, in0=u1, in1=dmag))
            s_dve.inc(v.tensor_mul(out=k10i, in0=k10i, in1=dmag))
            v.wait_ge(s_dve.h, s_dve.n)
            s_dve.inc(v.tensor_sub(out=k00r, in0=k00r, in1=u1))
            s_dve.inc(v.tensor_sub(out=k00i, in0=k00i, in1=k10i))
            v.wait_ge(s_dve.h, s_dve.n)
            s_dve.inc(v.tensor_mul(out=k01r, in0=cre[:], in1=k00r))
            s_dve.inc(v.tensor_mul(out=u1, in0=cim[:], in1=k00i))
            s_dve.inc(v.tensor_mul(out=k01i, in0=cre[:], in1=k00i))
            s_dve.inc(v.tensor_mul(out=k10r, in0=cim[:], in1=k00r))
            v.wait_ge(s_dve.h, s_dve.n)
            s_dve.inc(v.tensor_sub(out=k01r, in0=k01r, in1=u1))
            s_dve.inc(v.tensor_add(out=k01i, in0=k01i, in1=k10r))

    with p.block("kc_idft") as blk:
        kr = kbuf[:].rearrange("p (g lt) -> p g lt", g=8)
        ar_re, ar_im = kr[:, 2, :], kr[:, 3, :]

        @blk.tensor
        def _(te):
            te.wait_ge(s_dve.h, s_dve.n)
            s_pe.inc(te.transpose(psD[:32, 0:128], ar_re, ident[:, :]))
            s_pe.inc(te.transpose(psD[:32, 512:512 + 128], ar_im, ident[:, :]))

        @blk.vector
        def _(v):
            v.wait_ge(s_pe.h, s_pe.n)
            s_dve.inc(v.tensor_copy(out=arT[:, 0:128], in_=psD[:32, 0:128]))
            s_dve.inc(v.tensor_copy(out=arT[:, 128:256], in_=psD[:32, 512:512 + 128]))

        @blk.tensor
        def _(te):
            te.wait_ge(s_dve.h, s_dve.n)
            arL, arH = arT[:32, 0:64], arT[:32, 64:128]
            aiL, aiH = arT[:32, 128:192], arT[:32, 192:256]
            wev_re, wev_im, wev_imn = w1[:32, 0:64], w1[:32, 64:128], w1[:32, 128:192]
            wod_re, wod_im, wod_imn = w1[:32, 192:256], w1[:32, 256:320], w1[:32, 320:384]
            pa_re, pa_im = psU[:NST, 1024:1088], psU[:NST, 1536:1600]
            s_pe.inc(te.matmul(pa_re, arL, wev_re, start=True, stop=False))
            te.matmul(pa_re, arH, wod_re, start=False, stop=False)
            te.matmul(pa_re, aiL, wev_imn, start=False, stop=False)
            s_pe.inc(te.matmul(pa_re, aiH, wod_imn, start=False, stop=True))
            s_pe.inc(te.matmul(pa_im, arL, wev_im, start=True, stop=False))
            te.matmul(pa_im, arH, wod_im, start=False, stop=False)
            te.matmul(pa_im, aiL, wev_re, start=False, stop=False)
            s_pe.inc(te.matmul(pa_im, aiH, wod_re, start=False, stop=True))

        @blk.vector
        def _(v):
            v.wait_ge(s_pe.h, s_pe.n)
            s_dve.inc(v.tensor_copy(out=absb[:, 0:64], in_=psU[:NST, 1024:1088]))
            s_dve.inc(v.tensor_copy(out=absb[:, 64:128], in_=psU[:NST, 1536:1600]))
            v.wait_ge(s_dve.h, s_dve.n)
            a_re, a_im = absb[:, 0:64], absb[:, 64:128]
            b_re, b_im = absb[:, 128:192], absb[:, 192:256]
            wt_re, wt_im = w2[:, 0:64], w2[:, 64:128]
            s_dve.inc(v.tensor_mul(out=b_re, in0=a_re, in1=wt_re))
            s_dve.inc(v.tensor_mul(out=b_im, in0=a_re, in1=wt_im))
            s_dve.inc(v.tensor_mul(out=scr32[:NST, 0:64], in0=a_im, in1=wt_im))
            s_dve.inc(v.tensor_mul(out=scr32[:NST, 64:128], in0=a_im, in1=wt_re))
            v.wait_ge(s_dve.h, s_dve.n)
            s_dve.inc(v.tensor_sub(out=b_re, in0=b_re, in1=scr32[:NST, 0:64]))
            s_dve.inc(v.tensor_add(out=b_im, in0=b_im, in1=scr32[:NST, 64:128]))

        @blk.tensor
        def _(te):
            te.wait_ge(s_dve.h, s_dve.n)
            pk = psD[:NST, 1024:1088]
            s_pe.inc(te.matmul(pk, w2[:, 128:192], absb[:, 128:192], start=True, stop=False))
            s_pe.inc(te.matmul(pk, w2[:, 192:256], absb[:, 192:256], start=False, stop=True))

        @blk.vector
        def _(v):
            v.wait_ge(s_pe.h, s_pe.n)
            s_dve.inc(v.tensor_add(out=psD[:1, 1024:1025], in0=psD[:1, 1024:1025],
                                   in1=ddsb[:, :]))
            v.wait_ge(s_dve.h, s_dve.n)
            s_dve.inc(v.tensor_copy(out=ksb[:], in_=psD[:NST, 1024:1088]))
            s_dve.inc(v.memset(zsb[:], 0.0))

        @blk.sync
        def _(sy):
            sy.wait_ge(s_dve.h, s_dve.n)
            s_dwr.inc(sy.dma_start(out=AP(kext, 0, [[64, 64], [1, 64]]), in_=zsb[:]), 16)
            s_dwr.inc(sy.dma_start(out=AP(kext, L, [[64, 64], [1, 64]]), in_=ksb[:]), 16)
            env["kext_done"] = s_dwr.n
            if "kext" in dbg:
                sy.wait_ge(s_dwr.h, s_dwr.n)
                s_dwr.inc(sy.dma_start(out=AP(dbg["kext"], 0, [[1, 1], [1, 2 * L]]),
                                       in_=AP(kext, 0, [[1, 1], [1, 2 * L]])), 16)
            if "tcat" in dbg:
                pass
            sy.wait_ge(s_dwr.h, s_dwr.n)

    if "tcat" in dbg:
        wrk = env_wrk(p)
        with p.block("kc_tcdbg") as blk:
            @blk.gpsimd
            def _(g):
                g.wait_ge(s_dwr.h, s_dwr.n)
                s_dg.inc(g.dma_start(out=wrk[:, 16384:16384 + 4480],
                                     in_=AP(kext, 3585, [[1, 128], [1, 4480]])), 16)
                g.wait_ge(s_dg.h, s_dg.n)
                s_dg.inc(g.dma_start(out=dbg["tcat"][:], in_=wrk[:, 16384:16384 + 4480]), 16)
                g.wait_ge(s_dg.h, s_dg.n)


def dump_dbg(p, env, st):
    if st not in env["dbg"]:
        return
    nc = p.nc
    with p.block("dbg_" + st) as blk:
        @blk.sync
        def _(sy):
            sy.wait_ge(p.s["dve"].h, p.s["dve"].n)
            sy.wait_ge(p.s["bnd"].h, p.s["bnd"].n)
            for t_ in range(p.NT):
                p.s["dst"].inc(sy.dma_start(
                    out=env["dbg"][st][t_ * 128:(t_ + 1) * 128, :],
                    in_=env["xres"][:, t_ * L:(t_ + 1) * L]), 16)
            sy.wait_ge(p.s["dst"].h, p.s["dst"].n)


def env_wrk(p):
    return p.t["wrk"]


def env_hsrc(p, which, ch):
    return p.t["hri"]


def build_idx(p, env):
    nc = p.nc
    s_dg, s_dve = p.s["dg"], p.s["dve"]
    idxn, idxt, idxf, idxf2 = (env[n] for n in ["idxn", "idxt", "idxf", "idxf2"])
    with p.block("idx") as blk:
        @blk.vector
        def _(v):
            v.wait_ge(s_dg.h, 16)
            s_dve.inc(v.tensor_scalar(out=idxf[:], in0=idxn[:], scalar1=127,
                                      scalar2=5, op0=ALU.bitwise_and,
                                      op1=ALU.logical_shift_left))
            s_dve.inc(v.tensor_scalar(out=idxf2[:], in0=idxn[:], scalar1=7,
                                      scalar2=None, op0=ALU.logical_shift_right))
            v.wait_ge(s_dve.h, s_dve.n)
            s_dve.inc(v.tensor_tensor(out=idxt[:], in0=idxf[:], in1=idxf2[:],
                                      op=ALU.add))


_CACHE = {}


def kernel(L0_attn_u, L0_attn_Lambda_ri, L0_attn_p_ri, L0_attn_q_ri,
           L0_attn_Vc_ri, L0_attn_Ct_ri, L0_attn_Bp, L0_attn_D,
           L0_attn_log_step, L0_ffn_up, L0_ffn_down, L0_ffn_perm):
    T = 8192 // NCORE
    x = np.ascontiguousarray(np.asarray(L0_attn_u, dtype=np.float32).reshape(8192, L))
    if "prog" not in _CACHE:
        _CACHE["prog"] = build_program(T=T)
        _CACHE["consts"] = host_constants()
    p, env, outs = _CACHE["prog"]
    consts = _CACHE["consts"]

    base = {
        "L0_attn_Lambda_ri": np.asarray(L0_attn_Lambda_ri, np.float32),
        "L0_attn_p_ri": np.asarray(L0_attn_p_ri, np.float32),
        "L0_attn_q_ri": np.asarray(L0_attn_q_ri, np.float32),
        "L0_attn_Vc_ri": np.ascontiguousarray(
            np.asarray(L0_attn_Vc_ri, np.float32).reshape(NST, 2 * NST)),
        "L0_attn_Ct_ri": np.ascontiguousarray(
            np.asarray(L0_attn_Ct_ri, np.float32).reshape(NST, 2)),
        "L0_attn_Bp": np.asarray(L0_attn_Bp, np.float32).reshape(NST, 1),
        "L0_attn_D": np.asarray(L0_attn_D, np.float32).reshape(1, 1),
        "L0_attn_log_step": np.asarray(L0_attn_log_step, np.float32).reshape(1, 1),
        "L0_ffn_up": np.asarray(L0_ffn_up, np.float32),
        "L0_ffn_down": np.asarray(L0_ffn_down, np.float32),
        "permT": np.ascontiguousarray(
            np.asarray(L0_ffn_perm).astype(np.uint32).reshape(32, 128).T),
    }
    for nm in CONST_SPECS:
        base["c_" + nm] = consts[nm]

    in_maps = [dict(base, x=np.ascontiguousarray(x[c * T:(c + 1) * T]))
               for c in range(NCORE)]
    res = run_bass_kernel_spmd(p.nc, in_maps, core_ids=list(range(NCORE)))
    y = np.concatenate([np.asarray(r["y"]) for r in res.results], axis=0)
    return y.reshape(2, 4096, L)


def b0_loc(p, env, tag, t):
    if tag == "f1A":
        sid = t % 3
        if sid == 2:
            return env["delt"], 0, 2
    else:
        sid = t % 2
    return env["wrk"], env["B0_O"] + sid * 4096, sid


def ln_tile(p, env, blk, t, rev, tag):
    nc = p.nc
    s_dve, s_act, s_dx = p.s["dve"], p.s["act"], p.s["dx"]
    xt = env["xres"][:, t * L:(t + 1) * L]
    wrk, B0_O = env["wrk"], env["B0_O"]
    b0h, slot, sid = b0_loc(p, env, tag, t)
    wrkt = b0h
    scr32 = env["scr32"]
    sums = env["sums"]

    if rev:
        nf = b0h.shape[1]
        b0dst = AP(b0h, slot + 127, [[nf, 128], [128, 32], [-1, 128]])
    else:
        b0dst = b0h[:, slot:slot + 4096]

    sdx, sdxv = env["x_load"][t]

    @blk.vector
    def _(v):
        v.wait_ge(sdx.h, sdxv)
        v.wait_ge(p.s["bnd"].h, env.get("x_ready", {}).get(t, 0))
        v.wait_ge(p.s["dtp"].h, env["b0_free"][sid])
        s_dve.inc(v.scalar_tensor_tensor(
            out=wrkt[:, slot:slot + 2048], in0=xt[:, 0:2048], scalar=1.0,
            in1=xt[:, 0:2048], op0=ALU.bypass, op1=ALU.mult,
            accum_out=scr32[:, 160 + t:161 + t]))
        s_dve.inc(v.scalar_tensor_tensor(
            out=wrkt[:, slot + 2048:slot + 4096], in0=xt[:, 2048:4096], scalar=1.0,
            in1=xt[:, 2048:4096], op0=ALU.bypass, op1=ALU.mult,
            accum_out=scr32[:, 176 + t:177 + t]))
        env["sq_done"][t] = s_dve.n

    @blk.scalar
    def _(sc):
        sc.wait_ge(sdx.h, sdxv)
        sc.wait_ge(s_dve.h, env["sq_done"][t])
        s_act.inc(sc.activation(out=b0dst, in_=xt, func=AF.Copy,
                                accum_out=sums[:, t:t + 1]))
        env["act_cast"][t] = s_act.n


def ln_half(p, env, blk, h, ts, tag):
    s_dve, s_act = p.s["dve"], p.s["act"]
    sums, mcol, rstd, nmr, scr32 = (env[n] for n in
                                    ["sums", "mcol", "rstd", "nmr", "scr32"])
    ssq = env["ssq"]
    wrk, B0_O = env["wrk"], env["B0_O"]
    n = len(ts)
    t0 = ts[0]
    sl = slice(t0, t0 + n)

    @blk.vector
    def _(v):
        v.wait_ge(s_act.h, s_act.n)
        v.wait_ge(s_dve.h, s_dve.n)
        s_dve.inc(v.tensor_scalar(out=mcol[:, sl], in0=sums[:, sl],
                                  scalar1=1.0 / L, scalar2=None, op0=ALU.mult))
        s_dve.inc(v.tensor_add(out=ssq[:, sl], in0=scr32[:, 160 + t0:160 + t0 + n],
                               in1=scr32[:, 176 + t0:176 + t0 + n]))
        v.wait_ge(s_dve.h, s_dve.n)
        s_dve.inc(v.tensor_mul(out=scr32[:, 184:184 + n], in0=mcol[:, sl],
                               in1=mcol[:, sl]))
        v.wait_ge(s_dve.h, s_dve.n)
        s_dve.inc(v.scalar_tensor_tensor(out=ssq[:, sl], in0=ssq[:, sl],
                                         scalar=1.0 / L, in1=scr32[:, 184:184 + n],
                                         op0=ALU.mult, op1=ALU.subtract))

    @blk.scalar
    def _(sc):
        sc.wait_ge(s_dve.h, s_dve.n)
        s_act.inc(sc.activation(out=rstd[:, sl], in_=ssq[:, sl], func=AF.Sqrt,
                                bias=env["epsc"][:, 0:1], scale=1.0))

    @blk.vector
    def _(v):
        v.wait_ge(s_act.h, s_act.n)
        s_dve.inc(v.reciprocal(out=rstd[:, sl], in_=rstd[:, sl]))
        v.wait_ge(s_dve.h, s_dve.n)
        s_dve.inc(v.scalar_tensor_tensor(out=nmr[:, sl], in0=mcol[:, sl],
                                         scalar=-1.0, in1=rstd[:, sl],
                                         op0=ALU.mult, op1=ALU.mult))
        v.wait_ge(s_dve.h, s_dve.n)
        for t in ts:
            b0h, slot, _sid = b0_loc(p, env, tag, t)
            s_dve.inc(v.tensor_scalar(out=b0h[:, slot:slot + 4096],
                                      in0=b0h[:, slot:slot + 4096],
                                      scalar1=rstd[:, t:t + 1],
                                      scalar2=nmr[:, t:t + 1],
                                      op0=ALU.mult, op1=ALU.add))
            env["b0_ready"][t] = s_dve.n


def ln_pair(p, env, tag, h, ts, rev, kc_after=False, xbars=True):
    nc = p.nc
    with p.block(f"{tag}_lnp{ts[0]}") as blk:
        for t in ts:
            ln_tile(p, env, blk, t, rev, tag)
        ln_half(p, env, blk, h, ts, tag)
    if xbars:
        ln_xbars(p, env, tag, h, ts)
    if kc_after:
        kc_step(p, env, 2)


def ln_xbars(p, env, tag, h, ts):
    nc = p.nc
    NTH, TPH = p.NT // p.NH, p.TPH
    s_dve, s_dtp, s_dwr, s_pe = p.s["dve"], p.s["dtp"], p.s["dwr"], p.s["pe"]
    wrk, B0_O, FMT_O = env["wrk"], env["B0_O"], env["FMT_O"]
    with p.block(f"{tag}_xb{ts[0]}") as blk:
        @blk.sync
        def _(sy):
            for t in ts:
                sy.wait_ge(s_dve.h, env["b0_ready"][t])
            sy.wait_ge(s_dwr.h, s_dwr.n)
            sy.wait_ge(p.s["dc"].h, p.s["dc"].n)
            sy.wait_ge(p.s["dg"].h, p.s["dg"].n)
            for nm in ["gt0", "gt1", "sc", "ctp0", "ctp1", "cr0", "cr1"]:
                if nm in p.s:
                    sy.wait_ge(p.s[nm].h, p.s[nm].n)
            for t in ts:
                i = t - h * NTH
                b0h, slot, _sid = b0_loc(p, env, tag, t)
                sy.wait_ge(s_pe.h, max(env.get("stripe_free", {}).get(i, 0),
                                       env.get("wrk_pe_free", 0),
                                       env.get("uth_pe_gate", 0)))
                dst = wrk[:, FMT_O:FMT_O + 32 * TPH] \
                    .rearrange("p (ft tk) -> p ft tk", ft=32)[:, :, i * 128:(i + 1) * 128]
                s_dtp.inc(sy.dma_start(out=dst, in_=b0h[:, slot:slot + 4096],
                                       transpose=True), 16)
            sy.wait_ge(s_dtp.h, s_dtp.n)
            for t in ts:
                env["b0_free"][b0_loc(p, env, tag, t)[2]] = s_dtp.n
                env.setdefault("uth_pair_xb", {})[t] = s_dtp.n


def s1_write_half(p, env, tag, h):
    nc = p.nc
    TPH, T = p.TPH, p.T
    s_dwr = p.s["dwr"]
    wrk, FMT_O = env["wrk"], env["FMT_O"]
    s1 = env["s1"]
    with p.block(f"{tag}_s1w{h}") as blk:
        @blk.sync
        def _(sy):
            src = wrk[:, FMT_O:FMT_O + 32 * TPH] \
                .rearrange("p (ft tk) -> p ft tk", ft=32)
            dst = AP(s1, h * TPH, [[32 * T, 128], [T, 32], [1, TPH]])
            s_dwr.inc(sy.dma_start(out=dst, in_=src), 16)
            env["s1_half"][h] = s_dwr.n
    env.setdefault("done_s1w", set()).add((tag, h))


def ln_and_transpose_half(p, env, tag, h, rev, write_s1, kc_after=False):
    NTH = p.NT // p.NH
    for i in range(0, NTH, 2):
        ts = list(range(h * NTH + i, h * NTH + min(i + 2, NTH)))
        if (tag, ts[0]) in env.get("done_pairs", set()):
            continue
        if (tag, ts[0]) in env.get("computed_pairs", set()):
            ln_xbars(p, env, tag, h, ts)
            continue
        ln_pair(p, env, tag, h, ts, rev, kc_after=kc_after)
    if write_s1 and (tag, h) not in env.get("done_s1w", set()):
        s1_write_half(p, env, tag, h)


def build_ffn(p, env, tag):
    nc = p.nc
    T, NT, NH, TPH = p.T, p.NT, p.NH, p.TPH
    NTH = NT // NH
    s_pe, s_dve, s_act, s_dg, s_dtp, s_dwr = (p.s[n] for n in
                                              ["pe", "dve", "act", "dg", "dtp", "dwr"])
    s_gt = [p.sem("gt0"), p.sem("gt1")]
    s_sc = p.sem("sc")
    wrk = env["wrk"]
    GB_O, WU_O, WD_O, HID_O, DO_O = (env[n] for n in
                                     ["GB_O", "WU_O", "WD_O", "HID_O", "DO_O"])
    up_in, dn_in, s1, s2, idxt, idxn = (env[n] for n in
                                        ["up_in", "dn_in", "s1", "s2", "idxt", "idxn"])
    psU, psD = env["psU"], env["psD"]
    delt, xres = env["delt"], env["xres"]

    env.setdefault("b0_free", {0: 0, 1: 0, 2: 0})
    env.setdefault("sq_done", {})
    env.setdefault("b0_freed", {0: 0, 1: 0})
    env.setdefault("b0_ready", {})
    env.setdefault("act_cast", {})
    env.setdefault("s1_half", {})
    env.setdefault("delt_free", 0)

    if env.pop("f2A_defer_xb", False) and tag == "f2":
        ln_xbars(p, env, "f2A", 1, [6, 7])
        s1_write_half(p, env, "f2A", 1)
    else:
        for h in range(NH):
            ln_and_transpose_half(p, env, tag + "A", h, rev=False, write_s1=True,
                                  kc_after=(tag == "f1"))
    if tag == "f1":
        kc_flush(p, env)

    CHS = [(c * 512, min(512, T - c * 512)) for c in range((T + 511) // 512)]
    gb_free = {k: 0 for k in range(4)}
    wu_free = {0: 0, 1: 0}
    dout_free = {0: 0, 1: 0}
    gather_done = {}
    w_done = {}
    gelu_done = {}
    evac_done = {}
    up_done = {}
    s1_final = s_dwr.n

    def prefetch2(g, head):
        if head >= 16:
            return
        g.wait_ge(s_dwr.h, s1_final)
        g.wait_ge(s_dtp.h, s_dtp.n)
        sg = s_gt[head % 2]
        for e in range(2):
            k = 2 * head + e
            g.wait_ge(s_pe.h, gb_free[k % 4])
            sg.inc(g.indirect_dma_start(
                out=wrk[:, GB_O + (k % 4) * T:GB_O + (k % 4) * T + T],
                out_offset=None,
                in_=AP(s1, 0, [[T, L], [1, T]]),
                in_offset=IndirectOffsetOnAxis(ap=idxt[:, k:k + 1], axis=0)), 16)
        g.wait_ge(s_pe.h, wu_free[head % 2])
        sg.inc(g.dma_start(
            out=wrk[:, WU_O + (head % 2) * 2048:WU_O + (head % 2) * 2048 + 2048],
            in_=up_in[head].rearrange("(c pp) m -> pp c m", pp=128)), 16)
        sg.inc(g.dma_start(
            out=wrk[:, WD_O + (head % 2) * 2048:WD_O + (head % 2) * 2048 + 2048],
            in_=dn_in[head].rearrange("(c pp) m -> pp c m", pp=128)), 16)
        gather_done[head] = (sg, sg.n)

    for head in range(17):
        if head < 16:
            for j in range(8):
                gelu_done[(head, j)] = s_act.n + j + 1
        with p.block(f"{tag}B{head}") as blk:
            @blk.gpsimd
            def _(g):
                if head > 0:
                    ph = head - 1
                    g.wait_ge(s_dve.h, evac_done[ph])
                    for e in range(2):
                        k = 2 * ph + e
                        s_sc.inc(g.indirect_dma_start(
                            out=s2[:, :],
                            out_offset=IndirectOffsetOnAxis(ap=idxn[:, k:k + 1], axis=0),
                            in_=wrk[:, DO_O + e * T:DO_O + e * T + T],
                            in_offset=None), 16)
                    dout_free[0] = dout_free[1] = s_sc.n
                if head == 0:
                    prefetch2(g, 0)
                    prefetch2(g, 1)
                else:
                    prefetch2(g, head + 1)

            if head >= 16:
                continue

            @blk.tensor
            def _(te):
                if head == 0 and env.get("kc_done_dve"):
                    te.wait_ge(s_dve.h, env["kc_done_dve"])
                te.wait_ge(gather_done[head][0].h, gather_done[head][1])
                wus = WU_O + (head % 2) * 2048
                wds = WD_O + (head % 2) * 2048
                gb0 = wrk[:, GB_O + ((2 * head) % 4) * T:GB_O + ((2 * head) % 4) * T + T]
                gb1 = wrk[:, GB_O + ((2 * head + 1) % 4) * T:GB_O + ((2 * head + 1) % 4) * T + T]
                for j in range(9):
                    if j < 8:
                        ju = j % 2
                        if j >= 2:
                            te.wait_ge(s_act.h, gelu_done[(head, j - 2)])
                        elif head > 0:
                            te.wait_ge(s_act.h, gelu_done[(head - 1, 8 - 2 + j)])
                        for co, cn in CHS:
                            te.matmul(psU[:, ju * 1024 + co:ju * 1024 + co + cn],
                                      wrk[:, wus + j * 128:wus + j * 128 + 128],
                                      gb0[:, co:co + cn], start=True, stop=False)
                        for co, cn in CHS:
                            mm = te.matmul(psU[:, ju * 1024 + co:ju * 1024 + co + cn],
                                           wrk[:, wus + 1024 + j * 128:wus + 1024 + j * 128 + 128],
                                           gb1[:, co:co + cn], start=False, stop=True)
                        s_pe.inc(mm)
                        up_done[(head, j)] = s_pe.n
                    if j >= 1:
                        jd = j - 1
                        te.wait_ge(s_act.h, gelu_done[(head, jd)])
                        if jd == 0 and head > 0:
                            te.wait_ge(s_dve.h, evac_done[head - 1])
                        hs = HID_O + (jd % 3) * T
                        for e in range(2):
                            for ci, (co, cn) in enumerate(CHS):
                                mm = te.matmul(
                                    psD[:, (2 * e + ci) * 512:(2 * e + ci) * 512 + cn],
                                    wrk[:, wds + jd * 256 + e * 128:wds + jd * 256 + e * 128 + 128],
                                    wrk[:, hs + co:hs + co + cn],
                                    start=(jd == 0), stop=(jd == 7))
                        if jd == 7:
                            s_pe.inc(mm)
                    if j < 8:
                        gb_free[(2 * head) % 4] = s_pe.n
                        gb_free[(2 * head + 1) % 4] = s_pe.n
                wu_free[head % 2] = s_pe.n
                env["ffnB_pe_done"] = s_pe.n
                env["wrk_pe_free"] = s_pe.n

            @blk.scalar
            def _(sc):
                for j in range(8):
                    sc.wait_ge(s_pe.h, up_done[(head, j)])
                    tw = CHS[-1][0] + CHS[-1][1]
                    n = s_act.inc(sc.activation(
                        out=wrk[:, HID_O + (j % 3) * T:HID_O + (j % 3) * T + tw],
                        in_=psU[:, (j % 2) * 1024:(j % 2) * 1024 + tw], func=AF.Gelu))
                    assert gelu_done[(head, j)] == n

            @blk.vector
            def _(v):
                v.wait_ge(s_pe.h, s_pe.n)
                if head > 0:
                    v.wait_ge(s_sc.h, dout_free[0])
                tw = CHS[-1][0] + CHS[-1][1]
                for e in range(2):
                    for ci, (co, cn) in enumerate(CHS):
                        s_dve.inc(v.tensor_copy(
                            out=wrk[:, DO_O + e * T + co:DO_O + e * T + co + cn],
                            in_=psD[:, (2 * e + ci) * 512:(2 * e + ci) * 512 + cn]))
                evac_done[head] = s_dve.n

    s_bnd = p.sem("bnd")
    psb = psU.bitcast(BF16)
    identb = env["identb"]
    NQ = 4 if T >= 512 else 1
    QW = T // NQ
    last = (tag == "f2")
    s_dst = p.s["dst"]
    y_out = env["y_out"]
    post_tile_cb = env.pop("postC_cb_next", None) if tag == "f1" else None
    post_tile_cb = post_tile_cb or {}

    if C_DEBUG == "skip":
        env.setdefault("x_ready", {})
        for tt in range(NT):
            env["x_ready"][tt] = s_bnd.n
        env["delt_free"] = p.s["dve"].n
        return
    read_done = {}
    TPQ0 = NT // NQ
    for q in range(NQ):
        env[f"{tag}_cq{q}"] = p.s["pe"].n + (q + 1) * TPQ0 * 32
    s_cr = [p.sem("cr0"), p.sem("cr1")]
    with p.block(f"{tag}Crd") as blk:
        @blk.sync
        def _(sy):
            sy.wait_ge(s_sc.h, s_sc.n)
            sy.wait_ge(s_gt[0].h, s_gt[0].n)
            sy.wait_ge(s_gt[1].h, s_gt[1].n)
            sy.wait_ge(s_dg.h, s_dg.n)
            for q in range(NQ):
                if q >= 2 and C_DEBUG not in ("read",):
                    sy.wait_ge(s_pe.h, env[f"{tag}_cq{q - 2}"])
                for piece in range(4):
                    src = AP(s2, q * QW + piece * 8 * 128 * T,
                             [[T, 128], [128 * T, 8], [1, QW]])
                    s_cr[q % 2].inc(sy.dma_start(
                        out=wrk[:, (q % 2) * 32 * QW + piece * 8 * QW:
                                (q % 2) * 32 * QW + (piece + 1) * 8 * QW]
                        .rearrange("p (g tok) -> p g tok", g=8),
                        in_=src), 16)
                read_done[q] = (s_cr[q % 2], s_cr[q % 2].n)

    TPQ = NT // NQ
    trans_done = {}
    slot_free = {k: 0 for k in range(8)}
    bnd_base = s_bnd.n
    kk = [0]
    if C_DEBUG == "read":
        env.setdefault("x_ready", {})
        for tt in range(NT):
            env["x_ready"][tt] = s_bnd.n
        env["delt_free"] = p.s["dve"].n
        return
    for q in range(NQ):
        with p.block(f"{tag}Cmm{q}") as blk:
            @blk.tensor
            def _(te):
                if q == 0:
                    te.wait_ge(s_act.h, s_act.n)
                    te.wait_ge(s_dve.h, s_dve.n)
                te.wait_ge(read_done[q][0].h, read_done[q][1])
                for tl in range(TPQ):
                    tt = q * TPQ + tl
                    for g in range(32):
                        k = kk[0]
                        G = k // 8
                        if G >= 2 and k % 8 == 0 and C_DEBUG not in ("mm", "nowar"):
                            te.wait_ge(s_bnd.h, bnd_base + (G - 1) * 8)
                        mm = te.transpose(
                            psb[:, (G % 2) * 1024 + (k % 8) * 128:
                                (G % 2) * 1024 + (k % 8) * 128 + 128],
                            wrk[:, (q % 2) * 32 * QW + g * QW + tl * 128:
                                (q % 2) * 32 * QW + g * QW + tl * 128 + 128],
                            identb[:])
                        s_pe.inc(mm)
                        if k % 8 == 7:
                            trans_done[G] = s_pe.n
                        kk[0] += 1
                assert env[f"{tag}_cq{q}"] == s_pe.n

            @blk.vector
            def _(v):
                if C_DEBUG == "mm":
                    return
                for tl in range(TPQ):
                    tt = q * TPQ + tl
                    for g in range(32):
                        k2 = tt * 32 + g
                        G2 = k2 // 8
                        if k2 % 8 == 0:
                            v.wait_ge(s_pe.h, trans_done[G2])
                        xv = xres[:, tt * L + g * 128:tt * L + g * 128 + 128]
                        s_bnd.inc(v.scalar_tensor_tensor(
                            out=xv,
                            in0=psb[:, (G2 % 2) * 1024 + (k2 % 8) * 128:
                                    (G2 % 2) * 1024 + (k2 % 8) * 128 + 128],
                            scalar=0.5, in1=xv, op0=ALU.mult, op1=ALU.add))
                    env.setdefault("x_ready", {})[tt] = s_bnd.n

            if last and C_DEBUG != "mm":
                @blk.scalar
                def _(sc):
                    for tl in range(TPQ):
                        tt = q * TPQ + tl
                        sc.wait_ge(s_bnd.h, bnd_base + (tt + 1) * 32)
                        s_dst.inc(sc.dma_start(
                            out=y_out[tt * 128:(tt + 1) * 128, :],
                            in_=xres[:, tt * L:(tt + 1) * L]), 16)
                env["stored"] = True
        if q in post_tile_cb:
            post_tile_cb[q]()
    if C_DEBUG == "mm":
        env.setdefault("x_ready", {})
        for tt in range(NT):
            env["x_ready"][tt] = bnd_base
        env["stored"] = False
    env["c_stt_done"] = s_bnd.n
    env["uth_pe_gate"] = env[f"{tag}_cq{NQ - 1}"]

    env["delt_free"] = s_dve.n
def ssm_half_mm(p, env, h, cbs):
    nc = p.nc
    T, NT, NH, TPH = p.T, p.NT, p.NH, p.TPH
    NTH = NT // NH
    s_pe, s_dve, s_dg = p.s["pe"], p.s["dve"], p.s["dg"]
    s_bnd = p.sem("bnd")
    wrk, xres = env["wrk"], env["xres"]
    FMT_O = env["FMT_O"]
    psU, psD = env["psU"], env["psD"]
    delt_t, tct_t = env["delt"], env["tct"]
    band_free = env["band_free"]

    def band_ap(b):
        return (psU if b < 4 else psD)[:, (b % 4) * 512:(b % 4 + 1) * 512]

    pe_base = s_pe.n
    env.setdefault("stripe_free", {})
    for tloc in range(NTH):
        env["stripe_free"][tloc] = pe_base + 8 * (tloc + 1)
    s_dtp_h = p.s["dtp"]
    uth_ready = s_dtp_h.n

    for tloc in range(NTH):
        t = h * NTH + tloc
        with p.block(f"ssmM{h}_{tloc}") as blk:
            stt = {}

            @blk.tensor
            def _(te):
                te.wait_ge(s_dg.h, env["tcat_done"])
                te.wait_ge(s_dtp_h.h, env.get("uth_pair_xb", {}).get(t, uth_ready))
                for jb in range(32):
                    lhs = wrk[:, FMT_O + jb * TPH + tloc * 128:
                              FMT_O + jb * TPH + tloc * 128 + 128]
                    bmin = max(0, jb // 4)
                    for b in range(bmin, 8):
                        if jb == 0:
                            te.wait_ge(s_bnd.h, band_free[b])
                        sidx = 3 + 4 * b - jb
                        st_, sp_ = (jb == 0), (jb == min(4 * b + 3, 31))
                        if sidx <= 28:
                            mm = te.matmul(band_ap(b), lhs,
                                           delt_t[:, sidx * 128:sidx * 128 + 512],
                                           start=st_, stop=sp_)
                        else:
                            mm = te.matmul(band_ap(b), lhs,
                                           tct_t[:, (sidx - 29) * 128:(sidx - 29) * 128 + 512],
                                           start=st_, stop=sp_)
                        if sp_:
                            s_pe.inc(mm)
                            stt[b] = s_pe.n

            @blk.vector
            def _(v):
                for b in range(8):
                    v.wait_ge(s_pe.h, stt[b])
                    xv = xres[:, t * L + b * 512:t * L + (b + 1) * 512]
                    s_bnd.inc(v.tensor_add(out=xv, in0=xv, in1=band_ap(b)))
                    band_free[b] = s_bnd.n
                env.setdefault("x_ready", {})[t] = s_bnd.n
        assert s_pe.n == env["stripe_free"][tloc], (s_pe.n, env["stripe_free"][tloc])
        if tloc in cbs:
            cbs[tloc]()


def build_ssm(p, env, post_cbs=None):
    nc = p.nc
    NTH = p.NT // p.NH
    s_dve, s_dg, s_dtp = p.s["dve"], p.s["dg"], p.s["dtp"]
    kext = env["kext"]
    env["band_free"] = {b: env.get("c_stt_done", 0) for b in range(8)}

    with p.block("ssmT") as blk:
        @blk.gpsimd
        def _(g):
            g.wait_ge(p.s["dwr"].h, env.get("kext_done", p.s["dwr"].n))
            g.wait_ge(s_dtp.h, s_dtp.n)
            s_dg.inc(g.dma_start(out=env["delt"][:, 0:4096],
                                 in_=AP(kext, 3585, [[1, 128], [1, 4096]])), 16)
            s_dg.inc(g.dma_start(out=env["tct"][:],
                                 in_=AP(kext, 3585 + 3712, [[1, 128], [1, 768]])), 16)
        env["tcat_done"] = s_dg.n

    ln_and_transpose_half(p, env, "ssm0", 0, rev=True, write_s1=False)

    if p.NH == 2:
        half1 = list(range(p.NT // 2, p.NT))
        prs = [half1[i:i + 2] for i in range(0, NTH, 2)]
        cbs0 = {}
        for k, pr in enumerate(prs):
            def _cb(pr=pr):
                ln_pair(p, env, "ssm1", 1, pr, True)
                env.setdefault("done_pairs", set()).add(("ssm1", pr[0]))
            cbs0[min(k, NTH - 1)] = _cb
        ssm_half_mm(p, env, 0, cbs0)
        ssm_half_mm(p, env, 1, post_cbs or {})
    else:
        ssm_half_mm(p, env, 0, post_cbs or {})


def build_store(p, env):
    nc = p.nc
    NT = p.NT
    s_dve, s_dst = p.s["dve"], p.s["dst"]
    xres, y_out = env["xres"], env["y_out"]
    with p.block("store") as blk:
        @blk.sync
        def _(sy):
            sy.wait_ge(s_dve.h, s_dve.n)
            sy.wait_ge(p.s["bnd"].h, p.s["bnd"].n)
            if not env.get("stored"):
                for t in range(NT):
                    s_dst.inc(sy.dma_start(out=y_out[t * 128:(t + 1) * 128, :],
                                           in_=xres[:, t * L:(t + 1) * L]), 16)
            sy.wait_ge(s_dst.h, s_dst.n)


# version 14
# speedup vs baseline: 1.1698x; 1.0115x over previous
import numpy as np
import ml_dtypes
import concourse.bass as bass
import concourse.mybir as mybir
from concourse.bass import AP, IndirectOffsetOnAxis
from concourse.bass_utils import run_bass_kernel_spmd

F32 = mybir.dt.float32
BF16 = mybir.dt.bfloat16
U32 = mybir.dt.uint32
AF = mybir.ActivationFunctionType
ALU = mybir.AluOpType
C_DEBUG = None
AXX = mybir.AxisListType.X

L = 4096
NST = 64
NCORE = 8
EPS = 1e-5
KCH = 512
NKC = L // KCH


def host_constants():
    lf = np.arange(L, dtype=np.float32) / np.float32(L)
    Omega = np.exp(np.complex64(-2j * np.pi) * lf.astype(np.complex64)).astype(np.complex64)
    h = ((np.complex64(1.0) - Omega) / (np.complex64(1.0) + Omega)).astype(np.complex64)
    c = (np.complex64(2.0) / (np.complex64(1.0) + Omega)).astype(np.complex64)

    def tile32(v):
        return np.ascontiguousarray(v.reshape(32, 128).T)

    consts = {
        "hre_b": np.ascontiguousarray(np.broadcast_to(h.real.astype(np.float32), (NST, L))),
        "him_b": np.ascontiguousarray(np.broadcast_to(h.imag.astype(np.float32), (NST, L))),
        "c_re": tile32(c.real.astype(np.float32)),
        "c_im": tile32(c.imag.astype(np.float32)),
    }
    lt_ = np.arange(32)
    W64 = np.exp(2j * np.pi * np.arange(64) / 64.0)
    ev = W64[np.newaxis, :] ** (2 * lt_)[:, None]
    od = W64[np.newaxis, :] ** (2 * lt_ + 1)[:, None]
    consts["W64ev_re"] = ev.real.astype(np.float32)
    consts["W64ev_im"] = ev.imag.astype(np.float32)
    consts["W64ev_imn"] = (-ev.imag).astype(np.float32)
    consts["W64od_re"] = od.real.astype(np.float32)
    consts["W64od_im"] = od.imag.astype(np.float32)
    consts["W64od_imn"] = (-od.imag).astype(np.float32)
    l1_ = np.arange(64)
    Wt = np.exp(2j * np.pi * np.outer(l1_, np.arange(64)) / 4096.0)
    consts["Wt_re"] = Wt.real.astype(np.float32)
    consts["Wt_im"] = Wt.imag.astype(np.float32)
    W2 = np.exp(2j * np.pi * np.outer(l1_, np.arange(64)) / 64.0) / L
    consts["W2_re"] = W2.real.astype(np.float32)
    consts["W2_imn"] = (-W2.imag).astype(np.float32)
    consts["ones64"] = np.ones((1, 64), np.float32)
    consts["ident"] = np.eye(128, dtype=np.float32)
    consts["identb"] = np.eye(128, dtype=np.float32).astype(ml_dtypes.bfloat16)
    return consts


CONST_SPECS = {
    "hre_b": ([NST, L], F32), "him_b": ([NST, L], F32),
    "c_re": ([128, 32], F32), "c_im": ([128, 32], F32),
    "W64ev_re": ([32, 64], F32), "W64ev_im": ([32, 64], F32),
    "W64ev_imn": ([32, 64], F32),
    "W64od_re": ([32, 64], F32), "W64od_im": ([32, 64], F32),
    "W64od_imn": ([32, 64], F32),
    "Wt_re": ([64, 64], F32), "Wt_im": ([64, 64], F32),
    "W2_re": ([64, 64], F32), "W2_imn": ([64, 64], F32),
    "ones64": ([1, 64], F32), "ident": ([128, 128], F32),
    "identb": ([128, 128], BF16),
}


class Sem:
    __slots__ = ("h", "n")

    def __init__(self, nc, name):
        self.h = nc.alloc_semaphore(name)
        self.n = 0

    def inc(self, inst, k=1):
        inst.then_inc(self.h, k)
        self.n += k
        return self.n


class Prog:
    def __init__(self, T):
        self.blk = None
        self.T = T
        self.NT = T // 128
        self.NH = 2 if self.NT >= 2 else 1
        self.TPH = T // self.NH
        self.nc = bass.Bass("TRN2", target_bir_lowering=False)
        self.s = {}
        self.t = {}

    def sem(self, name):
        if name not in self.s:
            self.s[name] = Sem(self.nc, name)
        return self.s[name]

    from contextlib import contextmanager

    @contextmanager
    def block(self, name):
        yield self.blk


def build_program(T=1024, debug=(), phases=("f1", "ssm", "f2"), hoist_f2a=True):
    p = Prog(T)
    nc = p.nc
    NT, NH, TPH = p.NT, p.NH, p.TPH
    NTH = NT // NH

    x_in = nc.dram_tensor("x", [T, L], F32, kind="ExternalInput")
    y_out = nc.dram_tensor("y", [T, L], F32, kind="ExternalOutput")
    lam_in = nc.dram_tensor("L0_attn_Lambda_ri", [NST, 2], F32, kind="ExternalInput")
    p_in = nc.dram_tensor("L0_attn_p_ri", [NST, 2], F32, kind="ExternalInput")
    q_in = nc.dram_tensor("L0_attn_q_ri", [NST, 2], F32, kind="ExternalInput")
    vc_in = nc.dram_tensor("L0_attn_Vc_ri", [NST, 2 * NST], F32, kind="ExternalInput")
    ct_in = nc.dram_tensor("L0_attn_Ct_ri", [NST, 2], F32, kind="ExternalInput")
    bp_in = nc.dram_tensor("L0_attn_Bp", [NST, 1], F32, kind="ExternalInput")
    d_in = nc.dram_tensor("L0_attn_D", [1, 1], F32, kind="ExternalInput")
    ls_in = nc.dram_tensor("L0_attn_log_step", [1, 1], F32, kind="ExternalInput")
    up_in = nc.dram_tensor("L0_ffn_up", [16, 256, 1024], F32, kind="ExternalInput")
    dn_in = nc.dram_tensor("L0_ffn_down", [16, 1024, 256], F32, kind="ExternalInput")
    pm_in = nc.dram_tensor("permT", [128, 32], U32, kind="ExternalInput")
    cin = {nm: nc.dram_tensor("c_" + nm, shp, dt, kind="ExternalInput")
           for nm, (shp, dt) in CONST_SPECS.items()}

    kext = nc.dram_tensor("kext", [2 * L], F32, kind="Internal")
    s1 = nc.dram_tensor("s1", [L * T], BF16, kind="Internal")
    s2 = nc.dram_tensor("s2", [L, T], BF16, kind="Internal")
    dbg = {}
    if "kext" in debug:
        dbg["kext"] = nc.dram_tensor("dbg_kext", [2 * L], F32, kind="ExternalOutput")
    if "tcat" in debug:
        dbg["tcat"] = nc.dram_tensor("dbg_tcat", [128, 4480], BF16, kind="ExternalOutput")
    for st in ("x1", "x2", "ln1", "s2d"):
        if st in debug:
            dbg[st] = nc.dram_tensor("dbg_" + st, [T, L],
                                     BF16 if st in ("ln1", "s2d") else F32,
                                     kind="ExternalOutput")

    XBYTES = NT * L * 4
    WRK_E = 24576
    RES_BYTES = XBYTES + WRK_E * 2 + 8192 + 5 * (KCH * 4)
    SB_BASE = (nc._sbuf_addr_for_side("left") + 63) & ~63
    nc.alloc_sbuf_tensor("reserve", [128, SB_BASE + RES_BYTES - 16384],
                         mybir.dt.uint8)

    def at(name, shape, dtype, off):
        h = nc.alloc_sbuf_tensor_at(name, shape, dtype, offset=SB_BASE + off)
        p.t[name] = h
        return h

    off = 0
    xres = at("xres", [128, NT * L], F32, 0); off += XBYTES
    wrk = at("wrk", [128, WRK_E], BF16, off); off += WRK_E * 2
    delt = at("delt", [128, 4096], BF16, off); off += 8192
    kch = {}
    for nm in ["dre", "dim", "mg", "rcre", "rcim"]:
        kch[nm] = at("k_" + nm, [NST, KCH], F32, off); off += KCH * 4

    def mk(name, shape, dtype):
        h = nc.alloc_sbuf_tensor(name, shape, dtype)
        p.t[name] = h
        return h

    sums = mk("sums", [128, NT], F32)
    ssq = mk("ssq", [128, NT], F32)
    mcol = mk("mcol", [128, NT], F32)
    rstd = mk("rstd", [128, NT], F32)
    nmr = mk("nmr", [128, NT], F32)
    idxn = mk("idxn", [128, 32], U32)
    idxt = mk("idxt", [128, 32], U32)
    idxf = mk("idxf", [128, 32], U32)
    idxf2 = mk("idxf2", [128, 32], U32)
    kbuf = mk("kbuf", [128, 256], F32)
    c1sb = mk("c1sb", [128, 256], F32)
    scr32 = mk("scr32", [128, 192], F32)
    cre = mk("cre", [128, 32], F32)
    cim = mk("cim", [128, 32], F32)
    ident = mk("ident", [128, 128], F32)
    identb = mk("identb", [128, 128], BF16)

    lam = mk("lam", [NST, 2], F32)
    pri = mk("pri", [NST, 2], F32)
    qri = mk("qri", [NST, 2], F32)
    ctri = mk("ctri", [NST, 2], F32)
    bp = mk("bp", [NST, 1], F32)
    vmat = mk("vmat", [NST, 8], F32)
    vcs = mk("vcs", [NST, 2 * NST], F32)
    vctr = mk("vctr", [NST, 2 * NST], F32)
    bvec = mk("bvec", [NST, 2], F32)
    w1 = mk("w1", [32, 6 * 64], F32)
    w2 = mk("w2", [NST, 8 * 64], F32)
    arT = mk("arT", [32, 2 * 128], F32)
    absb = mk("absb", [NST, 4 * 64], F32)
    ksb = mk("ksb", [NST, NST], F32)
    zsb = mk("zsb", [NST, NST], F32)
    sca = mk("sca", [1, 8], F32)
    s2c = mk("s2c", [NST, 1], F32)
    onesb = mk("onesb", [1, NST], F32)
    ddsb = mk("ddsb", [1, 1], F32)
    epsc = mk("epsc", [128, 1], F32)
    tct = mk("tct", [128, 768], BF16)

    FMT_O = 0
    B0_O = 32 * TPH
    GB_O = 0
    WU_O = 4 * T
    WD_O = 4 * T + 4096
    HID_O = 4 * T + 8192
    DO_O = 7 * T + 8192
    TC_O = B0_O

    psU = nc.alloc_psum_tensor("psU", [128, 2048], F32)
    psD = nc.alloc_psum_tensor("psD", [128, 2048], F32)

    def band_ap(b):
        return (psU if b < 4 else psD)[:, (b % 4) * 512:(b % 4 + 1) * 512]

    s_dx = p.sem("dx")
    s_dx2 = p.sem("dx2")
    s_dc = p.sem("dc")
    s_dg = p.sem("dg")
    s_dtp = p.sem("dtp")
    s_dwr = p.sem("dwr")
    s_pe = p.sem("pe")
    s_dve = p.sem("dve")
    s_act = p.sem("act")
    s_dst = p.sem("dst")
    s_bnd = p.sem("bnd")

    env = dict(locals())
    env['HOIST_F2A'] = hoist_f2a

    p.blk = nc.Block("all", no_gpsimd_drain=True)
    p.blk.__enter__()

    with p.block("ph0") as blk:
        @blk.sync
        def _(sy):
            s_dc.inc(sy.dma_start(out=cre[:], in_=cin["c_re"][:]), 16)
            s_dc.inc(sy.dma_start(out=cim[:], in_=cin["c_im"][:]), 16)
            s_dc.inc(sy.dma_start(out=ident[:], in_=cin["ident"][:]), 16)
            s_dc.inc(sy.dma_start(out=identb[:], in_=cin["identb"][:]), 16)
            s_dc.inc(sy.dma_start(out=onesb[:], in_=cin["ones64"][:]), 16)
            for i, nm in enumerate(["W64ev_re", "W64ev_im", "W64ev_imn",
                                    "W64od_re", "W64od_im", "W64od_imn"]):
                s_dc.inc(sy.dma_start(out=w1[:, i * 64:(i + 1) * 64], in_=cin[nm][:]), 16)
            for i, nm in enumerate(["Wt_re", "Wt_im", "W2_re", "W2_imn"]):
                s_dc.inc(sy.dma_start(out=w2[:, i * 64:(i + 1) * 64], in_=cin[nm][:]), 16)
            env["x_load"] = {}
            xsems = [s_dx, s_dx2, p.sem("dx3"), p.sem("dx4")]
            for t_ in range(NT):
                sm = xsems[(t_ // 2) % 4]
                sm.inc(sy.dma_start(out=xres[:, t_ * L:(t_ + 1) * L],
                                    in_=x_in[t_ * 128:(t_ + 1) * 128, :]), 16)
            for t_ in range(NT):
                sm = xsems[(t_ // 2) % 4]
                env["x_load"][t_] = (sm, sm.n)

        @blk.scalar
        def _(sc):
            for dst_, src_ in [(lam, lam_in), (pri, p_in), (qri, q_in),
                               (ctri, ct_in), (bp, bp_in), (ddsb, d_in),
                               (vcs, vc_in)]:
                s_dc.inc(sc.dma_start(out=dst_[:], in_=src_[:]), 16)
            s_dc.inc(sc.dma_start(out=sca[:, 0:1], in_=ls_in[:]), 16)

        @blk.gpsimd
        def _(g):
            s_dg.inc(g.dma_start(out=idxn[:], in_=pm_in[:]), 16)

        @blk.vector
        def _(v):
            p.s["dve"].inc(v.memset(epsc[:], EPS))

    build_idx(p, env)
    build_kcomp_head(p, env)
    env["kc_pending"] = list(range(NKC))
    if "f1" in phases:
        if "ssm" in phases and NH == 2 and NT // NH == 4:
            def _pc1():
                ln_pair(p, env, "ssm0", 0, [0, 1], True, xbars=False)
                env.setdefault("computed_pairs", set()).add(("ssm0", 0))
            env["postC_cb_next"] = {1: _pc1}
        build_ffn(p, env, "f1")
    else:
        kc_flush(p, env)
    dump_dbg(p, env, "x1")
    if "ssm" in phases:
        cbs1 = {}
        if "f2" in phases and NH == 2 and NT // NH == 4:
            if env.get('HOIST_F2A', True):
                env.setdefault("stripe_map", {}).update(
                    {("f2A", t): t % 2 for t in range(NT)})

            def _pcb(h, ts):
                def f():
                    ln_pair(p, env, "f2A", h, ts, False)
                    s1_write_pair(p, env, "f2A", ts)
                    env.setdefault("done_pairs", set()).add(("f2A", ts[0]))
                return f

            def _cb3():
                ln_pair(p, env, "f2A", 1, [6, 7], False, xbars=False)
                env.setdefault("done_pairs", set()).add(("f2A", 6))
                env["f2A_defer_xb"] = True

            cbs1 = ({0: _pcb(0, [0, 1]), 1: _pcb(0, [2, 3]),
                     2: _pcb(1, [4, 5]), 3: _cb3}
                    if env.get('HOIST_F2A', True) else {})
        elif "f2" in phases and NH == 2 and NT // NH == 2:
            def _cb0():
                ln_pair(p, env, "f2A", 0, [0, 1], False)
                env.setdefault("done_pairs", set()).add(("f2A", 0))
            cbs1 = {0: _cb0}
        build_ssm(p, env, post_cbs=cbs1)
    dump_dbg(p, env, "x2")
    if "f2" in phases:
        build_ffn(p, env, "f2")
    build_store(p, env)
    finish_program(p)
    out_names = ["y"] + ["dbg_" + k for k in dbg]
    return p, env, out_names


def finish_program(p):
    nc = p.nc
    with p.block("fin") as blk:
        @blk.sync
        def _(sy):
            for nm in ["dx", "dx2", "dx3", "dx4", "dc", "dtp", "dwr", "dst",
                       "ctp0", "ctp1", "cr0", "cr1"]:
                if p.s[nm].n if nm in p.s else 0:
                    sy.wait_ge(p.s[nm].h, p.s[nm].n)

        @blk.gpsimd
        def _(g):
            for nm in ["dg", "gt0", "gt1", "sc"]:
                if nm in p.s and p.s[nm].n:
                    g.wait_ge(p.s[nm].h, p.s[nm].n)
    p.blk.__exit__(None, None, None)


def kc_step(p, env, n=2):
    for _ in range(n):
        if env.get("kc_pending"):
            ch = env["kc_pending"].pop(0)
            kc_chunk(p, env, ch)


def kc_flush(p, env):
    while env.get("kc_pending"):
        kc_chunk(p, env, env["kc_pending"].pop(0))
    if not env.get("kc_tail_done"):
        build_kcomp_tail(p, env)
        env["kc_tail_done"] = True
        env["kc_done_dve"] = p.s["dve"].n
        env["kc_done_pe"] = p.s["pe"].n


def build_kcomp_head(p, env):
    nc = p.nc
    s_dc, s_dve, s_act, s_pe, s_dwr, s_dg = (p.s[n] for n in
                                             ["dc", "dve", "act", "pe", "dwr", "dg"])
    g_ = env
    kch = g_["kch"]
    (lam, pri, qri, ctri, bp, vmat, vcs, vctr, bvec, w1, w2, arT, absb, ksb,
     zsb, sca, s2c, onesb, ddsb, ident, cre, cim, kbuf, c1sb, scr32,
     psU, psD, kext, dbg) = (g_[n] for n in [
         "lam", "pri", "qri", "ctri", "bp", "vmat", "vcs", "vctr", "bvec",
         "w1", "w2", "arT", "absb", "ksb", "zsb", "sca", "s2c", "onesb",
         "ddsb", "ident", "cre", "cim", "kbuf", "c1sb", "scr32",
         "psU", "psD", "kext", "dbg"])
    NCONST = s_dc.n

    with p.block("kc_sc") as blk:
        @blk.scalar
        def _(sc):
            sc.wait_ge(s_dc.h, NCONST)
            s_act.inc(sc.activation(out=sca[:, 1:2], in_=sca[:, 0:1], func=AF.Exp))

        @blk.vector
        def _(v):
            v.wait_ge(s_act.h, s_act.n)
            s_dve.inc(v.reciprocal(out=sca[:, 2:3], in_=sca[:, 1:2]))
            v.wait_ge(s_dve.h, s_dve.n)
            s_dve.inc(v.tensor_scalar(out=sca[:, 3:4], in0=sca[:, 2:3],
                                      scalar1=2.0, scalar2=None, op0=ALU.mult))

        @blk.tensor
        def _(te):
            te.wait_ge(s_dve.h, s_dve.n)
            s_pe.inc(te.matmul(psD[:NST, 0:1], onesb[:, :NST], sca[:, 3:4],
                               start=True, stop=True))

        @blk.vector
        def _(v):
            v.wait_ge(s_pe.h, s_pe.n)
            s_dve.inc(v.tensor_copy(out=s2c[:], in_=psD[:NST, 0:1]))

    with p.block("kc_vv") as blk:
        @blk.vector
        def _(v):
            v.wait_ge(s_dc.h, NCONST)
            vcr = vcs[:].rearrange("p (c two) -> p two c", two=2)
            s_dve.inc(v.tensor_copy(out=vctr[:, 0:NST], in_=vcr[:, 0, :]))
            s_dve.inc(v.tensor_copy(out=vctr[:, NST:2 * NST], in_=vcr[:, 1, :]))

        @blk.tensor
        def _(te):
            te.wait_ge(s_dve.h, s_dve.n)
            s_pe.inc(te.transpose(psU[:NST, 0:NST], vctr[:NST, 0:NST], ident[:NST, :NST]))
            s_pe.inc(te.transpose(psU[:NST, 512:512 + NST], vctr[:NST, NST:2 * NST],
                                  ident[:NST, :NST]))

        @blk.vector
        def _(v):
            v.wait_ge(s_pe.h, s_pe.n)
            s_dve.inc(v.tensor_copy(out=vcs[:, 0:NST], in_=psU[:NST, 0:NST]))
            s_dve.inc(v.tensor_copy(out=vcs[:, NST:2 * NST], in_=psU[:NST, 512:512 + NST]))

        @blk.tensor
        def _(te):
            te.wait_ge(s_dve.h, s_dve.n)
            s_pe.inc(te.matmul(psD[:NST, 512:513], vcs[:NST, 0:NST], bp[:NST, :],
                               start=True, stop=True))
            s_pe.inc(te.matmul(psD[:NST, 1024:1025], vcs[:NST, NST:2 * NST], bp[:NST, :],
                               start=True, stop=True))

        @blk.vector
        def _(v):
            v.wait_ge(s_pe.h, s_pe.n)
            s_dve.inc(v.tensor_copy(out=bvec[:, 0:1], in_=psD[:NST, 512:513]))
            s_dve.inc(v.tensor_copy(out=bvec[:, 1:2], in_=psD[:NST, 1024:1025]))
            v.wait_ge(s_dve.h, s_dve.n)

            def cmul_conj(col, a_re, a_im, c_re_, c_im_):
                s_dve.inc(v.tensor_mul(out=scr32[:NST, 0:1], in0=a_re, in1=c_re_))
                s_dve.inc(v.tensor_mul(out=scr32[:NST, 1:2], in0=a_im, in1=c_im_))
                s_dve.inc(v.tensor_mul(out=scr32[:NST, 2:3], in0=a_re, in1=c_im_))
                s_dve.inc(v.tensor_mul(out=scr32[:NST, 3:4], in0=a_im, in1=c_re_))
                v.wait_ge(s_dve.h, s_dve.n)
                s_dve.inc(v.tensor_add(out=vmat[:, col:col + 1],
                                       in0=scr32[:NST, 0:1], in1=scr32[:NST, 1:2]))
                s_dve.inc(v.tensor_sub(out=vmat[:, col + 1:col + 2],
                                       in0=scr32[:NST, 2:3], in1=scr32[:NST, 3:4]))
                v.wait_ge(s_dve.h, s_dve.n)

            cmul_conj(0, ctri[:, 0:1], ctri[:, 1:2], bvec[:, 0:1], bvec[:, 1:2])
            cmul_conj(2, ctri[:, 0:1], ctri[:, 1:2], pri[:, 0:1], pri[:, 1:2])
            cmul_conj(4, qri[:, 0:1], qri[:, 1:2], bvec[:, 0:1], bvec[:, 1:2])
            cmul_conj(6, qri[:, 0:1], qri[:, 1:2], pri[:, 0:1], pri[:, 1:2])


def kc_chunk(p, env, ch):
    nc = p.nc
    g_ = env
    kch = env["kch"]
    s_dc, s_dve, s_act, s_pe, s_dwr, s_dg = (p.s[n] for n in
                                             ["dc", "dve", "act", "pe", "dwr", "dg"])
    vmat, psU = env["vmat"], env["psU"]
    lam, s2c = env["lam"], env["s2c"]
    NLT = KCH // 128
    if True:
        with p.block(f"kc_ch{ch}") as blk:
            @blk.gpsimd
            def _(g2):
                g2.wait_ge(s_dve.h, s_dve.n)
                g2.wait_ge(s_pe.h, s_pe.n)
                s_dg.inc(g2.dma_start(out=kch["dre"][:],
                                      in_=g_["cin"]["hre_b"][:, ch * KCH:(ch + 1) * KCH]), 16)
                s_dg.inc(g2.dma_start(out=kch["dim"][:],
                                      in_=g_["cin"]["him_b"][:, ch * KCH:(ch + 1) * KCH]), 16)

            @blk.vector
            def _(v):
                v.wait_ge(s_dg.h, s_dg.n)
                s_dve.inc(v.tensor_scalar(out=kch["dre"][:], in0=kch["dre"][:],
                                          scalar1=s2c[:], scalar2=lam[:, 0:1],
                                          op0=ALU.mult, op1=ALU.subtract))
                s_dve.inc(v.tensor_scalar(out=kch["dim"][:], in0=kch["dim"][:],
                                          scalar1=s2c[:], scalar2=lam[:, 1:2],
                                          op0=ALU.mult, op1=ALU.subtract))
                v.wait_ge(s_dve.h, s_dve.n)
                s_dve.inc(v.tensor_mul(out=kch["mg"][:], in0=kch["dre"][:], in1=kch["dre"][:]))
                s_dve.inc(v.tensor_mul(out=kch["rcim"][:], in0=kch["dim"][:], in1=kch["dim"][:]))
                v.wait_ge(s_dve.h, s_dve.n)
                s_dve.inc(v.tensor_add(out=kch["mg"][:], in0=kch["mg"][:], in1=kch["rcim"][:]))
                v.wait_ge(s_dve.h, s_dve.n)
                s_dve.inc(v.reciprocal(out=kch["mg"][:], in_=kch["mg"][:]))
                v.wait_ge(s_dve.h, s_dve.n)
                s_dve.inc(v.tensor_mul(out=kch["rcre"][:], in0=kch["dre"][:], in1=kch["mg"][:]))
                s_dve.inc(v.scalar_tensor_tensor(out=kch["rcim"][:], in0=kch["dim"][:],
                                                 scalar=-1.0, in1=kch["mg"][:],
                                                 op0=ALU.mult, op1=ALU.mult))

            @blk.tensor
            def _(te):
                te.wait_ge(s_dve.h, s_dve.n)
                for i in range(NLT):
                    lt = ch * NLT + i
                    s_pe.inc(te.matmul(psU[:, lt * 8:lt * 8 + 8],
                                       kch["rcre"][:, i * 128:(i + 1) * 128],
                                       vmat[:NST, :], start=True, stop=True))
                    s_pe.inc(te.matmul(psU[:, 512 + lt * 8:512 + lt * 8 + 8],
                                       kch["rcim"][:, i * 128:(i + 1) * 128],
                                       vmat[:NST, :], start=True, stop=True))


def build_kcomp_tail(p, env):
    nc = p.nc
    g_ = env
    (lam, pri, qri, ctri, bp, vmat, vcs, vctr, bvec, w1, w2, arT, absb, ksb,
     zsb, sca, s2c, onesb, ddsb, ident, cre, cim, kbuf, c1sb, scr32,
     psU, psD, kext, dbg) = (g_[n] for n in [
         "lam", "pri", "qri", "ctri", "bp", "vmat", "vcs", "vctr", "bvec",
         "w1", "w2", "arT", "absb", "ksb", "zsb", "sca", "s2c", "onesb",
         "ddsb", "ident", "cre", "cim", "kbuf", "c1sb", "scr32",
         "psU", "psD", "kext", "dbg"])
    s_dc, s_dve, s_act, s_pe, s_dwr, s_dg = (p.s[n] for n in
                                             ["dc", "dve", "act", "pe", "dwr", "dg"])
    with p.block("kc_comb") as blk:
        @blk.vector
        def _(v):
            v.wait_ge(s_pe.h, s_pe.n)
            s_dve.inc(v.tensor_copy(out=c1sb[:], in_=psU[:, 0:256]))
            v.wait_ge(s_dve.h, s_dve.n)
            c1r = c1sb[:].rearrange("p (lt j) -> p j lt", j=8)
            c2r = psU[:, 512:768].rearrange("p (lt j) -> p j lt", j=8)
            kr = kbuf[:].rearrange("p (g lt) -> p g lt", g=8)
            for j in range(4):
                s_dve.inc(v.tensor_sub(out=kr[:, 2 * j, :], in0=c1r[:, 2 * j, :],
                                       in1=c2r[:, 2 * j + 1, :]))
                s_dve.inc(v.tensor_add(out=kr[:, 2 * j + 1, :], in0=c1r[:, 2 * j + 1, :],
                                       in1=c2r[:, 2 * j, :]))
            v.wait_ge(s_dve.h, s_dve.n)

    with p.block("kc_ar") as blk:
        @blk.vector
        def _(v):
            kr = kbuf[:].rearrange("p (g lt) -> p g lt", g=8)
            k00r, k00i = kr[:, 0, :], kr[:, 1, :]
            k01r, k01i = kr[:, 2, :], kr[:, 3, :]
            k10r, k10i = kr[:, 4, :], kr[:, 5, :]
            k11r, k11i = kr[:, 6, :], kr[:, 7, :]
            S = scr32[:].rearrange("p (g c) -> p g c", c=32)
            tre, tim, dre, dmag, u1 = (S[:, i, :] for i in range(5))
            s_dve.inc(v.tensor_mul(out=tre, in0=k01r, in1=k10r))
            s_dve.inc(v.tensor_mul(out=u1, in0=k01i, in1=k10i))
            s_dve.inc(v.tensor_mul(out=tim, in0=k01r, in1=k10i))
            s_dve.inc(v.tensor_mul(out=dre, in0=k01i, in1=k10r))
            v.wait_ge(s_dve.h, s_dve.n)
            s_dve.inc(v.tensor_sub(out=tre, in0=tre, in1=u1))
            s_dve.inc(v.tensor_add(out=tim, in0=tim, in1=dre))
            v.wait_ge(s_dve.h, s_dve.n)
            s_dve.inc(v.tensor_scalar(out=dre, in0=k11r, scalar1=1.0, scalar2=None,
                                      op0=ALU.add))
            v.wait_ge(s_dve.h, s_dve.n)
            s_dve.inc(v.tensor_mul(out=dmag, in0=dre, in1=dre))
            s_dve.inc(v.tensor_mul(out=u1, in0=k11i, in1=k11i))
            v.wait_ge(s_dve.h, s_dve.n)
            s_dve.inc(v.tensor_add(out=dmag, in0=dmag, in1=u1))
            v.wait_ge(s_dve.h, s_dve.n)
            s_dve.inc(v.reciprocal(out=dmag, in_=dmag))
            v.wait_ge(s_dve.h, s_dve.n)
            s_dve.inc(v.tensor_mul(out=u1, in0=tre, in1=dre))
            s_dve.inc(v.tensor_mul(out=k10r, in0=tim, in1=k11i))
            s_dve.inc(v.tensor_mul(out=k10i, in0=tim, in1=dre))
            s_dve.inc(v.tensor_mul(out=k01r, in0=tre, in1=k11i))
            v.wait_ge(s_dve.h, s_dve.n)
            s_dve.inc(v.tensor_add(out=u1, in0=u1, in1=k10r))
            s_dve.inc(v.tensor_sub(out=k10i, in0=k10i, in1=k01r))
            v.wait_ge(s_dve.h, s_dve.n)
            s_dve.inc(v.tensor_mul(out=u1, in0=u1, in1=dmag))
            s_dve.inc(v.tensor_mul(out=k10i, in0=k10i, in1=dmag))
            v.wait_ge(s_dve.h, s_dve.n)
            s_dve.inc(v.tensor_sub(out=k00r, in0=k00r, in1=u1))
            s_dve.inc(v.tensor_sub(out=k00i, in0=k00i, in1=k10i))
            v.wait_ge(s_dve.h, s_dve.n)
            s_dve.inc(v.tensor_mul(out=k01r, in0=cre[:], in1=k00r))
            s_dve.inc(v.tensor_mul(out=u1, in0=cim[:], in1=k00i))
            s_dve.inc(v.tensor_mul(out=k01i, in0=cre[:], in1=k00i))
            s_dve.inc(v.tensor_mul(out=k10r, in0=cim[:], in1=k00r))
            v.wait_ge(s_dve.h, s_dve.n)
            s_dve.inc(v.tensor_sub(out=k01r, in0=k01r, in1=u1))
            s_dve.inc(v.tensor_add(out=k01i, in0=k01i, in1=k10r))

    with p.block("kc_idft") as blk:
        kr = kbuf[:].rearrange("p (g lt) -> p g lt", g=8)
        ar_re, ar_im = kr[:, 2, :], kr[:, 3, :]

        @blk.tensor
        def _(te):
            te.wait_ge(s_dve.h, s_dve.n)
            s_pe.inc(te.transpose(psD[:32, 0:128], ar_re, ident[:, :]))
            s_pe.inc(te.transpose(psD[:32, 512:512 + 128], ar_im, ident[:, :]))

        @blk.vector
        def _(v):
            v.wait_ge(s_pe.h, s_pe.n)
            s_dve.inc(v.tensor_copy(out=arT[:, 0:128], in_=psD[:32, 0:128]))
            s_dve.inc(v.tensor_copy(out=arT[:, 128:256], in_=psD[:32, 512:512 + 128]))

        @blk.tensor
        def _(te):
            te.wait_ge(s_dve.h, s_dve.n)
            arL, arH = arT[:32, 0:64], arT[:32, 64:128]
            aiL, aiH = arT[:32, 128:192], arT[:32, 192:256]
            wev_re, wev_im, wev_imn = w1[:32, 0:64], w1[:32, 64:128], w1[:32, 128:192]
            wod_re, wod_im, wod_imn = w1[:32, 192:256], w1[:32, 256:320], w1[:32, 320:384]
            pa_re, pa_im = psU[:NST, 1024:1088], psU[:NST, 1536:1600]
            s_pe.inc(te.matmul(pa_re, arL, wev_re, start=True, stop=False))
            te.matmul(pa_re, arH, wod_re, start=False, stop=False)
            te.matmul(pa_re, aiL, wev_imn, start=False, stop=False)
            s_pe.inc(te.matmul(pa_re, aiH, wod_imn, start=False, stop=True))
            s_pe.inc(te.matmul(pa_im, arL, wev_im, start=True, stop=False))
            te.matmul(pa_im, arH, wod_im, start=False, stop=False)
            te.matmul(pa_im, aiL, wev_re, start=False, stop=False)
            s_pe.inc(te.matmul(pa_im, aiH, wod_re, start=False, stop=True))

        @blk.vector
        def _(v):
            v.wait_ge(s_pe.h, s_pe.n)
            s_dve.inc(v.tensor_copy(out=absb[:, 0:64], in_=psU[:NST, 1024:1088]))
            s_dve.inc(v.tensor_copy(out=absb[:, 64:128], in_=psU[:NST, 1536:1600]))
            v.wait_ge(s_dve.h, s_dve.n)
            a_re, a_im = absb[:, 0:64], absb[:, 64:128]
            b_re, b_im = absb[:, 128:192], absb[:, 192:256]
            wt_re, wt_im = w2[:, 0:64], w2[:, 64:128]
            s_dve.inc(v.tensor_mul(out=b_re, in0=a_re, in1=wt_re))
            s_dve.inc(v.tensor_mul(out=b_im, in0=a_re, in1=wt_im))
            s_dve.inc(v.tensor_mul(out=scr32[:NST, 0:64], in0=a_im, in1=wt_im))
            s_dve.inc(v.tensor_mul(out=scr32[:NST, 64:128], in0=a_im, in1=wt_re))
            v.wait_ge(s_dve.h, s_dve.n)
            s_dve.inc(v.tensor_sub(out=b_re, in0=b_re, in1=scr32[:NST, 0:64]))
            s_dve.inc(v.tensor_add(out=b_im, in0=b_im, in1=scr32[:NST, 64:128]))

        @blk.tensor
        def _(te):
            te.wait_ge(s_dve.h, s_dve.n)
            pk = psD[:NST, 1024:1088]
            s_pe.inc(te.matmul(pk, w2[:, 128:192], absb[:, 128:192], start=True, stop=False))
            s_pe.inc(te.matmul(pk, w2[:, 192:256], absb[:, 192:256], start=False, stop=True))

        @blk.vector
        def _(v):
            v.wait_ge(s_pe.h, s_pe.n)
            s_dve.inc(v.tensor_add(out=psD[:1, 1024:1025], in0=psD[:1, 1024:1025],
                                   in1=ddsb[:, :]))
            v.wait_ge(s_dve.h, s_dve.n)
            s_dve.inc(v.tensor_copy(out=ksb[:], in_=psD[:NST, 1024:1088]))
            s_dve.inc(v.memset(zsb[:], 0.0))

        @blk.sync
        def _(sy):
            sy.wait_ge(s_dve.h, s_dve.n)
            s_dwr.inc(sy.dma_start(out=AP(kext, 0, [[64, 64], [1, 64]]), in_=zsb[:]), 16)
            s_dwr.inc(sy.dma_start(out=AP(kext, L, [[64, 64], [1, 64]]), in_=ksb[:]), 16)
            env["kext_done"] = s_dwr.n
            if "kext" in dbg:
                sy.wait_ge(s_dwr.h, s_dwr.n)
                s_dwr.inc(sy.dma_start(out=AP(dbg["kext"], 0, [[1, 1], [1, 2 * L]]),
                                       in_=AP(kext, 0, [[1, 1], [1, 2 * L]])), 16)
            if "tcat" in dbg:
                pass
            sy.wait_ge(s_dwr.h, s_dwr.n)

    if "tcat" in dbg:
        wrk = env_wrk(p)
        with p.block("kc_tcdbg") as blk:
            @blk.gpsimd
            def _(g):
                g.wait_ge(s_dwr.h, s_dwr.n)
                s_dg.inc(g.dma_start(out=wrk[:, 16384:16384 + 4480],
                                     in_=AP(kext, 3585, [[1, 128], [1, 4480]])), 16)
                g.wait_ge(s_dg.h, s_dg.n)
                s_dg.inc(g.dma_start(out=dbg["tcat"][:], in_=wrk[:, 16384:16384 + 4480]), 16)
                g.wait_ge(s_dg.h, s_dg.n)


def dump_dbg(p, env, st):
    if st not in env["dbg"]:
        return
    nc = p.nc
    with p.block("dbg_" + st) as blk:
        @blk.sync
        def _(sy):
            sy.wait_ge(p.s["dve"].h, p.s["dve"].n)
            sy.wait_ge(p.s["bnd"].h, p.s["bnd"].n)
            for t_ in range(p.NT):
                p.s["dst"].inc(sy.dma_start(
                    out=env["dbg"][st][t_ * 128:(t_ + 1) * 128, :],
                    in_=env["xres"][:, t_ * L:(t_ + 1) * L]), 16)
            sy.wait_ge(p.s["dst"].h, p.s["dst"].n)


def env_wrk(p):
    return p.t["wrk"]


def env_hsrc(p, which, ch):
    return p.t["hri"]


def build_idx(p, env):
    nc = p.nc
    s_dg, s_dve = p.s["dg"], p.s["dve"]
    idxn, idxt, idxf, idxf2 = (env[n] for n in ["idxn", "idxt", "idxf", "idxf2"])
    with p.block("idx") as blk:
        @blk.vector
        def _(v):
            v.wait_ge(s_dg.h, 16)
            s_dve.inc(v.tensor_scalar(out=idxf[:], in0=idxn[:], scalar1=127,
                                      scalar2=5, op0=ALU.bitwise_and,
                                      op1=ALU.logical_shift_left))
            s_dve.inc(v.tensor_scalar(out=idxf2[:], in0=idxn[:], scalar1=7,
                                      scalar2=None, op0=ALU.logical_shift_right))
            v.wait_ge(s_dve.h, s_dve.n)
            s_dve.inc(v.tensor_tensor(out=idxt[:], in0=idxf[:], in1=idxf2[:],
                                      op=ALU.add))


_CACHE = {}


def kernel(L0_attn_u, L0_attn_Lambda_ri, L0_attn_p_ri, L0_attn_q_ri,
           L0_attn_Vc_ri, L0_attn_Ct_ri, L0_attn_Bp, L0_attn_D,
           L0_attn_log_step, L0_ffn_up, L0_ffn_down, L0_ffn_perm):
    T = 8192 // NCORE
    x = np.ascontiguousarray(np.asarray(L0_attn_u, dtype=np.float32).reshape(8192, L))
    if "prog" not in _CACHE:
        _CACHE["prog"] = build_program(T=T)
        _CACHE["consts"] = host_constants()
    p, env, outs = _CACHE["prog"]
    consts = _CACHE["consts"]

    base = {
        "L0_attn_Lambda_ri": np.asarray(L0_attn_Lambda_ri, np.float32),
        "L0_attn_p_ri": np.asarray(L0_attn_p_ri, np.float32),
        "L0_attn_q_ri": np.asarray(L0_attn_q_ri, np.float32),
        "L0_attn_Vc_ri": np.ascontiguousarray(
            np.asarray(L0_attn_Vc_ri, np.float32).reshape(NST, 2 * NST)),
        "L0_attn_Ct_ri": np.ascontiguousarray(
            np.asarray(L0_attn_Ct_ri, np.float32).reshape(NST, 2)),
        "L0_attn_Bp": np.asarray(L0_attn_Bp, np.float32).reshape(NST, 1),
        "L0_attn_D": np.asarray(L0_attn_D, np.float32).reshape(1, 1),
        "L0_attn_log_step": np.asarray(L0_attn_log_step, np.float32).reshape(1, 1),
        "L0_ffn_up": np.asarray(L0_ffn_up, np.float32),
        "L0_ffn_down": np.asarray(L0_ffn_down, np.float32),
        "permT": np.ascontiguousarray(
            np.asarray(L0_ffn_perm).astype(np.uint32).reshape(32, 128).T),
    }
    for nm in CONST_SPECS:
        base["c_" + nm] = consts[nm]

    in_maps = [dict(base, x=np.ascontiguousarray(x[c * T:(c + 1) * T]))
               for c in range(NCORE)]
    res = run_bass_kernel_spmd(p.nc, in_maps, core_ids=list(range(NCORE)))
    y = np.concatenate([np.asarray(r["y"]) for r in res.results], axis=0)
    return y.reshape(2, 4096, L)


def b0_loc(p, env, tag, t):
    if tag == "f1A":
        sid = t % 3
        if sid == 2:
            return env["delt"], 0, 2
    else:
        sid = t % 2
    return env["wrk"], env["B0_O"] + sid * 4096, sid


def ln_tile(p, env, blk, t, rev, tag):
    nc = p.nc
    s_dve, s_act, s_dx = p.s["dve"], p.s["act"], p.s["dx"]
    xt = env["xres"][:, t * L:(t + 1) * L]
    wrk, B0_O = env["wrk"], env["B0_O"]
    b0h, slot, sid = b0_loc(p, env, tag, t)
    wrkt = b0h
    scr32 = env["scr32"]
    sums = env["sums"]

    if rev:
        nf = b0h.shape[1]
        b0dst = AP(b0h, slot + 127, [[nf, 128], [128, 32], [-1, 128]])
    else:
        b0dst = b0h[:, slot:slot + 4096]

    sdx, sdxv = env["x_load"][t]

    @blk.vector
    def _(v):
        v.wait_ge(sdx.h, sdxv)
        v.wait_ge(p.s["bnd"].h, env.get("x_ready", {}).get(t, 0))
        v.wait_ge(p.s["dtp"].h, env["b0_free"][sid])
        s_dve.inc(v.scalar_tensor_tensor(
            out=wrkt[:, slot:slot + 2048], in0=xt[:, 0:2048], scalar=1.0,
            in1=xt[:, 0:2048], op0=ALU.bypass, op1=ALU.mult,
            accum_out=scr32[:, 160 + t:161 + t]))
        s_dve.inc(v.scalar_tensor_tensor(
            out=wrkt[:, slot + 2048:slot + 4096], in0=xt[:, 2048:4096], scalar=1.0,
            in1=xt[:, 2048:4096], op0=ALU.bypass, op1=ALU.mult,
            accum_out=scr32[:, 176 + t:177 + t]))
        env["sq_done"][t] = s_dve.n

    @blk.scalar
    def _(sc):
        sc.wait_ge(sdx.h, sdxv)
        sc.wait_ge(s_dve.h, env["sq_done"][t])
        s_act.inc(sc.activation(out=b0dst, in_=xt, func=AF.Copy,
                                accum_out=sums[:, t:t + 1]))
        env["act_cast"][t] = s_act.n


def ln_half(p, env, blk, h, ts, tag):
    s_dve, s_act = p.s["dve"], p.s["act"]
    sums, mcol, rstd, nmr, scr32 = (env[n] for n in
                                    ["sums", "mcol", "rstd", "nmr", "scr32"])
    ssq = env["ssq"]
    wrk, B0_O = env["wrk"], env["B0_O"]
    n = len(ts)
    t0 = ts[0]
    sl = slice(t0, t0 + n)

    @blk.vector
    def _(v):
        v.wait_ge(s_act.h, s_act.n)
        v.wait_ge(s_dve.h, s_dve.n)
        s_dve.inc(v.tensor_scalar(out=mcol[:, sl], in0=sums[:, sl],
                                  scalar1=1.0 / L, scalar2=None, op0=ALU.mult))
        s_dve.inc(v.tensor_add(out=ssq[:, sl], in0=scr32[:, 160 + t0:160 + t0 + n],
                               in1=scr32[:, 176 + t0:176 + t0 + n]))
        v.wait_ge(s_dve.h, s_dve.n)
        s_dve.inc(v.tensor_mul(out=scr32[:, 184:184 + n], in0=mcol[:, sl],
                               in1=mcol[:, sl]))
        v.wait_ge(s_dve.h, s_dve.n)
        s_dve.inc(v.scalar_tensor_tensor(out=ssq[:, sl], in0=ssq[:, sl],
                                         scalar=1.0 / L, in1=scr32[:, 184:184 + n],
                                         op0=ALU.mult, op1=ALU.subtract))

    @blk.scalar
    def _(sc):
        sc.wait_ge(s_dve.h, s_dve.n)
        s_act.inc(sc.activation(out=rstd[:, sl], in_=ssq[:, sl], func=AF.Sqrt,
                                bias=env["epsc"][:, 0:1], scale=1.0))

    @blk.vector
    def _(v):
        v.wait_ge(s_act.h, s_act.n)
        s_dve.inc(v.reciprocal(out=rstd[:, sl], in_=rstd[:, sl]))
        v.wait_ge(s_dve.h, s_dve.n)
        s_dve.inc(v.scalar_tensor_tensor(out=nmr[:, sl], in0=mcol[:, sl],
                                         scalar=-1.0, in1=rstd[:, sl],
                                         op0=ALU.mult, op1=ALU.mult))
        v.wait_ge(s_dve.h, s_dve.n)
        for t in ts:
            b0h, slot, _sid = b0_loc(p, env, tag, t)
            s_dve.inc(v.tensor_scalar(out=b0h[:, slot:slot + 4096],
                                      in0=b0h[:, slot:slot + 4096],
                                      scalar1=rstd[:, t:t + 1],
                                      scalar2=nmr[:, t:t + 1],
                                      op0=ALU.mult, op1=ALU.add))
            env["b0_ready"][t] = s_dve.n


def ln_pair(p, env, tag, h, ts, rev, kc_after=False, xbars=True):
    nc = p.nc
    with p.block(f"{tag}_lnp{ts[0]}") as blk:
        for t in ts:
            ln_tile(p, env, blk, t, rev, tag)
        ln_half(p, env, blk, h, ts, tag)
    if xbars:
        ln_xbars(p, env, tag, h, ts)
    if kc_after:
        kc_step(p, env, 2)


def ln_xbars(p, env, tag, h, ts):
    nc = p.nc
    NTH, TPH = p.NT // p.NH, p.TPH
    s_dve, s_dtp, s_dwr, s_pe = p.s["dve"], p.s["dtp"], p.s["dwr"], p.s["pe"]
    wrk, B0_O, FMT_O = env["wrk"], env["B0_O"], env["FMT_O"]
    with p.block(f"{tag}_xb{ts[0]}") as blk:
        @blk.sync
        def _(sy):
            for t in ts:
                sy.wait_ge(s_dve.h, env["b0_ready"][t])
            sy.wait_ge(s_dwr.h, s_dwr.n)
            sy.wait_ge(p.s["dc"].h, p.s["dc"].n)
            sy.wait_ge(p.s["dg"].h, p.s["dg"].n)
            for nm in ["gt0", "gt1", "sc", "ctp0", "ctp1", "cr0", "cr1"]:
                if nm in p.s:
                    sy.wait_ge(p.s[nm].h, p.s[nm].n)
            for t in ts:
                i = env.get("stripe_map", {}).get((tag, t), t - h * NTH)
                b0h, slot, _sid = b0_loc(p, env, tag, t)
                sy.wait_ge(s_pe.h, max(env.get("stripe_free", {}).get(i, 0),
                                       env.get("wrk_pe_free", 0),
                                       env.get("uth_pe_gate", 0)))
                dst = wrk[:, FMT_O:FMT_O + 32 * TPH] \
                    .rearrange("p (ft tk) -> p ft tk", ft=32)[:, :, i * 128:(i + 1) * 128]
                s_dtp.inc(sy.dma_start(out=dst, in_=b0h[:, slot:slot + 4096],
                                       transpose=True), 16)
            sy.wait_ge(s_dtp.h, s_dtp.n)
            for t in ts:
                env["b0_free"][b0_loc(p, env, tag, t)[2]] = s_dtp.n
                env.setdefault("uth_pair_xb", {})[t] = s_dtp.n


def s1_write_half(p, env, tag, h):
    nc = p.nc
    TPH, T = p.TPH, p.T
    s_dwr = p.s["dwr"]
    wrk, FMT_O = env["wrk"], env["FMT_O"]
    s1 = env["s1"]
    with p.block(f"{tag}_s1w{h}") as blk:
        @blk.sync
        def _(sy):
            src = wrk[:, FMT_O:FMT_O + 32 * TPH] \
                .rearrange("p (ft tk) -> p ft tk", ft=32)
            dst = AP(s1, h * TPH, [[32 * T, 128], [T, 32], [1, TPH]])
            s_dwr.inc(sy.dma_start(out=dst, in_=src), 16)
            env["s1_half"][h] = s_dwr.n
    env.setdefault("done_s1w", set()).add((tag, h))


def s1_write_pair(p, env, tag, ts):
    T, TPH = p.T, p.TPH
    s_dwr = p.s["dwr"]
    wrk, FMT_O, s1 = env["wrk"], env["FMT_O"], env["s1"]
    smap = env["stripe_map"]
    with p.block(f"{tag}_s1p{ts[0]}") as blk:
        @blk.sync
        def _(sy):
            sy.wait_ge(p.s["dtp"].h, p.s["dtp"].n)
            for t in ts:
                i = smap[(tag, t)]
                src = wrk[:, FMT_O:FMT_O + 32 * TPH] \
                    .rearrange("p (ft tk) -> p ft tk", ft=32)[:, :, i * 128:(i + 1) * 128]
                s_dwr.inc(sy.dma_start(
                    out=AP(s1, t * 128, [[32 * T, 128], [T, 32], [1, 128]]),
                    in_=src), 16)


def ln_and_transpose_half(p, env, tag, h, rev, write_s1, kc_after=False):
    NTH = p.NT // p.NH
    for i in range(0, NTH, 2):
        ts = list(range(h * NTH + i, h * NTH + min(i + 2, NTH)))
        if (tag, ts[0]) in env.get("done_pairs", set()):
            continue
        if (tag, ts[0]) in env.get("computed_pairs", set()):
            ln_xbars(p, env, tag, h, ts)
            continue
        ln_pair(p, env, tag, h, ts, rev, kc_after=kc_after)
    if write_s1 and (tag, h) not in env.get("done_s1w", set()):
        s1_write_half(p, env, tag, h)


def build_ffn(p, env, tag):
    nc = p.nc
    T, NT, NH, TPH = p.T, p.NT, p.NH, p.TPH
    NTH = NT // NH
    s_pe, s_dve, s_act, s_dg, s_dtp, s_dwr = (p.s[n] for n in
                                              ["pe", "dve", "act", "dg", "dtp", "dwr"])
    s_gt = [p.sem("gt0"), p.sem("gt1")]
    s_sc = p.sem("sc")
    wrk = env["wrk"]
    GB_O, WU_O, WD_O, HID_O, DO_O = (env[n] for n in
                                     ["GB_O", "WU_O", "WD_O", "HID_O", "DO_O"])
    up_in, dn_in, s1, s2, idxt, idxn = (env[n] for n in
                                        ["up_in", "dn_in", "s1", "s2", "idxt", "idxn"])
    psU, psD = env["psU"], env["psD"]
    delt, xres = env["delt"], env["xres"]

    env.setdefault("b0_free", {0: 0, 1: 0, 2: 0})
    env.setdefault("sq_done", {})
    env.setdefault("b0_freed", {0: 0, 1: 0})
    env.setdefault("b0_ready", {})
    env.setdefault("act_cast", {})
    env.setdefault("s1_half", {})
    env.setdefault("delt_free", 0)

    if env.pop("f2A_defer_xb", False) and tag == "f2":
        ln_xbars(p, env, "f2A", 1, [6, 7])
        s1_write_pair(p, env, "f2A", [6, 7])
        env.setdefault("done_s1w", set()).update({("f2A", 0), ("f2A", 1)})
    else:
        for h in range(NH):
            ln_and_transpose_half(p, env, tag + "A", h, rev=False, write_s1=True,
                                  kc_after=(tag == "f1"))
    if tag == "f1":
        kc_flush(p, env)

    CHS = [(c * 512, min(512, T - c * 512)) for c in range((T + 511) // 512)]
    gb_free = {k: 0 for k in range(4)}
    wu_free = {0: 0, 1: 0}
    dout_free = {0: 0, 1: 0}
    gather_done = {}
    w_done = {}
    gelu_done = {}
    evac_done = {}
    up_done = {}
    s1_final = s_dwr.n

    def prefetch2(g, head):
        if head >= 16:
            return
        g.wait_ge(s_dwr.h, s1_final)
        g.wait_ge(s_dtp.h, s_dtp.n)
        sg = s_gt[head % 2]
        for e in range(2):
            k = 2 * head + e
            g.wait_ge(s_pe.h, gb_free[k % 4])
            sg.inc(g.indirect_dma_start(
                out=wrk[:, GB_O + (k % 4) * T:GB_O + (k % 4) * T + T],
                out_offset=None,
                in_=AP(s1, 0, [[T, L], [1, T]]),
                in_offset=IndirectOffsetOnAxis(ap=idxt[:, k:k + 1], axis=0)), 16)
        g.wait_ge(s_pe.h, wu_free[head % 2])
        sg.inc(g.dma_start(
            out=wrk[:, WU_O + (head % 2) * 2048:WU_O + (head % 2) * 2048 + 2048],
            in_=up_in[head].rearrange("(c pp) m -> pp c m", pp=128)), 16)
        sg.inc(g.dma_start(
            out=wrk[:, WD_O + (head % 2) * 2048:WD_O + (head % 2) * 2048 + 2048],
            in_=dn_in[head].rearrange("(c pp) m -> pp c m", pp=128)), 16)
        gather_done[head] = (sg, sg.n)

    for head in range(17):
        if head < 16:
            for j in range(8):
                gelu_done[(head, j)] = s_act.n + j + 1
        with p.block(f"{tag}B{head}") as blk:
            @blk.gpsimd
            def _(g):
                if head > 0:
                    ph = head - 1
                    g.wait_ge(s_dve.h, evac_done[ph])
                    for e in range(2):
                        k = 2 * ph + e
                        s_sc.inc(g.indirect_dma_start(
                            out=s2[:, :],
                            out_offset=IndirectOffsetOnAxis(ap=idxn[:, k:k + 1], axis=0),
                            in_=wrk[:, DO_O + e * T:DO_O + e * T + T],
                            in_offset=None), 16)
                    dout_free[0] = dout_free[1] = s_sc.n
                if head == 0:
                    prefetch2(g, 0)
                    prefetch2(g, 1)
                else:
                    prefetch2(g, head + 1)

            if head >= 16:
                continue

            @blk.tensor
            def _(te):
                if head == 0 and env.get("kc_done_dve"):
                    te.wait_ge(s_dve.h, env["kc_done_dve"])
                te.wait_ge(gather_done[head][0].h, gather_done[head][1])
                wus = WU_O + (head % 2) * 2048
                wds = WD_O + (head % 2) * 2048
                gb0 = wrk[:, GB_O + ((2 * head) % 4) * T:GB_O + ((2 * head) % 4) * T + T]
                gb1 = wrk[:, GB_O + ((2 * head + 1) % 4) * T:GB_O + ((2 * head + 1) % 4) * T + T]
                for j in range(9):
                    if j < 8:
                        ju = j % 2
                        if j >= 2:
                            te.wait_ge(s_act.h, gelu_done[(head, j - 2)])
                        elif head > 0:
                            te.wait_ge(s_act.h, gelu_done[(head - 1, 8 - 2 + j)])
                        for co, cn in CHS:
                            te.matmul(psU[:, ju * 1024 + co:ju * 1024 + co + cn],
                                      wrk[:, wus + j * 128:wus + j * 128 + 128],
                                      gb0[:, co:co + cn], start=True, stop=False)
                        for co, cn in CHS:
                            mm = te.matmul(psU[:, ju * 1024 + co:ju * 1024 + co + cn],
                                           wrk[:, wus + 1024 + j * 128:wus + 1024 + j * 128 + 128],
                                           gb1[:, co:co + cn], start=False, stop=True)
                        s_pe.inc(mm)
                        up_done[(head, j)] = s_pe.n
                    if j >= 1:
                        jd = j - 1
                        te.wait_ge(s_act.h, gelu_done[(head, jd)])
                        if jd == 0 and head > 0:
                            te.wait_ge(s_dve.h, evac_done[head - 1])
                        hs = HID_O + (jd % 3) * T
                        for e in range(2):
                            for ci, (co, cn) in enumerate(CHS):
                                mm = te.matmul(
                                    psD[:, (2 * e + ci) * 512:(2 * e + ci) * 512 + cn],
                                    wrk[:, wds + jd * 256 + e * 128:wds + jd * 256 + e * 128 + 128],
                                    wrk[:, hs + co:hs + co + cn],
                                    start=(jd == 0), stop=(jd == 7))
                        if jd == 7:
                            s_pe.inc(mm)
                    if j < 8:
                        gb_free[(2 * head) % 4] = s_pe.n
                        gb_free[(2 * head + 1) % 4] = s_pe.n
                wu_free[head % 2] = s_pe.n
                env["ffnB_pe_done"] = s_pe.n
                env["wrk_pe_free"] = s_pe.n

            @blk.scalar
            def _(sc):
                for j in range(8):
                    sc.wait_ge(s_pe.h, up_done[(head, j)])
                    tw = CHS[-1][0] + CHS[-1][1]
                    n = s_act.inc(sc.activation(
                        out=wrk[:, HID_O + (j % 3) * T:HID_O + (j % 3) * T + tw],
                        in_=psU[:, (j % 2) * 1024:(j % 2) * 1024 + tw], func=AF.Gelu))
                    assert gelu_done[(head, j)] == n

            @blk.vector
            def _(v):
                v.wait_ge(s_pe.h, s_pe.n)
                if head > 0:
                    v.wait_ge(s_sc.h, dout_free[0])
                tw = CHS[-1][0] + CHS[-1][1]
                for e in range(2):
                    for ci, (co, cn) in enumerate(CHS):
                        s_dve.inc(v.tensor_copy(
                            out=wrk[:, DO_O + e * T + co:DO_O + e * T + co + cn],
                            in_=psD[:, (2 * e + ci) * 512:(2 * e + ci) * 512 + cn]))
                evac_done[head] = s_dve.n

    s_bnd = p.sem("bnd")
    psb = psU.bitcast(BF16)
    identb = env["identb"]
    NQ = 4 if T >= 512 else 1
    QW = T // NQ
    last = (tag == "f2")
    s_dst = p.s["dst"]
    y_out = env["y_out"]
    post_tile_cb = env.pop("postC_cb_next", None) if tag == "f1" else None
    post_tile_cb = post_tile_cb or {}

    if C_DEBUG == "skip":
        env.setdefault("x_ready", {})
        for tt in range(NT):
            env["x_ready"][tt] = s_bnd.n
        env["delt_free"] = p.s["dve"].n
        return
    read_done = {}
    TPQ0 = NT // NQ
    for q in range(NQ):
        env[f"{tag}_cq{q}"] = p.s["pe"].n + (q + 1) * TPQ0 * 32
    s_cr = [p.sem("cr0"), p.sem("cr1")]
    with p.block(f"{tag}Crd") as blk:
        @blk.sync
        def _(sy):
            sy.wait_ge(s_sc.h, s_sc.n)
            sy.wait_ge(s_gt[0].h, s_gt[0].n)
            sy.wait_ge(s_gt[1].h, s_gt[1].n)
            sy.wait_ge(s_dg.h, s_dg.n)
            for q in range(NQ):
                if q >= 2 and C_DEBUG not in ("read",):
                    sy.wait_ge(s_pe.h, env[f"{tag}_cq{q - 2}"])
                for piece in range(4):
                    src = AP(s2, q * QW + piece * 8 * 128 * T,
                             [[T, 128], [128 * T, 8], [1, QW]])
                    s_cr[q % 2].inc(sy.dma_start(
                        out=wrk[:, (q % 2) * 32 * QW + piece * 8 * QW:
                                (q % 2) * 32 * QW + (piece + 1) * 8 * QW]
                        .rearrange("p (g tok) -> p g tok", g=8),
                        in_=src), 16)
                read_done[q] = (s_cr[q % 2], s_cr[q % 2].n)

    TPQ = NT // NQ
    trans_done = {}
    slot_free = {k: 0 for k in range(8)}
    bnd_base = s_bnd.n
    kk = [0]
    if C_DEBUG == "read":
        env.setdefault("x_ready", {})
        for tt in range(NT):
            env["x_ready"][tt] = s_bnd.n
        env["delt_free"] = p.s["dve"].n
        return
    for q in range(NQ):
        with p.block(f"{tag}Cmm{q}") as blk:
            @blk.tensor
            def _(te):
                if q == 0:
                    te.wait_ge(s_act.h, s_act.n)
                    te.wait_ge(s_dve.h, s_dve.n)
                te.wait_ge(read_done[q][0].h, read_done[q][1])
                for tl in range(TPQ):
                    tt = q * TPQ + tl
                    for g in range(32):
                        k = kk[0]
                        G = k // 8
                        if G >= 2 and k % 8 == 0 and C_DEBUG not in ("mm", "nowar"):
                            te.wait_ge(s_bnd.h, bnd_base + (G - 1) * 8)
                        mm = te.transpose(
                            psb[:, (G % 2) * 1024 + (k % 8) * 128:
                                (G % 2) * 1024 + (k % 8) * 128 + 128],
                            wrk[:, (q % 2) * 32 * QW + g * QW + tl * 128:
                                (q % 2) * 32 * QW + g * QW + tl * 128 + 128],
                            identb[:])
                        s_pe.inc(mm)
                        if k % 8 == 7:
                            trans_done[G] = s_pe.n
                        kk[0] += 1
                assert env[f"{tag}_cq{q}"] == s_pe.n

            @blk.vector
            def _(v):
                if C_DEBUG == "mm":
                    return
                for tl in range(TPQ):
                    tt = q * TPQ + tl
                    for g in range(32):
                        k2 = tt * 32 + g
                        G2 = k2 // 8
                        if k2 % 8 == 0:
                            v.wait_ge(s_pe.h, trans_done[G2])
                        xv = xres[:, tt * L + g * 128:tt * L + g * 128 + 128]
                        s_bnd.inc(v.scalar_tensor_tensor(
                            out=xv,
                            in0=psb[:, (G2 % 2) * 1024 + (k2 % 8) * 128:
                                    (G2 % 2) * 1024 + (k2 % 8) * 128 + 128],
                            scalar=0.5, in1=xv, op0=ALU.mult, op1=ALU.add))
                    env.setdefault("x_ready", {})[tt] = s_bnd.n

            if last and C_DEBUG != "mm":
                @blk.scalar
                def _(sc):
                    for tl in range(TPQ):
                        tt = q * TPQ + tl
                        sc.wait_ge(s_bnd.h, bnd_base + (tt + 1) * 32)
                        s_dst.inc(sc.dma_start(
                            out=y_out[tt * 128:(tt + 1) * 128, :],
                            in_=xres[:, tt * L:(tt + 1) * L]), 16)
                env["stored"] = True
        if q in post_tile_cb:
            post_tile_cb[q]()
    if C_DEBUG == "mm":
        env.setdefault("x_ready", {})
        for tt in range(NT):
            env["x_ready"][tt] = bnd_base
        env["stored"] = False
    env["c_stt_done"] = s_bnd.n
    env["uth_pe_gate"] = env[f"{tag}_cq{NQ - 1}"]

    env["delt_free"] = s_dve.n
def ssm_half_mm(p, env, h, cbs):
    nc = p.nc
    T, NT, NH, TPH = p.T, p.NT, p.NH, p.TPH
    NTH = NT // NH
    s_pe, s_dve, s_dg = p.s["pe"], p.s["dve"], p.s["dg"]
    s_bnd = p.sem("bnd")
    wrk, xres = env["wrk"], env["xres"]
    FMT_O = env["FMT_O"]
    psU, psD = env["psU"], env["psD"]
    delt_t, tct_t = env["delt"], env["tct"]
    band_free = env["band_free"]

    def band_ap(b):
        return (psU if b < 4 else psD)[:, (b % 4) * 512:(b % 4 + 1) * 512]

    pe_base = s_pe.n
    env.setdefault("stripe_free", {})
    for tloc in range(NTH):
        env["stripe_free"][tloc] = pe_base + 8 * (tloc + 1)
    s_dtp_h = p.s["dtp"]
    uth_ready = s_dtp_h.n

    for tloc in range(NTH):
        t = h * NTH + tloc
        with p.block(f"ssmM{h}_{tloc}") as blk:
            stt = {}

            @blk.tensor
            def _(te):
                te.wait_ge(s_dg.h, env["tcat_done"])
                te.wait_ge(s_dtp_h.h, env.get("uth_pair_xb", {}).get(t, uth_ready))
                for jb in range(32):
                    lhs = wrk[:, FMT_O + jb * TPH + tloc * 128:
                              FMT_O + jb * TPH + tloc * 128 + 128]
                    bmin = max(0, jb // 4)
                    for b in range(bmin, 8):
                        if jb == 0:
                            te.wait_ge(s_bnd.h, band_free[b])
                        sidx = 3 + 4 * b - jb
                        st_, sp_ = (jb == 0), (jb == min(4 * b + 3, 31))
                        if sidx <= 28:
                            mm = te.matmul(band_ap(b), lhs,
                                           delt_t[:, sidx * 128:sidx * 128 + 512],
                                           start=st_, stop=sp_)
                        else:
                            mm = te.matmul(band_ap(b), lhs,
                                           tct_t[:, (sidx - 29) * 128:(sidx - 29) * 128 + 512],
                                           start=st_, stop=sp_)
                        if sp_:
                            s_pe.inc(mm)
                            stt[b] = s_pe.n

            @blk.vector
            def _(v):
                for b in range(8):
                    v.wait_ge(s_pe.h, stt[b])
                    xv = xres[:, t * L + b * 512:t * L + (b + 1) * 512]
                    s_bnd.inc(v.tensor_add(out=xv, in0=xv, in1=band_ap(b)))
                    band_free[b] = s_bnd.n
                env.setdefault("x_ready", {})[t] = s_bnd.n
        assert s_pe.n == env["stripe_free"][tloc], (s_pe.n, env["stripe_free"][tloc])
        if tloc in cbs:
            cbs[tloc]()


def build_ssm(p, env, post_cbs=None):
    nc = p.nc
    NTH = p.NT // p.NH
    s_dve, s_dg, s_dtp = p.s["dve"], p.s["dg"], p.s["dtp"]
    kext = env["kext"]
    env["band_free"] = {b: env.get("c_stt_done", 0) for b in range(8)}

    with p.block("ssmT") as blk:
        @blk.gpsimd
        def _(g):
            g.wait_ge(p.s["dwr"].h, env.get("kext_done", p.s["dwr"].n))
            g.wait_ge(s_dtp.h, s_dtp.n)
            s_dg.inc(g.dma_start(out=env["delt"][:, 0:4096],
                                 in_=AP(kext, 3585, [[1, 128], [1, 4096]])), 16)
            s_dg.inc(g.dma_start(out=env["tct"][:],
                                 in_=AP(kext, 3585 + 3712, [[1, 128], [1, 768]])), 16)
        env["tcat_done"] = s_dg.n

    ln_and_transpose_half(p, env, "ssm0", 0, rev=True, write_s1=False)

    if p.NH == 2:
        half1 = list(range(p.NT // 2, p.NT))
        prs = [half1[i:i + 2] for i in range(0, NTH, 2)]
        cbs0 = {}
        for k, pr in enumerate(prs):
            def _cb(pr=pr):
                ln_pair(p, env, "ssm1", 1, pr, True)
                env.setdefault("done_pairs", set()).add(("ssm1", pr[0]))
            cbs0[min(k, NTH - 1)] = _cb
        ssm_half_mm(p, env, 0, cbs0)
        ssm_half_mm(p, env, 1, post_cbs or {})
    else:
        ssm_half_mm(p, env, 0, post_cbs or {})


def build_store(p, env):
    nc = p.nc
    NT = p.NT
    s_dve, s_dst = p.s["dve"], p.s["dst"]
    xres, y_out = env["xres"], env["y_out"]
    with p.block("store") as blk:
        @blk.sync
        def _(sy):
            sy.wait_ge(s_dve.h, s_dve.n)
            sy.wait_ge(p.s["bnd"].h, p.s["bnd"].n)
            if not env.get("stored"):
                for t in range(NT):
                    s_dst.inc(sy.dma_start(out=y_out[t * 128:(t + 1) * 128, :],
                                           in_=xres[:, t * L:(t + 1) * L]), 16)
            sy.wait_ge(s_dst.h, s_dst.n)


# version 15
# speedup vs baseline: 1.1950x; 1.0216x over previous
import numpy as np
import ml_dtypes
import concourse.bass as bass
import concourse.mybir as mybir
from concourse.bass import AP, IndirectOffsetOnAxis
from concourse.bass_utils import run_bass_kernel_spmd

F32 = mybir.dt.float32
BF16 = mybir.dt.bfloat16
U32 = mybir.dt.uint32
AF = mybir.ActivationFunctionType
ALU = mybir.AluOpType
C_DEBUG = None
AXX = mybir.AxisListType.X

L = 4096
NST = 64
NCORE = 8
EPS = 1e-5
KCH = 512
NKC = L // KCH


def host_constants():
    lf = np.arange(L, dtype=np.float32) / np.float32(L)
    Omega = np.exp(np.complex64(-2j * np.pi) * lf.astype(np.complex64)).astype(np.complex64)
    h = ((np.complex64(1.0) - Omega) / (np.complex64(1.0) + Omega)).astype(np.complex64)
    c = (np.complex64(2.0) / (np.complex64(1.0) + Omega)).astype(np.complex64)

    def tile32(v):
        return np.ascontiguousarray(v.reshape(32, 128).T)

    consts = {
        "hre_b": np.ascontiguousarray(np.broadcast_to(h.real.astype(np.float32), (NST, L))),
        "him_b": np.ascontiguousarray(np.broadcast_to(h.imag.astype(np.float32), (NST, L))),
        "c_re": tile32(c.real.astype(np.float32)),
        "c_im": tile32(c.imag.astype(np.float32)),
    }
    lt_ = np.arange(32)
    W64 = np.exp(2j * np.pi * np.arange(64) / 64.0)
    ev = W64[np.newaxis, :] ** (2 * lt_)[:, None]
    od = W64[np.newaxis, :] ** (2 * lt_ + 1)[:, None]
    consts["W64ev_re"] = ev.real.astype(np.float32)
    consts["W64ev_im"] = ev.imag.astype(np.float32)
    consts["W64ev_imn"] = (-ev.imag).astype(np.float32)
    consts["W64od_re"] = od.real.astype(np.float32)
    consts["W64od_im"] = od.imag.astype(np.float32)
    consts["W64od_imn"] = (-od.imag).astype(np.float32)
    l1_ = np.arange(64)
    Wt = np.exp(2j * np.pi * np.outer(l1_, np.arange(64)) / 4096.0)
    consts["Wt_re"] = Wt.real.astype(np.float32)
    consts["Wt_im"] = Wt.imag.astype(np.float32)
    W2 = np.exp(2j * np.pi * np.outer(l1_, np.arange(64)) / 64.0) / L
    consts["W2_re"] = W2.real.astype(np.float32)
    consts["W2_imn"] = (-W2.imag).astype(np.float32)
    consts["ones64"] = np.ones((1, 64), np.float32)
    consts["ident"] = np.eye(128, dtype=np.float32)
    consts["identb"] = np.eye(128, dtype=np.float32).astype(ml_dtypes.bfloat16)
    return consts


CONST_SPECS = {
    "hre_b": ([NST, L], F32), "him_b": ([NST, L], F32),
    "c_re": ([128, 32], F32), "c_im": ([128, 32], F32),
    "W64ev_re": ([32, 64], F32), "W64ev_im": ([32, 64], F32),
    "W64ev_imn": ([32, 64], F32),
    "W64od_re": ([32, 64], F32), "W64od_im": ([32, 64], F32),
    "W64od_imn": ([32, 64], F32),
    "Wt_re": ([64, 64], F32), "Wt_im": ([64, 64], F32),
    "W2_re": ([64, 64], F32), "W2_imn": ([64, 64], F32),
    "ones64": ([1, 64], F32), "ident": ([128, 128], F32),
    "identb": ([128, 128], BF16),
}


class Sem:
    __slots__ = ("h", "n")

    def __init__(self, nc, name):
        self.h = nc.alloc_semaphore(name)
        self.n = 0

    def inc(self, inst, k=1):
        inst.then_inc(self.h, k)
        self.n += k
        return self.n


class Prog:
    def __init__(self, T):
        self.blk = None
        self.T = T
        self.NT = T // 128
        self.NH = 2 if self.NT >= 2 else 1
        self.TPH = T // self.NH
        self.nc = bass.Bass("TRN2", target_bir_lowering=False)
        self.s = {}
        self.t = {}

    def sem(self, name):
        if name not in self.s:
            self.s[name] = Sem(self.nc, name)
        return self.s[name]

    from contextlib import contextmanager

    @contextmanager
    def block(self, name):
        yield self.blk


def build_program(T=1024, debug=(), phases=("f1", "ssm", "f2"), hoist_f2a=True):
    p = Prog(T)
    nc = p.nc
    NT, NH, TPH = p.NT, p.NH, p.TPH
    NTH = NT // NH

    x_in = nc.dram_tensor("x", [T, L], F32, kind="ExternalInput")
    y_out = nc.dram_tensor("y", [T, L], F32, kind="ExternalOutput")
    lam_in = nc.dram_tensor("L0_attn_Lambda_ri", [NST, 2], F32, kind="ExternalInput")
    p_in = nc.dram_tensor("L0_attn_p_ri", [NST, 2], F32, kind="ExternalInput")
    q_in = nc.dram_tensor("L0_attn_q_ri", [NST, 2], F32, kind="ExternalInput")
    vc_in = nc.dram_tensor("L0_attn_Vc_ri", [NST, 2 * NST], F32, kind="ExternalInput")
    ct_in = nc.dram_tensor("L0_attn_Ct_ri", [NST, 2], F32, kind="ExternalInput")
    bp_in = nc.dram_tensor("L0_attn_Bp", [NST, 1], F32, kind="ExternalInput")
    d_in = nc.dram_tensor("L0_attn_D", [1, 1], F32, kind="ExternalInput")
    ls_in = nc.dram_tensor("L0_attn_log_step", [1, 1], F32, kind="ExternalInput")
    up_in = nc.dram_tensor("L0_ffn_up", [16, 256, 1024], F32, kind="ExternalInput")
    dn_in = nc.dram_tensor("L0_ffn_down", [16, 1024, 256], F32, kind="ExternalInput")
    pm_in = nc.dram_tensor("permT", [128, 32], U32, kind="ExternalInput")
    cin = {nm: nc.dram_tensor("c_" + nm, shp, dt, kind="ExternalInput")
           for nm, (shp, dt) in CONST_SPECS.items()}

    kext = nc.dram_tensor("kext", [2 * L], F32, kind="Internal")
    s1 = nc.dram_tensor("s1", [L * T], BF16, kind="Internal")
    s2 = nc.dram_tensor("s2", [L, T], BF16, kind="Internal")
    dbg = {}
    if "kext" in debug:
        dbg["kext"] = nc.dram_tensor("dbg_kext", [2 * L], F32, kind="ExternalOutput")
    if "tcat" in debug:
        dbg["tcat"] = nc.dram_tensor("dbg_tcat", [128, 4480], BF16, kind="ExternalOutput")
    for st in ("x1", "x2", "ln1", "s2d"):
        if st in debug:
            dbg[st] = nc.dram_tensor("dbg_" + st, [T, L],
                                     BF16 if st in ("ln1", "s2d") else F32,
                                     kind="ExternalOutput")

    XBYTES = NT * L * 4
    WRK_E = 24576
    RES_BYTES = XBYTES + WRK_E * 2 + 8192 + 5 * (KCH * 4)
    SB_BASE = (nc._sbuf_addr_for_side("left") + 63) & ~63
    nc.alloc_sbuf_tensor("reserve", [128, SB_BASE + RES_BYTES - 16384],
                         mybir.dt.uint8)

    def at(name, shape, dtype, off):
        h = nc.alloc_sbuf_tensor_at(name, shape, dtype, offset=SB_BASE + off)
        p.t[name] = h
        return h

    off = 0
    xres = at("xres", [128, NT * L], F32, 0); off += XBYTES
    wrk = at("wrk", [128, WRK_E], BF16, off); off += WRK_E * 2
    delt = at("delt", [128, 4096], BF16, off); off += 8192
    kch = {}
    for nm in ["dre", "dim", "mg", "rcre", "rcim"]:
        kch[nm] = at("k_" + nm, [NST, KCH], F32, off); off += KCH * 4

    def mk(name, shape, dtype):
        h = nc.alloc_sbuf_tensor(name, shape, dtype)
        p.t[name] = h
        return h

    sums = mk("sums", [128, NT], F32)
    ssq = mk("ssq", [128, NT], F32)
    mcol = mk("mcol", [128, NT], F32)
    rstd = mk("rstd", [128, NT], F32)
    nmr = mk("nmr", [128, NT], F32)
    idxn = mk("idxn", [128, 32], U32)
    idxt = mk("idxt", [128, 32], U32)
    idxf = mk("idxf", [128, 32], U32)
    idxf2 = mk("idxf2", [128, 32], U32)
    kbuf = mk("kbuf", [128, 256], F32)
    c1sb = mk("c1sb", [128, 256], F32)
    scr32 = mk("scr32", [128, 192], F32)
    cre = mk("cre", [128, 32], F32)
    cim = mk("cim", [128, 32], F32)
    ident = mk("ident", [128, 128], F32)
    identb = mk("identb", [128, 128], BF16)

    lam = mk("lam", [NST, 2], F32)
    pri = mk("pri", [NST, 2], F32)
    qri = mk("qri", [NST, 2], F32)
    ctri = mk("ctri", [NST, 2], F32)
    bp = mk("bp", [NST, 1], F32)
    vmat = mk("vmat", [NST, 8], F32)
    vcs = mk("vcs", [NST, 2 * NST], F32)
    vctr = mk("vctr", [NST, 2 * NST], F32)
    bvec = mk("bvec", [NST, 2], F32)
    w1 = mk("w1", [32, 6 * 64], F32)
    w2 = mk("w2", [NST, 8 * 64], F32)
    arT = mk("arT", [32, 2 * 128], F32)
    absb = mk("absb", [NST, 4 * 64], F32)
    ksb = mk("ksb", [NST, NST], F32)
    zsb = mk("zsb", [NST, NST], F32)
    sca = mk("sca", [1, 8], F32)
    s2c = mk("s2c", [NST, 1], F32)
    onesb = mk("onesb", [1, NST], F32)
    ddsb = mk("ddsb", [1, 1], F32)
    epsc = mk("epsc", [128, 1], F32)
    tct = mk("tct", [128, 768], BF16)

    FMT_O = 0
    B0_O = 32 * TPH
    GB_O = 0
    WU_O = 4 * T
    WD_O = 4 * T + 4096
    HID_O = 4 * T + 8192
    DO_O = 7 * T + 8192
    TC_O = B0_O

    psU = nc.alloc_psum_tensor("psU", [128, 2048], F32)
    psD = nc.alloc_psum_tensor("psD", [128, 2048], F32)

    def band_ap(b):
        return (psU if b < 4 else psD)[:, (b % 4) * 512:(b % 4 + 1) * 512]

    s_dx = p.sem("dx")
    s_dx2 = p.sem("dx2")
    s_dc = p.sem("dc")
    s_dg = p.sem("dg")
    s_dtp = p.sem("dtp")
    s_dwr = p.sem("dwr")
    s_pe = p.sem("pe")
    s_dve = p.sem("dve")
    s_act = p.sem("act")
    s_dst = p.sem("dst")
    s_bnd = p.sem("bnd")

    env = dict(locals())
    env['HOIST_F2A'] = hoist_f2a

    p.blk = nc.Block("all", no_gpsimd_drain=True)
    p.blk.__enter__()

    with p.block("ph0") as blk:
        @blk.sync
        def _(sy):
            s_dc.inc(sy.dma_start(out=cre[:], in_=cin["c_re"][:]), 16)
            s_dc.inc(sy.dma_start(out=cim[:], in_=cin["c_im"][:]), 16)
            s_dc.inc(sy.dma_start(out=ident[:], in_=cin["ident"][:]), 16)
            s_dc.inc(sy.dma_start(out=identb[:], in_=cin["identb"][:]), 16)
            s_dc.inc(sy.dma_start(out=onesb[:], in_=cin["ones64"][:]), 16)
            for i, nm in enumerate(["W64ev_re", "W64ev_im", "W64ev_imn",
                                    "W64od_re", "W64od_im", "W64od_imn"]):
                s_dc.inc(sy.dma_start(out=w1[:, i * 64:(i + 1) * 64], in_=cin[nm][:]), 16)
            for i, nm in enumerate(["Wt_re", "Wt_im", "W2_re", "W2_imn"]):
                s_dc.inc(sy.dma_start(out=w2[:, i * 64:(i + 1) * 64], in_=cin[nm][:]), 16)
            env["x_load"] = {}
            xsems = [s_dx, s_dx2, p.sem("dx3"), p.sem("dx4")]
            for t_ in range(NT):
                sm = xsems[(t_ // 2) % 4]
                sm.inc(sy.dma_start(out=xres[:, t_ * L:(t_ + 1) * L],
                                    in_=x_in[t_ * 128:(t_ + 1) * 128, :]), 16)
            for t_ in range(NT):
                sm = xsems[(t_ // 2) % 4]
                env["x_load"][t_] = (sm, sm.n)

        @blk.scalar
        def _(sc):
            for dst_, src_ in [(lam, lam_in), (pri, p_in), (qri, q_in),
                               (ctri, ct_in), (bp, bp_in), (ddsb, d_in),
                               (vcs, vc_in)]:
                s_dc.inc(sc.dma_start(out=dst_[:], in_=src_[:]), 16)
            s_dc.inc(sc.dma_start(out=sca[:, 0:1], in_=ls_in[:]), 16)

        @blk.gpsimd
        def _(g):
            s_dg.inc(g.dma_start(out=idxn[:], in_=pm_in[:]), 16)

        @blk.vector
        def _(v):
            p.s["dve"].inc(v.memset(epsc[:], EPS))

    build_idx(p, env)
    build_kcomp_head(p, env)
    env["kc_pending"] = list(range(NKC))
    if "f1" in phases:
        if "ssm" in phases and NH == 2 and NT // NH == 4:
            def _pc1():
                ln_pair(p, env, "ssm0", 0, [0, 1], True, xbars=False)
                env.setdefault("computed_pairs", set()).add(("ssm0", 0))
            env["postC_cb_next"] = {1: _pc1}
        build_ffn(p, env, "f1")
    else:
        kc_flush(p, env)
    dump_dbg(p, env, "x1")
    if "ssm" in phases:
        cbs1 = {}
        if "f2" in phases and NH == 2 and NT // NH == 4:
            if env.get('HOIST_F2A', True):
                env.setdefault("stripe_map", {}).update(
                    {("f2A", t): t % 2 for t in range(NT)})

            def _pcb(h, ts):
                def f():
                    ln_pair(p, env, "f2A", h, ts, False)
                    s1_write_pair(p, env, "f2A", ts)
                    env.setdefault("done_pairs", set()).add(("f2A", ts[0]))
                return f

            def _cb3():
                ln_pair(p, env, "f2A", 1, [6, 7], False, xbars=False)
                env.setdefault("done_pairs", set()).add(("f2A", 6))
                env["f2A_defer_xb"] = True

            cbs1 = ({0: _pcb(0, [0, 1]), 1: _pcb(0, [2, 3]),
                     2: _pcb(1, [4, 5]), 3: _cb3}
                    if env.get('HOIST_F2A', True) else {})
        elif "f2" in phases and NH == 2 and NT // NH == 2:
            def _cb0():
                ln_pair(p, env, "f2A", 0, [0, 1], False)
                env.setdefault("done_pairs", set()).add(("f2A", 0))
            cbs1 = {0: _cb0}
        build_ssm(p, env, post_cbs=cbs1)
    dump_dbg(p, env, "x2")
    if "f2" in phases:
        build_ffn(p, env, "f2")
    build_store(p, env)
    finish_program(p)
    out_names = ["y"] + ["dbg_" + k for k in dbg]
    return p, env, out_names


def finish_program(p):
    nc = p.nc
    with p.block("fin") as blk:
        @blk.sync
        def _(sy):
            for nm in ["dx", "dx2", "dx3", "dx4", "dc", "dtp", "dwr", "dst",
                       "ctp0", "ctp1", "cr0", "cr1"]:
                if p.s[nm].n if nm in p.s else 0:
                    sy.wait_ge(p.s[nm].h, p.s[nm].n)

        @blk.gpsimd
        def _(g):
            for nm in ["dg", "gt0", "gt1", "sc"]:
                if nm in p.s and p.s[nm].n:
                    g.wait_ge(p.s[nm].h, p.s[nm].n)
    p.blk.__exit__(None, None, None)


def kc_step(p, env, n=2):
    for _ in range(n):
        if env.get("kc_pending"):
            ch = env["kc_pending"].pop(0)
            kc_chunk(p, env, ch)


def kc_flush(p, env):
    while env.get("kc_pending"):
        kc_chunk(p, env, env["kc_pending"].pop(0))
    if not env.get("kc_tail_done"):
        build_kcomp_tail(p, env)
        env["kc_tail_done"] = True
        env["kc_done_dve"] = p.s["dve"].n
        env["kc_done_pe"] = p.s["pe"].n


def build_kcomp_head(p, env):
    nc = p.nc
    s_dc, s_dve, s_act, s_pe, s_dwr, s_dg = (p.s[n] for n in
                                             ["dc", "dve", "act", "pe", "dwr", "dg"])
    g_ = env
    kch = g_["kch"]
    (lam, pri, qri, ctri, bp, vmat, vcs, vctr, bvec, w1, w2, arT, absb, ksb,
     zsb, sca, s2c, onesb, ddsb, ident, cre, cim, kbuf, c1sb, scr32,
     psU, psD, kext, dbg) = (g_[n] for n in [
         "lam", "pri", "qri", "ctri", "bp", "vmat", "vcs", "vctr", "bvec",
         "w1", "w2", "arT", "absb", "ksb", "zsb", "sca", "s2c", "onesb",
         "ddsb", "ident", "cre", "cim", "kbuf", "c1sb", "scr32",
         "psU", "psD", "kext", "dbg"])
    NCONST = s_dc.n

    with p.block("kc_sc") as blk:
        @blk.scalar
        def _(sc):
            sc.wait_ge(s_dc.h, NCONST)
            s_act.inc(sc.activation(out=sca[:, 1:2], in_=sca[:, 0:1], func=AF.Exp))

        @blk.vector
        def _(v):
            v.wait_ge(s_act.h, s_act.n)
            s_dve.inc(v.reciprocal(out=sca[:, 2:3], in_=sca[:, 1:2]))
            v.wait_ge(s_dve.h, s_dve.n)
            s_dve.inc(v.tensor_scalar(out=sca[:, 3:4], in0=sca[:, 2:3],
                                      scalar1=2.0, scalar2=None, op0=ALU.mult))

        @blk.tensor
        def _(te):
            te.wait_ge(s_dve.h, s_dve.n)
            s_pe.inc(te.matmul(psD[:NST, 0:1], onesb[:, :NST], sca[:, 3:4],
                               start=True, stop=True))

        @blk.vector
        def _(v):
            v.wait_ge(s_pe.h, s_pe.n)
            s_dve.inc(v.tensor_copy(out=s2c[:], in_=psD[:NST, 0:1]))

    with p.block("kc_vv") as blk:
        @blk.vector
        def _(v):
            v.wait_ge(s_dc.h, NCONST)
            vcr = vcs[:].rearrange("p (c two) -> p two c", two=2)
            s_dve.inc(v.tensor_copy(out=vctr[:, 0:NST], in_=vcr[:, 0, :]))
            s_dve.inc(v.tensor_copy(out=vctr[:, NST:2 * NST], in_=vcr[:, 1, :]))

        @blk.tensor
        def _(te):
            te.wait_ge(s_dve.h, s_dve.n)
            s_pe.inc(te.transpose(psU[:NST, 0:NST], vctr[:NST, 0:NST], ident[:NST, :NST]))
            s_pe.inc(te.transpose(psU[:NST, 512:512 + NST], vctr[:NST, NST:2 * NST],
                                  ident[:NST, :NST]))

        @blk.vector
        def _(v):
            v.wait_ge(s_pe.h, s_pe.n)
            s_dve.inc(v.tensor_copy(out=vcs[:, 0:NST], in_=psU[:NST, 0:NST]))
            s_dve.inc(v.tensor_copy(out=vcs[:, NST:2 * NST], in_=psU[:NST, 512:512 + NST]))

        @blk.tensor
        def _(te):
            te.wait_ge(s_dve.h, s_dve.n)
            s_pe.inc(te.matmul(psD[:NST, 512:513], vcs[:NST, 0:NST], bp[:NST, :],
                               start=True, stop=True))
            s_pe.inc(te.matmul(psD[:NST, 1024:1025], vcs[:NST, NST:2 * NST], bp[:NST, :],
                               start=True, stop=True))

        @blk.vector
        def _(v):
            v.wait_ge(s_pe.h, s_pe.n)
            s_dve.inc(v.tensor_copy(out=bvec[:, 0:1], in_=psD[:NST, 512:513]))
            s_dve.inc(v.tensor_copy(out=bvec[:, 1:2], in_=psD[:NST, 1024:1025]))
            v.wait_ge(s_dve.h, s_dve.n)

            def cmul_conj(col, a_re, a_im, c_re_, c_im_):
                s_dve.inc(v.tensor_mul(out=scr32[:NST, 0:1], in0=a_re, in1=c_re_))
                s_dve.inc(v.tensor_mul(out=scr32[:NST, 1:2], in0=a_im, in1=c_im_))
                s_dve.inc(v.tensor_mul(out=scr32[:NST, 2:3], in0=a_re, in1=c_im_))
                s_dve.inc(v.tensor_mul(out=scr32[:NST, 3:4], in0=a_im, in1=c_re_))
                v.wait_ge(s_dve.h, s_dve.n)
                s_dve.inc(v.tensor_add(out=vmat[:, col:col + 1],
                                       in0=scr32[:NST, 0:1], in1=scr32[:NST, 1:2]))
                s_dve.inc(v.tensor_sub(out=vmat[:, col + 1:col + 2],
                                       in0=scr32[:NST, 2:3], in1=scr32[:NST, 3:4]))
                v.wait_ge(s_dve.h, s_dve.n)

            cmul_conj(0, ctri[:, 0:1], ctri[:, 1:2], bvec[:, 0:1], bvec[:, 1:2])
            cmul_conj(2, ctri[:, 0:1], ctri[:, 1:2], pri[:, 0:1], pri[:, 1:2])
            cmul_conj(4, qri[:, 0:1], qri[:, 1:2], bvec[:, 0:1], bvec[:, 1:2])
            cmul_conj(6, qri[:, 0:1], qri[:, 1:2], pri[:, 0:1], pri[:, 1:2])


def kc_chunk(p, env, ch):
    nc = p.nc
    g_ = env
    kch = env["kch"]
    s_dc, s_dve, s_act, s_pe, s_dwr, s_dg = (p.s[n] for n in
                                             ["dc", "dve", "act", "pe", "dwr", "dg"])
    vmat, psU = env["vmat"], env["psU"]
    lam, s2c = env["lam"], env["s2c"]
    NLT = KCH // 128
    if True:
        with p.block(f"kc_ch{ch}") as blk:
            @blk.gpsimd
            def _(g2):
                g2.wait_ge(s_dve.h, s_dve.n)
                g2.wait_ge(s_pe.h, s_pe.n)
                s_dg.inc(g2.dma_start(out=kch["dre"][:],
                                      in_=g_["cin"]["hre_b"][:, ch * KCH:(ch + 1) * KCH]), 16)
                s_dg.inc(g2.dma_start(out=kch["dim"][:],
                                      in_=g_["cin"]["him_b"][:, ch * KCH:(ch + 1) * KCH]), 16)

            @blk.vector
            def _(v):
                v.wait_ge(s_dg.h, s_dg.n)
                s_dve.inc(v.tensor_scalar(out=kch["dre"][:], in0=kch["dre"][:],
                                          scalar1=s2c[:], scalar2=lam[:, 0:1],
                                          op0=ALU.mult, op1=ALU.subtract))
                s_dve.inc(v.tensor_scalar(out=kch["dim"][:], in0=kch["dim"][:],
                                          scalar1=s2c[:], scalar2=lam[:, 1:2],
                                          op0=ALU.mult, op1=ALU.subtract))
                v.wait_ge(s_dve.h, s_dve.n)
                s_dve.inc(v.tensor_mul(out=kch["mg"][:], in0=kch["dre"][:], in1=kch["dre"][:]))
                s_dve.inc(v.tensor_mul(out=kch["rcim"][:], in0=kch["dim"][:], in1=kch["dim"][:]))
                v.wait_ge(s_dve.h, s_dve.n)
                s_dve.inc(v.tensor_add(out=kch["mg"][:], in0=kch["mg"][:], in1=kch["rcim"][:]))
                v.wait_ge(s_dve.h, s_dve.n)
                s_dve.inc(v.reciprocal(out=kch["mg"][:], in_=kch["mg"][:]))
                v.wait_ge(s_dve.h, s_dve.n)
                s_dve.inc(v.tensor_mul(out=kch["rcre"][:], in0=kch["dre"][:], in1=kch["mg"][:]))
                s_dve.inc(v.scalar_tensor_tensor(out=kch["rcim"][:], in0=kch["dim"][:],
                                                 scalar=-1.0, in1=kch["mg"][:],
                                                 op0=ALU.mult, op1=ALU.mult))

            @blk.tensor
            def _(te):
                te.wait_ge(s_dve.h, s_dve.n)
                for i in range(NLT):
                    lt = ch * NLT + i
                    s_pe.inc(te.matmul(psU[:, lt * 8:lt * 8 + 8],
                                       kch["rcre"][:, i * 128:(i + 1) * 128],
                                       vmat[:NST, :], start=True, stop=True))
                    s_pe.inc(te.matmul(psU[:, 512 + lt * 8:512 + lt * 8 + 8],
                                       kch["rcim"][:, i * 128:(i + 1) * 128],
                                       vmat[:NST, :], start=True, stop=True))


def build_kcomp_tail(p, env):
    nc = p.nc
    g_ = env
    (lam, pri, qri, ctri, bp, vmat, vcs, vctr, bvec, w1, w2, arT, absb, ksb,
     zsb, sca, s2c, onesb, ddsb, ident, cre, cim, kbuf, c1sb, scr32,
     psU, psD, kext, dbg) = (g_[n] for n in [
         "lam", "pri", "qri", "ctri", "bp", "vmat", "vcs", "vctr", "bvec",
         "w1", "w2", "arT", "absb", "ksb", "zsb", "sca", "s2c", "onesb",
         "ddsb", "ident", "cre", "cim", "kbuf", "c1sb", "scr32",
         "psU", "psD", "kext", "dbg"])
    s_dc, s_dve, s_act, s_pe, s_dwr, s_dg = (p.s[n] for n in
                                             ["dc", "dve", "act", "pe", "dwr", "dg"])
    with p.block("kc_comb") as blk:
        @blk.vector
        def _(v):
            v.wait_ge(s_pe.h, s_pe.n)
            s_dve.inc(v.tensor_copy(out=c1sb[:], in_=psU[:, 0:256]))
            v.wait_ge(s_dve.h, s_dve.n)
            c1r = c1sb[:].rearrange("p (lt j) -> p j lt", j=8)
            c2r = psU[:, 512:768].rearrange("p (lt j) -> p j lt", j=8)
            kr = kbuf[:].rearrange("p (g lt) -> p g lt", g=8)
            for j in range(4):
                s_dve.inc(v.tensor_sub(out=kr[:, 2 * j, :], in0=c1r[:, 2 * j, :],
                                       in1=c2r[:, 2 * j + 1, :]))
                s_dve.inc(v.tensor_add(out=kr[:, 2 * j + 1, :], in0=c1r[:, 2 * j + 1, :],
                                       in1=c2r[:, 2 * j, :]))
            v.wait_ge(s_dve.h, s_dve.n)

    with p.block("kc_ar") as blk:
        @blk.vector
        def _(v):
            kr = kbuf[:].rearrange("p (g lt) -> p g lt", g=8)
            k00r, k00i = kr[:, 0, :], kr[:, 1, :]
            k01r, k01i = kr[:, 2, :], kr[:, 3, :]
            k10r, k10i = kr[:, 4, :], kr[:, 5, :]
            k11r, k11i = kr[:, 6, :], kr[:, 7, :]
            S = scr32[:].rearrange("p (g c) -> p g c", c=32)
            tre, tim, dre, dmag, u1 = (S[:, i, :] for i in range(5))
            s_dve.inc(v.tensor_mul(out=tre, in0=k01r, in1=k10r))
            s_dve.inc(v.tensor_mul(out=u1, in0=k01i, in1=k10i))
            s_dve.inc(v.tensor_mul(out=tim, in0=k01r, in1=k10i))
            s_dve.inc(v.tensor_mul(out=dre, in0=k01i, in1=k10r))
            v.wait_ge(s_dve.h, s_dve.n)
            s_dve.inc(v.tensor_sub(out=tre, in0=tre, in1=u1))
            s_dve.inc(v.tensor_add(out=tim, in0=tim, in1=dre))
            v.wait_ge(s_dve.h, s_dve.n)
            s_dve.inc(v.tensor_scalar(out=dre, in0=k11r, scalar1=1.0, scalar2=None,
                                      op0=ALU.add))
            v.wait_ge(s_dve.h, s_dve.n)
            s_dve.inc(v.tensor_mul(out=dmag, in0=dre, in1=dre))
            s_dve.inc(v.tensor_mul(out=u1, in0=k11i, in1=k11i))
            v.wait_ge(s_dve.h, s_dve.n)
            s_dve.inc(v.tensor_add(out=dmag, in0=dmag, in1=u1))
            v.wait_ge(s_dve.h, s_dve.n)
            s_dve.inc(v.reciprocal(out=dmag, in_=dmag))
            v.wait_ge(s_dve.h, s_dve.n)
            s_dve.inc(v.tensor_mul(out=u1, in0=tre, in1=dre))
            s_dve.inc(v.tensor_mul(out=k10r, in0=tim, in1=k11i))
            s_dve.inc(v.tensor_mul(out=k10i, in0=tim, in1=dre))
            s_dve.inc(v.tensor_mul(out=k01r, in0=tre, in1=k11i))
            v.wait_ge(s_dve.h, s_dve.n)
            s_dve.inc(v.tensor_add(out=u1, in0=u1, in1=k10r))
            s_dve.inc(v.tensor_sub(out=k10i, in0=k10i, in1=k01r))
            v.wait_ge(s_dve.h, s_dve.n)
            s_dve.inc(v.tensor_mul(out=u1, in0=u1, in1=dmag))
            s_dve.inc(v.tensor_mul(out=k10i, in0=k10i, in1=dmag))
            v.wait_ge(s_dve.h, s_dve.n)
            s_dve.inc(v.tensor_sub(out=k00r, in0=k00r, in1=u1))
            s_dve.inc(v.tensor_sub(out=k00i, in0=k00i, in1=k10i))
            v.wait_ge(s_dve.h, s_dve.n)
            s_dve.inc(v.tensor_mul(out=k01r, in0=cre[:], in1=k00r))
            s_dve.inc(v.tensor_mul(out=u1, in0=cim[:], in1=k00i))
            s_dve.inc(v.tensor_mul(out=k01i, in0=cre[:], in1=k00i))
            s_dve.inc(v.tensor_mul(out=k10r, in0=cim[:], in1=k00r))
            v.wait_ge(s_dve.h, s_dve.n)
            s_dve.inc(v.tensor_sub(out=k01r, in0=k01r, in1=u1))
            s_dve.inc(v.tensor_add(out=k01i, in0=k01i, in1=k10r))

    with p.block("kc_idft") as blk:
        kr = kbuf[:].rearrange("p (g lt) -> p g lt", g=8)
        ar_re, ar_im = kr[:, 2, :], kr[:, 3, :]

        @blk.tensor
        def _(te):
            te.wait_ge(s_dve.h, s_dve.n)
            s_pe.inc(te.transpose(psD[:32, 0:128], ar_re, ident[:, :]))
            s_pe.inc(te.transpose(psD[:32, 512:512 + 128], ar_im, ident[:, :]))

        @blk.vector
        def _(v):
            v.wait_ge(s_pe.h, s_pe.n)
            s_dve.inc(v.tensor_copy(out=arT[:, 0:128], in_=psD[:32, 0:128]))
            s_dve.inc(v.tensor_copy(out=arT[:, 128:256], in_=psD[:32, 512:512 + 128]))

        @blk.tensor
        def _(te):
            te.wait_ge(s_dve.h, s_dve.n)
            arL, arH = arT[:32, 0:64], arT[:32, 64:128]
            aiL, aiH = arT[:32, 128:192], arT[:32, 192:256]
            wev_re, wev_im, wev_imn = w1[:32, 0:64], w1[:32, 64:128], w1[:32, 128:192]
            wod_re, wod_im, wod_imn = w1[:32, 192:256], w1[:32, 256:320], w1[:32, 320:384]
            pa_re, pa_im = psU[:NST, 1024:1088], psU[:NST, 1536:1600]
            s_pe.inc(te.matmul(pa_re, arL, wev_re, start=True, stop=False))
            te.matmul(pa_re, arH, wod_re, start=False, stop=False)
            te.matmul(pa_re, aiL, wev_imn, start=False, stop=False)
            s_pe.inc(te.matmul(pa_re, aiH, wod_imn, start=False, stop=True))
            s_pe.inc(te.matmul(pa_im, arL, wev_im, start=True, stop=False))
            te.matmul(pa_im, arH, wod_im, start=False, stop=False)
            te.matmul(pa_im, aiL, wev_re, start=False, stop=False)
            s_pe.inc(te.matmul(pa_im, aiH, wod_re, start=False, stop=True))

        @blk.vector
        def _(v):
            v.wait_ge(s_pe.h, s_pe.n)
            s_dve.inc(v.tensor_copy(out=absb[:, 0:64], in_=psU[:NST, 1024:1088]))
            s_dve.inc(v.tensor_copy(out=absb[:, 64:128], in_=psU[:NST, 1536:1600]))
            v.wait_ge(s_dve.h, s_dve.n)
            a_re, a_im = absb[:, 0:64], absb[:, 64:128]
            b_re, b_im = absb[:, 128:192], absb[:, 192:256]
            wt_re, wt_im = w2[:, 0:64], w2[:, 64:128]
            s_dve.inc(v.tensor_mul(out=b_re, in0=a_re, in1=wt_re))
            s_dve.inc(v.tensor_mul(out=b_im, in0=a_re, in1=wt_im))
            s_dve.inc(v.tensor_mul(out=scr32[:NST, 0:64], in0=a_im, in1=wt_im))
            s_dve.inc(v.tensor_mul(out=scr32[:NST, 64:128], in0=a_im, in1=wt_re))
            v.wait_ge(s_dve.h, s_dve.n)
            s_dve.inc(v.tensor_sub(out=b_re, in0=b_re, in1=scr32[:NST, 0:64]))
            s_dve.inc(v.tensor_add(out=b_im, in0=b_im, in1=scr32[:NST, 64:128]))

        @blk.tensor
        def _(te):
            te.wait_ge(s_dve.h, s_dve.n)
            pk = psD[:NST, 1024:1088]
            s_pe.inc(te.matmul(pk, w2[:, 128:192], absb[:, 128:192], start=True, stop=False))
            s_pe.inc(te.matmul(pk, w2[:, 192:256], absb[:, 192:256], start=False, stop=True))

        @blk.vector
        def _(v):
            v.wait_ge(s_pe.h, s_pe.n)
            s_dve.inc(v.tensor_add(out=psD[:1, 1024:1025], in0=psD[:1, 1024:1025],
                                   in1=ddsb[:, :]))
            v.wait_ge(s_dve.h, s_dve.n)
            s_dve.inc(v.tensor_copy(out=ksb[:], in_=psD[:NST, 1024:1088]))
            s_dve.inc(v.memset(zsb[:], 0.0))

        @blk.sync
        def _(sy):
            sy.wait_ge(s_dve.h, s_dve.n)
            s_dwr.inc(sy.dma_start(out=AP(kext, 0, [[64, 64], [1, 64]]), in_=zsb[:]), 16)
            s_dwr.inc(sy.dma_start(out=AP(kext, L, [[64, 64], [1, 64]]), in_=ksb[:]), 16)
            env["kext_done"] = s_dwr.n
            if "kext" in dbg:
                sy.wait_ge(s_dwr.h, s_dwr.n)
                s_dwr.inc(sy.dma_start(out=AP(dbg["kext"], 0, [[1, 1], [1, 2 * L]]),
                                       in_=AP(kext, 0, [[1, 1], [1, 2 * L]])), 16)
            if "tcat" in dbg:
                pass
            sy.wait_ge(s_dwr.h, s_dwr.n)

    if "tcat" in dbg:
        wrk = env_wrk(p)
        with p.block("kc_tcdbg") as blk:
            @blk.gpsimd
            def _(g):
                g.wait_ge(s_dwr.h, s_dwr.n)
                s_dg.inc(g.dma_start(out=wrk[:, 16384:16384 + 4480],
                                     in_=AP(kext, 3585, [[1, 128], [1, 4480]])), 16)
                g.wait_ge(s_dg.h, s_dg.n)
                s_dg.inc(g.dma_start(out=dbg["tcat"][:], in_=wrk[:, 16384:16384 + 4480]), 16)
                g.wait_ge(s_dg.h, s_dg.n)


def dump_dbg(p, env, st):
    if st not in env["dbg"]:
        return
    nc = p.nc
    with p.block("dbg_" + st) as blk:
        @blk.sync
        def _(sy):
            sy.wait_ge(p.s["dve"].h, p.s["dve"].n)
            sy.wait_ge(p.s["bnd"].h, p.s["bnd"].n)
            for t_ in range(p.NT):
                p.s["dst"].inc(sy.dma_start(
                    out=env["dbg"][st][t_ * 128:(t_ + 1) * 128, :],
                    in_=env["xres"][:, t_ * L:(t_ + 1) * L]), 16)
            sy.wait_ge(p.s["dst"].h, p.s["dst"].n)


def env_wrk(p):
    return p.t["wrk"]


def env_hsrc(p, which, ch):
    return p.t["hri"]


def build_idx(p, env):
    nc = p.nc
    s_dg, s_dve = p.s["dg"], p.s["dve"]
    idxn, idxt, idxf, idxf2 = (env[n] for n in ["idxn", "idxt", "idxf", "idxf2"])
    with p.block("idx") as blk:
        @blk.vector
        def _(v):
            v.wait_ge(s_dg.h, 16)
            s_dve.inc(v.tensor_scalar(out=idxf[:], in0=idxn[:], scalar1=127,
                                      scalar2=5, op0=ALU.bitwise_and,
                                      op1=ALU.logical_shift_left))
            s_dve.inc(v.tensor_scalar(out=idxf2[:], in0=idxn[:], scalar1=7,
                                      scalar2=None, op0=ALU.logical_shift_right))
            v.wait_ge(s_dve.h, s_dve.n)
            s_dve.inc(v.tensor_tensor(out=idxt[:], in0=idxf[:], in1=idxf2[:],
                                      op=ALU.add))


_CACHE = {}


def kernel(L0_attn_u, L0_attn_Lambda_ri, L0_attn_p_ri, L0_attn_q_ri,
           L0_attn_Vc_ri, L0_attn_Ct_ri, L0_attn_Bp, L0_attn_D,
           L0_attn_log_step, L0_ffn_up, L0_ffn_down, L0_ffn_perm):
    T = 8192 // NCORE
    x = np.ascontiguousarray(np.asarray(L0_attn_u, dtype=np.float32).reshape(8192, L))
    if "prog" not in _CACHE:
        _CACHE["prog"] = build_program(T=T)
        _CACHE["consts"] = host_constants()
    p, env, outs = _CACHE["prog"]
    consts = _CACHE["consts"]

    base = {
        "L0_attn_Lambda_ri": np.asarray(L0_attn_Lambda_ri, np.float32),
        "L0_attn_p_ri": np.asarray(L0_attn_p_ri, np.float32),
        "L0_attn_q_ri": np.asarray(L0_attn_q_ri, np.float32),
        "L0_attn_Vc_ri": np.ascontiguousarray(
            np.asarray(L0_attn_Vc_ri, np.float32).reshape(NST, 2 * NST)),
        "L0_attn_Ct_ri": np.ascontiguousarray(
            np.asarray(L0_attn_Ct_ri, np.float32).reshape(NST, 2)),
        "L0_attn_Bp": np.asarray(L0_attn_Bp, np.float32).reshape(NST, 1),
        "L0_attn_D": np.asarray(L0_attn_D, np.float32).reshape(1, 1),
        "L0_attn_log_step": np.asarray(L0_attn_log_step, np.float32).reshape(1, 1),
        "L0_ffn_up": np.asarray(L0_ffn_up, np.float32),
        "L0_ffn_down": np.asarray(L0_ffn_down, np.float32),
        "permT": np.ascontiguousarray(
            np.asarray(L0_ffn_perm).astype(np.uint32).reshape(32, 128).T),
    }
    for nm in CONST_SPECS:
        base["c_" + nm] = consts[nm]

    in_maps = [dict(base, x=np.ascontiguousarray(x[c * T:(c + 1) * T]))
               for c in range(NCORE)]
    res = run_bass_kernel_spmd(p.nc, in_maps, core_ids=list(range(NCORE)))
    y = np.concatenate([np.asarray(r["y"]) for r in res.results], axis=0)
    return y.reshape(2, 4096, L)


def b0_loc(p, env, tag, t):
    if tag == "f1A":
        sid = t % 3
        if sid == 2:
            return env["delt"], 0, 2
    else:
        sid = t % 2
    return env["wrk"], env["B0_O"] + sid * 4096, sid


def ln_tile(p, env, blk, t, rev, tag):
    nc = p.nc
    s_dve, s_act, s_dx = p.s["dve"], p.s["act"], p.s["dx"]
    xt = env["xres"][:, t * L:(t + 1) * L]
    wrk, B0_O = env["wrk"], env["B0_O"]
    b0h, slot, sid = b0_loc(p, env, tag, t)
    wrkt = b0h
    scr32 = env["scr32"]
    sums = env["sums"]

    if rev:
        nf = b0h.shape[1]
        b0dst = AP(b0h, slot + 127, [[nf, 128], [128, 32], [-1, 128]])
    else:
        b0dst = b0h[:, slot:slot + 4096]

    sdx, sdxv = env["x_load"][t]

    @blk.scalar
    def _(sc):
        sc.wait_ge(sdx.h, sdxv)
        sc.wait_ge(p.s["bnd"].h, env.get("x_ready", {}).get(t, 0))
        sc.wait_ge(p.s["dtp"].h, env["b0_free"][sid])
        nsq = s_act.inc(sc.activation(out=wrkt[:, slot:slot + 4096], in_=xt,
                                      func=AF.Square,
                                      accum_out=scr32[:, 160 + t:161 + t]))
        sc.wait_ge(s_act.h, nsq)
        s_act.inc(sc.activation(out=b0dst, in_=xt, func=AF.Copy,
                                accum_out=sums[:, t:t + 1]))
        env["act_cast"][t] = s_act.n


def ln_half(p, env, blk, h, ts, tag):
    s_dve, s_act = p.s["dve"], p.s["act"]
    sums, mcol, rstd, nmr, scr32 = (env[n] for n in
                                    ["sums", "mcol", "rstd", "nmr", "scr32"])
    ssq = env["ssq"]
    wrk, B0_O = env["wrk"], env["B0_O"]
    n = len(ts)
    t0 = ts[0]
    sl = slice(t0, t0 + n)

    @blk.vector
    def _(v):
        v.wait_ge(s_act.h, s_act.n)
        v.wait_ge(s_dve.h, s_dve.n)
        s_dve.inc(v.tensor_scalar(out=mcol[:, sl], in0=sums[:, sl],
                                  scalar1=1.0 / L, scalar2=None, op0=ALU.mult))
        v.wait_ge(s_dve.h, s_dve.n)
        s_dve.inc(v.tensor_mul(out=scr32[:, 184:184 + n], in0=mcol[:, sl],
                               in1=mcol[:, sl]))
        v.wait_ge(s_dve.h, s_dve.n)
        s_dve.inc(v.scalar_tensor_tensor(out=ssq[:, sl],
                                         in0=scr32[:, 160 + t0:160 + t0 + n],
                                         scalar=1.0 / L, in1=scr32[:, 184:184 + n],
                                         op0=ALU.mult, op1=ALU.subtract))

    @blk.scalar
    def _(sc):
        sc.wait_ge(s_dve.h, s_dve.n)
        s_act.inc(sc.activation(out=rstd[:, sl], in_=ssq[:, sl], func=AF.Sqrt,
                                bias=env["epsc"][:, 0:1], scale=1.0))

    @blk.vector
    def _(v):
        v.wait_ge(s_act.h, s_act.n)
        s_dve.inc(v.reciprocal(out=rstd[:, sl], in_=rstd[:, sl]))
        v.wait_ge(s_dve.h, s_dve.n)
        s_dve.inc(v.scalar_tensor_tensor(out=nmr[:, sl], in0=mcol[:, sl],
                                         scalar=-1.0, in1=rstd[:, sl],
                                         op0=ALU.mult, op1=ALU.mult))
        v.wait_ge(s_dve.h, s_dve.n)
        for t in ts:
            b0h, slot, _sid = b0_loc(p, env, tag, t)
            s_dve.inc(v.tensor_scalar(out=b0h[:, slot:slot + 4096],
                                      in0=b0h[:, slot:slot + 4096],
                                      scalar1=rstd[:, t:t + 1],
                                      scalar2=nmr[:, t:t + 1],
                                      op0=ALU.mult, op1=ALU.add))
            env["b0_ready"][t] = s_dve.n


def ln_pair(p, env, tag, h, ts, rev, kc_after=False, xbars=True):
    nc = p.nc
    with p.block(f"{tag}_lnp{ts[0]}") as blk:
        for t in ts:
            ln_tile(p, env, blk, t, rev, tag)
        ln_half(p, env, blk, h, ts, tag)
    if xbars:
        ln_xbars(p, env, tag, h, ts)
    if kc_after:
        kc_step(p, env, 2)


def ln_xbars(p, env, tag, h, ts):
    nc = p.nc
    NTH, TPH = p.NT // p.NH, p.TPH
    s_dve, s_dtp, s_dwr, s_pe = p.s["dve"], p.s["dtp"], p.s["dwr"], p.s["pe"]
    wrk, B0_O, FMT_O = env["wrk"], env["B0_O"], env["FMT_O"]
    with p.block(f"{tag}_xb{ts[0]}") as blk:
        @blk.sync
        def _(sy):
            for t in ts:
                sy.wait_ge(s_dve.h, env["b0_ready"][t])
            sy.wait_ge(s_dwr.h, s_dwr.n)
            sy.wait_ge(p.s["dc"].h, p.s["dc"].n)
            sy.wait_ge(p.s["dg"].h, p.s["dg"].n)
            for nm in ["gt0", "gt1", "sc", "ctp0", "ctp1", "cr0", "cr1"]:
                if nm in p.s:
                    sy.wait_ge(p.s[nm].h, p.s[nm].n)
            for t in ts:
                i = env.get("stripe_map", {}).get((tag, t), t - h * NTH)
                b0h, slot, _sid = b0_loc(p, env, tag, t)
                sy.wait_ge(s_pe.h, max(env.get("stripe_free", {}).get(i, 0),
                                       env.get("wrk_pe_free", 0),
                                       env.get("uth_pe_gate", 0)))
                dst = wrk[:, FMT_O:FMT_O + 32 * TPH] \
                    .rearrange("p (ft tk) -> p ft tk", ft=32)[:, :, i * 128:(i + 1) * 128]
                s_dtp.inc(sy.dma_start(out=dst, in_=b0h[:, slot:slot + 4096],
                                       transpose=True), 16)
            sy.wait_ge(s_dtp.h, s_dtp.n)
            for t in ts:
                env["b0_free"][b0_loc(p, env, tag, t)[2]] = s_dtp.n
                env.setdefault("uth_pair_xb", {})[t] = s_dtp.n


def s1_write_half(p, env, tag, h):
    nc = p.nc
    TPH, T = p.TPH, p.T
    s_dwr = p.s["dwr"]
    wrk, FMT_O = env["wrk"], env["FMT_O"]
    s1 = env["s1"]
    with p.block(f"{tag}_s1w{h}") as blk:
        @blk.sync
        def _(sy):
            src = wrk[:, FMT_O:FMT_O + 32 * TPH] \
                .rearrange("p (ft tk) -> p ft tk", ft=32)
            dst = AP(s1, h * TPH, [[32 * T, 128], [T, 32], [1, TPH]])
            s_dwr.inc(sy.dma_start(out=dst, in_=src), 16)
            env["s1_half"][h] = s_dwr.n
    env.setdefault("done_s1w", set()).add((tag, h))


def s1_write_pair(p, env, tag, ts):
    T, TPH = p.T, p.TPH
    s_dwr = p.s["dwr"]
    wrk, FMT_O, s1 = env["wrk"], env["FMT_O"], env["s1"]
    smap = env["stripe_map"]
    with p.block(f"{tag}_s1p{ts[0]}") as blk:
        @blk.sync
        def _(sy):
            sy.wait_ge(p.s["dtp"].h, p.s["dtp"].n)
            for t in ts:
                i = smap[(tag, t)]
                src = wrk[:, FMT_O:FMT_O + 32 * TPH] \
                    .rearrange("p (ft tk) -> p ft tk", ft=32)[:, :, i * 128:(i + 1) * 128]
                s_dwr.inc(sy.dma_start(
                    out=AP(s1, t * 128, [[32 * T, 128], [T, 32], [1, 128]]),
                    in_=src), 16)


def ln_and_transpose_half(p, env, tag, h, rev, write_s1, kc_after=False):
    NTH = p.NT // p.NH
    for i in range(0, NTH, 2):
        ts = list(range(h * NTH + i, h * NTH + min(i + 2, NTH)))
        if (tag, ts[0]) in env.get("done_pairs", set()):
            continue
        if (tag, ts[0]) in env.get("computed_pairs", set()):
            ln_xbars(p, env, tag, h, ts)
            continue
        ln_pair(p, env, tag, h, ts, rev, kc_after=kc_after)
    if write_s1 and (tag, h) not in env.get("done_s1w", set()):
        s1_write_half(p, env, tag, h)


def build_ffn(p, env, tag):
    nc = p.nc
    T, NT, NH, TPH = p.T, p.NT, p.NH, p.TPH
    NTH = NT // NH
    s_pe, s_dve, s_act, s_dg, s_dtp, s_dwr = (p.s[n] for n in
                                              ["pe", "dve", "act", "dg", "dtp", "dwr"])
    s_gt = [p.sem("gt0"), p.sem("gt1")]
    s_sc = p.sem("sc")
    wrk = env["wrk"]
    GB_O, WU_O, WD_O, HID_O, DO_O = (env[n] for n in
                                     ["GB_O", "WU_O", "WD_O", "HID_O", "DO_O"])
    up_in, dn_in, s1, s2, idxt, idxn = (env[n] for n in
                                        ["up_in", "dn_in", "s1", "s2", "idxt", "idxn"])
    psU, psD = env["psU"], env["psD"]
    delt, xres = env["delt"], env["xres"]

    env.setdefault("b0_free", {0: 0, 1: 0, 2: 0})
    env.setdefault("sq_done", {})
    env.setdefault("b0_freed", {0: 0, 1: 0})
    env.setdefault("b0_ready", {})
    env.setdefault("act_cast", {})
    env.setdefault("s1_half", {})
    env.setdefault("delt_free", 0)

    if env.pop("f2A_defer_xb", False) and tag == "f2":
        ln_xbars(p, env, "f2A", 1, [6, 7])
        s1_write_pair(p, env, "f2A", [6, 7])
        env.setdefault("done_s1w", set()).update({("f2A", 0), ("f2A", 1)})
    else:
        for h in range(NH):
            ln_and_transpose_half(p, env, tag + "A", h, rev=False, write_s1=True,
                                  kc_after=(tag == "f1"))
    if tag == "f1":
        kc_flush(p, env)

    CHS = [(c * 512, min(512, T - c * 512)) for c in range((T + 511) // 512)]
    gb_free = {k: 0 for k in range(4)}
    wu_free = {0: 0, 1: 0}
    dout_free = {0: 0, 1: 0}
    gather_done = {}
    w_done = {}
    gelu_done = {}
    evac_done = {}
    up_done = {}
    s1_final = s_dwr.n

    def prefetch2(g, head):
        if head >= 16:
            return
        g.wait_ge(s_dwr.h, s1_final)
        g.wait_ge(s_dtp.h, s_dtp.n)
        sg = s_gt[head % 2]
        for e in range(2):
            k = 2 * head + e
            g.wait_ge(s_pe.h, gb_free[k % 4])
            sg.inc(g.indirect_dma_start(
                out=wrk[:, GB_O + (k % 4) * T:GB_O + (k % 4) * T + T],
                out_offset=None,
                in_=AP(s1, 0, [[T, L], [1, T]]),
                in_offset=IndirectOffsetOnAxis(ap=idxt[:, k:k + 1], axis=0)), 16)
        g.wait_ge(s_pe.h, wu_free[head % 2])
        sg.inc(g.dma_start(
            out=wrk[:, WU_O + (head % 2) * 2048:WU_O + (head % 2) * 2048 + 2048],
            in_=up_in[head].rearrange("(c pp) m -> pp c m", pp=128)), 16)
        sg.inc(g.dma_start(
            out=wrk[:, WD_O + (head % 2) * 2048:WD_O + (head % 2) * 2048 + 2048],
            in_=dn_in[head].rearrange("(c pp) m -> pp c m", pp=128)), 16)
        gather_done[head] = (sg, sg.n)

    for head in range(17):
        if head < 16:
            for j in range(8):
                gelu_done[(head, j)] = s_act.n + j + 1
        with p.block(f"{tag}B{head}") as blk:
            @blk.gpsimd
            def _(g):
                if head > 0:
                    ph = head - 1
                    g.wait_ge(s_dve.h, evac_done[ph])
                    for e in range(2):
                        k = 2 * ph + e
                        s_sc.inc(g.indirect_dma_start(
                            out=s2[:, :],
                            out_offset=IndirectOffsetOnAxis(ap=idxn[:, k:k + 1], axis=0),
                            in_=wrk[:, DO_O + e * T:DO_O + e * T + T],
                            in_offset=None), 16)
                    dout_free[0] = dout_free[1] = s_sc.n
                if head == 0:
                    prefetch2(g, 0)
                    prefetch2(g, 1)
                else:
                    prefetch2(g, head + 1)

            if head >= 16:
                continue

            @blk.tensor
            def _(te):
                if head == 0 and env.get("kc_done_dve"):
                    te.wait_ge(s_dve.h, env["kc_done_dve"])
                te.wait_ge(gather_done[head][0].h, gather_done[head][1])
                wus = WU_O + (head % 2) * 2048
                wds = WD_O + (head % 2) * 2048
                gb0 = wrk[:, GB_O + ((2 * head) % 4) * T:GB_O + ((2 * head) % 4) * T + T]
                gb1 = wrk[:, GB_O + ((2 * head + 1) % 4) * T:GB_O + ((2 * head + 1) % 4) * T + T]
                for j in range(9):
                    if j < 8:
                        ju = j % 2
                        if j >= 2:
                            te.wait_ge(s_act.h, gelu_done[(head, j - 2)])
                        elif head > 0:
                            te.wait_ge(s_act.h, gelu_done[(head - 1, 8 - 2 + j)])
                        for co, cn in CHS:
                            te.matmul(psU[:, ju * 1024 + co:ju * 1024 + co + cn],
                                      wrk[:, wus + j * 128:wus + j * 128 + 128],
                                      gb0[:, co:co + cn], start=True, stop=False)
                        for co, cn in CHS:
                            mm = te.matmul(psU[:, ju * 1024 + co:ju * 1024 + co + cn],
                                           wrk[:, wus + 1024 + j * 128:wus + 1024 + j * 128 + 128],
                                           gb1[:, co:co + cn], start=False, stop=True)
                        s_pe.inc(mm)
                        up_done[(head, j)] = s_pe.n
                    if j >= 1:
                        jd = j - 1
                        te.wait_ge(s_act.h, gelu_done[(head, jd)])
                        if jd == 0 and head > 0:
                            te.wait_ge(s_dve.h, evac_done[head - 1])
                        hs = HID_O + (jd % 3) * T
                        for e in range(2):
                            for ci, (co, cn) in enumerate(CHS):
                                mm = te.matmul(
                                    psD[:, (2 * e + ci) * 512:(2 * e + ci) * 512 + cn],
                                    wrk[:, wds + jd * 256 + e * 128:wds + jd * 256 + e * 128 + 128],
                                    wrk[:, hs + co:hs + co + cn],
                                    start=(jd == 0), stop=(jd == 7))
                        if jd == 7:
                            s_pe.inc(mm)
                    if j < 8:
                        gb_free[(2 * head) % 4] = s_pe.n
                        gb_free[(2 * head + 1) % 4] = s_pe.n
                wu_free[head % 2] = s_pe.n
                env["ffnB_pe_done"] = s_pe.n
                env["wrk_pe_free"] = s_pe.n

            @blk.scalar
            def _(sc):
                for j in range(8):
                    sc.wait_ge(s_pe.h, up_done[(head, j)])
                    tw = CHS[-1][0] + CHS[-1][1]
                    n = s_act.inc(sc.activation(
                        out=wrk[:, HID_O + (j % 3) * T:HID_O + (j % 3) * T + tw],
                        in_=psU[:, (j % 2) * 1024:(j % 2) * 1024 + tw], func=AF.Gelu))
                    assert gelu_done[(head, j)] == n

            @blk.vector
            def _(v):
                v.wait_ge(s_pe.h, s_pe.n)
                if head > 0:
                    v.wait_ge(s_sc.h, dout_free[0])
                tw = CHS[-1][0] + CHS[-1][1]
                for e in range(2):
                    for ci, (co, cn) in enumerate(CHS):
                        s_dve.inc(v.tensor_copy(
                            out=wrk[:, DO_O + e * T + co:DO_O + e * T + co + cn],
                            in_=psD[:, (2 * e + ci) * 512:(2 * e + ci) * 512 + cn]))
                evac_done[head] = s_dve.n

    s_bnd = p.sem("bnd")
    psb = psU.bitcast(BF16)
    identb = env["identb"]
    NQ = 4 if T >= 512 else 1
    QW = T // NQ
    last = (tag == "f2")
    s_dst = p.s["dst"]
    y_out = env["y_out"]
    post_tile_cb = env.pop("postC_cb_next", None) if tag == "f1" else None
    post_tile_cb = post_tile_cb or {}

    if C_DEBUG == "skip":
        env.setdefault("x_ready", {})
        for tt in range(NT):
            env["x_ready"][tt] = s_bnd.n
        env["delt_free"] = p.s["dve"].n
        return
    read_done = {}
    TPQ0 = NT // NQ
    for q in range(NQ):
        env[f"{tag}_cq{q}"] = p.s["pe"].n + (q + 1) * TPQ0 * 32
    s_cr = [p.sem("cr0"), p.sem("cr1")]
    with p.block(f"{tag}Crd") as blk:
        @blk.sync
        def _(sy):
            sy.wait_ge(s_sc.h, s_sc.n)
            sy.wait_ge(s_gt[0].h, s_gt[0].n)
            sy.wait_ge(s_gt[1].h, s_gt[1].n)
            sy.wait_ge(s_dg.h, s_dg.n)
            for q in range(NQ):
                if q >= 2 and C_DEBUG not in ("read",):
                    sy.wait_ge(s_pe.h, env[f"{tag}_cq{q - 2}"])
                for piece in range(4):
                    src = AP(s2, q * QW + piece * 8 * 128 * T,
                             [[T, 128], [128 * T, 8], [1, QW]])
                    s_cr[q % 2].inc(sy.dma_start(
                        out=wrk[:, (q % 2) * 32 * QW + piece * 8 * QW:
                                (q % 2) * 32 * QW + (piece + 1) * 8 * QW]
                        .rearrange("p (g tok) -> p g tok", g=8),
                        in_=src), 16)
                read_done[q] = (s_cr[q % 2], s_cr[q % 2].n)

    TPQ = NT // NQ
    trans_done = {}
    slot_free = {k: 0 for k in range(8)}
    bnd_base = s_bnd.n
    kk = [0]
    if C_DEBUG == "read":
        env.setdefault("x_ready", {})
        for tt in range(NT):
            env["x_ready"][tt] = s_bnd.n
        env["delt_free"] = p.s["dve"].n
        return
    for q in range(NQ):
        with p.block(f"{tag}Cmm{q}") as blk:
            @blk.tensor
            def _(te):
                if q == 0:
                    te.wait_ge(s_act.h, s_act.n)
                    te.wait_ge(s_dve.h, s_dve.n)
                te.wait_ge(read_done[q][0].h, read_done[q][1])
                for tl in range(TPQ):
                    tt = q * TPQ + tl
                    for g in range(32):
                        k = kk[0]
                        G = k // 8
                        if G >= 2 and k % 8 == 0 and C_DEBUG not in ("mm", "nowar"):
                            te.wait_ge(s_bnd.h, bnd_base + (G - 1) * 8)
                        mm = te.transpose(
                            psb[:, (G % 2) * 1024 + (k % 8) * 128:
                                (G % 2) * 1024 + (k % 8) * 128 + 128],
                            wrk[:, (q % 2) * 32 * QW + g * QW + tl * 128:
                                (q % 2) * 32 * QW + g * QW + tl * 128 + 128],
                            identb[:])
                        s_pe.inc(mm)
                        if k % 8 == 7:
                            trans_done[G] = s_pe.n
                        kk[0] += 1
                assert env[f"{tag}_cq{q}"] == s_pe.n

            @blk.vector
            def _(v):
                if C_DEBUG == "mm":
                    return
                for tl in range(TPQ):
                    tt = q * TPQ + tl
                    for g in range(32):
                        k2 = tt * 32 + g
                        G2 = k2 // 8
                        if k2 % 8 == 0:
                            v.wait_ge(s_pe.h, trans_done[G2])
                        xv = xres[:, tt * L + g * 128:tt * L + g * 128 + 128]
                        s_bnd.inc(v.scalar_tensor_tensor(
                            out=xv,
                            in0=psb[:, (G2 % 2) * 1024 + (k2 % 8) * 128:
                                    (G2 % 2) * 1024 + (k2 % 8) * 128 + 128],
                            scalar=0.5, in1=xv, op0=ALU.mult, op1=ALU.add))
                    env.setdefault("x_ready", {})[tt] = s_bnd.n

            if last and C_DEBUG != "mm":
                @blk.scalar
                def _(sc):
                    for tl in range(TPQ):
                        tt = q * TPQ + tl
                        sc.wait_ge(s_bnd.h, bnd_base + (tt + 1) * 32)
                        s_dst.inc(sc.dma_start(
                            out=y_out[tt * 128:(tt + 1) * 128, :],
                            in_=xres[:, tt * L:(tt + 1) * L]), 16)
                env["stored"] = True
        if q in post_tile_cb:
            post_tile_cb[q]()
    if C_DEBUG == "mm":
        env.setdefault("x_ready", {})
        for tt in range(NT):
            env["x_ready"][tt] = bnd_base
        env["stored"] = False
    env["c_stt_done"] = s_bnd.n
    env["uth_pe_gate"] = env[f"{tag}_cq{NQ - 1}"]

    env["delt_free"] = s_dve.n
def ssm_half_mm(p, env, h, cbs):
    nc = p.nc
    T, NT, NH, TPH = p.T, p.NT, p.NH, p.TPH
    NTH = NT // NH
    s_pe, s_dve, s_dg = p.s["pe"], p.s["dve"], p.s["dg"]
    s_bnd = p.sem("bnd")
    wrk, xres = env["wrk"], env["xres"]
    FMT_O = env["FMT_O"]
    psU, psD = env["psU"], env["psD"]
    delt_t, tct_t = env["delt"], env["tct"]
    band_free = env["band_free"]

    def band_ap(b):
        return (psU if b < 4 else psD)[:, (b % 4) * 512:(b % 4 + 1) * 512]

    pe_base = s_pe.n
    env.setdefault("stripe_free", {})
    for tloc in range(NTH):
        env["stripe_free"][tloc] = pe_base + 8 * (tloc + 1)
    s_dtp_h = p.s["dtp"]
    uth_ready = s_dtp_h.n

    for tloc in range(NTH):
        t = h * NTH + tloc
        with p.block(f"ssmM{h}_{tloc}") as blk:
            stt = {}

            @blk.tensor
            def _(te):
                te.wait_ge(s_dg.h, env["tcat_done"])
                te.wait_ge(s_dtp_h.h, env.get("uth_pair_xb", {}).get(t, uth_ready))
                for jb in range(32):
                    lhs = wrk[:, FMT_O + jb * TPH + tloc * 128:
                              FMT_O + jb * TPH + tloc * 128 + 128]
                    bmin = max(0, jb // 4)
                    for b in range(bmin, 8):
                        if jb == 0:
                            te.wait_ge(s_bnd.h, band_free[b])
                        sidx = 3 + 4 * b - jb
                        st_, sp_ = (jb == 0), (jb == min(4 * b + 3, 31))
                        if sidx <= 28:
                            mm = te.matmul(band_ap(b), lhs,
                                           delt_t[:, sidx * 128:sidx * 128 + 512],
                                           start=st_, stop=sp_)
                        else:
                            mm = te.matmul(band_ap(b), lhs,
                                           tct_t[:, (sidx - 29) * 128:(sidx - 29) * 128 + 512],
                                           start=st_, stop=sp_)
                        if sp_:
                            s_pe.inc(mm)
                            stt[b] = s_pe.n

            @blk.vector
            def _(v):
                for b in range(8):
                    v.wait_ge(s_pe.h, stt[b])
                    xv = xres[:, t * L + b * 512:t * L + (b + 1) * 512]
                    s_bnd.inc(v.tensor_add(out=xv, in0=xv, in1=band_ap(b)))
                    band_free[b] = s_bnd.n
                env.setdefault("x_ready", {})[t] = s_bnd.n
        assert s_pe.n == env["stripe_free"][tloc], (s_pe.n, env["stripe_free"][tloc])
        if tloc in cbs:
            cbs[tloc]()


def build_ssm(p, env, post_cbs=None):
    nc = p.nc
    NTH = p.NT // p.NH
    s_dve, s_dg, s_dtp = p.s["dve"], p.s["dg"], p.s["dtp"]
    kext = env["kext"]
    env["band_free"] = {b: env.get("c_stt_done", 0) for b in range(8)}

    with p.block("ssmT") as blk:
        @blk.gpsimd
        def _(g):
            g.wait_ge(p.s["dwr"].h, env.get("kext_done", p.s["dwr"].n))
            g.wait_ge(s_dtp.h, s_dtp.n)
            s_dg.inc(g.dma_start(out=env["delt"][:, 0:4096],
                                 in_=AP(kext, 3585, [[1, 128], [1, 4096]])), 16)
            s_dg.inc(g.dma_start(out=env["tct"][:],
                                 in_=AP(kext, 3585 + 3712, [[1, 128], [1, 768]])), 16)
        env["tcat_done"] = s_dg.n

    ln_and_transpose_half(p, env, "ssm0", 0, rev=True, write_s1=False)

    if p.NH == 2:
        half1 = list(range(p.NT // 2, p.NT))
        prs = [half1[i:i + 2] for i in range(0, NTH, 2)]
        cbs0 = {}
        for k, pr in enumerate(prs):
            def _cb(pr=pr):
                ln_pair(p, env, "ssm1", 1, pr, True)
                env.setdefault("done_pairs", set()).add(("ssm1", pr[0]))
            cbs0[min(k, NTH - 1)] = _cb
        ssm_half_mm(p, env, 0, cbs0)
        ssm_half_mm(p, env, 1, post_cbs or {})
    else:
        ssm_half_mm(p, env, 0, post_cbs or {})


def build_store(p, env):
    nc = p.nc
    NT = p.NT
    s_dve, s_dst = p.s["dve"], p.s["dst"]
    xres, y_out = env["xres"], env["y_out"]
    with p.block("store") as blk:
        @blk.sync
        def _(sy):
            sy.wait_ge(s_dve.h, s_dve.n)
            sy.wait_ge(p.s["bnd"].h, p.s["bnd"].n)
            if not env.get("stored"):
                for t in range(NT):
                    s_dst.inc(sy.dma_start(out=y_out[t * 128:(t + 1) * 128, :],
                                           in_=xres[:, t * L:(t + 1) * L]), 16)
            sy.wait_ge(s_dst.h, s_dst.n)
